# version 07
# baseline (speedup 1.0000x reference)
import contextlib
import numpy as np
import concourse.bass as bass
import concourse.mybir as mybir
from concourse.bass_utils import run_bass_kernel_spmd

F32 = mybir.dt.float32
U32 = mybir.dt.uint32
AF = mybir.ActivationFunctionType
ALU = mybir.AluOpType
AX = mybir.AxisListType

D = 1024
NCORE = 8
SEQLEN = 2048
NSEQ = 2
TOK = NSEQ * SEQLEN
NT_SEQ = SEQLEN // 128
NT = TOK // 128
HEADS = 16
NE = 8
CAP = 2048.0
C_PAD = 512
NSLOT = NE * C_PAD
ST_E = C_PAD // 128
TRASH0 = float(NSLOT)
GIDX_ROWS = NSLOT + TOK
I_FF = 4096
EPS_A = 1e-6
EPS_LN = 1e-5
INV_D = 1.0 / D


class Ch:
    def __init__(self, sem, step=1):
        self.sem, self.step, self.n = sem, step, 0

    def bump(self, instr):
        instr.then_inc(self.sem, self.step)
        self.n += self.step
        return self.n


def _sb(ctx, nc, name, shape, dt=F32):
    return ctx.enter_context(nc.sbuf_tensor(name, shape, dt))


def _ps(ctx, nc, name, shape):
    return ctx.enter_context(nc.psum_tensor(name, shape, F32))


def build_a(n_seq=NSEQ, with_p1=True, with_p2=True, p2s=9):
    nc = bass.Bass("TRN2", target_bir_lowering=False)
    x_in = nc.dram_tensor("x", [TOK, D], F32, kind="ExternalInput")
    wqkv_in = nc.dram_tensor("wqkv", [D, 3 * D], F32, kind="ExternalInput")
    wout_in = nc.dram_tensor("wout", [D, D], F32, kind="ExternalInput")
    bout_in = nc.dram_tensor("bout", [1, D], F32, kind="ExternalInput")
    gt_in = nc.dram_tensor("gt", [128, 64], F32, kind="ExternalInput")
    gam_in = nc.dram_tensor("gam", [128, 8], F32, kind="ExternalInput")
    bet_in = nc.dram_tensor("bet", [128, 8], F32, kind="ExternalInput")
    ident_in = nc.dram_tensor("ident", [128, 128], F32, kind="ExternalInput")
    cst_in = nc.dram_tensor("cst", [128, 32], F32, kind="ExternalInput")
    ones_in = nc.dram_tensor("onesrow", [1, 128], F32, kind="ExternalInput")

    x1_out = nc.dram_tensor("x1o", [TOK, D], F32, kind="ExternalOutput")
    scal_out = nc.dram_tensor("scalo", [TOK, 8], F32, kind="ExternalOutput")
    cnt_out = nc.dram_tensor("cnto", [8, 2], F32, kind="ExternalOutput")
    qscr = nc.dram_tensor("qscr", [TOK, D], F32)

    ctx = contextlib.ExitStack()
    with ctx:
        wq = _sb(ctx, nc, "wq", [128, 8 * D])
        wkv = _sb(ctx, nc, "wkv", [128, 8 * 2048])
        wout = _sb(ctx, nc, "wout_sb", [128, 8 * D])
        boutr = _sb(ctx, nc, "boutr", [1, D])
        gt = _sb(ctx, nc, "gt_sb", [128, 64])
        gamv = _sb(ctx, nc, "gamv_sb", [128, 8])
        betv = _sb(ctx, nc, "betv_sb", [128, 8])
        ident = _sb(ctx, nc, "ident_sb", [128, 128])
        cst = _sb(ctx, nc, "cst_sb", [128, 32])
        onesr = _sb(ctx, nc, "onesr", [1, 128])
        ggam = _sb(ctx, nc, "ggam", [128, 64])
        bgrow = _sb(ctx, nc, "bgrow", [1, 8])

        xt = [_sb(ctx, nc, f"xt{i}", [128, D]) for i in range(2)]
        xn1 = _sb(ctx, nc, "xn1", [128, D])
        xn1t = _sb(ctx, nc, "xn1t", [128, D])
        qsb = _sb(ctx, nc, "qsb", [128, D])
        ktile = _sb(ctx, nc, "ktile", [128, D])
        vtile = _sb(ctx, nc, "vtile", [128, D])
        kv2 = _sb(ctx, nc, "kv2", [128, D])
        ksum2 = _sb(ctx, nc, "ksum2", [128, 16])
        ksblk = _sb(ctx, nc, "ksblk", [128, 128])
        qt = [_sb(ctx, nc, f"qt{i}", [128, D]) for i in range(2)]
        zt = _sb(ctx, nc, "zt", [128, 16])
        z2 = _sb(ctx, nc, "z2", [128, 16])
        oz = _sb(ctx, nc, "oz", [128, D])
        ozt = _sb(ctx, nc, "ozt", [128, D])
        x1b = _sb(ctx, nc, "x1b", [128, D])
        t3d = [_sb(ctx, nc, f"t3d{i}", [128, D]) for i in range(2)]
        t3td = [_sb(ctx, nc, f"t3td{i}", [128, D]) for i in range(2)]
        lgt = _sb(ctx, nc, "lgt", [128, 8])
        eqt = _sb(ctx, nc, "eqt", [128, 8])
        ext = _sb(ctx, nc, "ext", [128, 8])
        cndt = _sb(ctx, nc, "cndt", [128, 8])
        oht = _sb(ctx, nc, "oht", [128, 8])
        scpkd = [_sb(ctx, nc, f"scpk{i}", [128, 8]) for i in range(2)]
        cntsb = _sb(ctx, nc, "cntsb", [8, 2])
        SC = ["s1c", "s1q", "m1", "msq", "ex2", "var", "sq1", "r1",
              "s2c", "s2q", "m2", "m2sq", "ex22", "var2", "sq2", "r2",
              "s3c", "s3q", "m3", "m3sq", "ex23", "var3", "sq3", "r3",
              "cc", "ss", "mx", "nmx", "idx", "wgt", "smx", "t0", "t1", "t1s"]
        sc = {k: _sb(ctx, nc, f"sc_{k}", [128, 1]) for k in SC}
        bnst = _sb(ctx, nc, "bnst", [128, 12])
        mv1 = _sb(ctx, nc, "mv1", [128, 2])
        mv2 = _sb(ctx, nc, "mv2", [128, 2])
        dumm = _sb(ctx, nc, "dumm", [128, 2])

        p_cnt = _ps(ctx, nc, "p_cnt", [8, 2])

        s_ld = Ch(ctx.enter_context(nc.semaphore("s_ld")), 16)
        s_x = [Ch(ctx.enter_context(nc.semaphore(f"s_x{j}")), 16) for j in range(2)]
        s_pe = Ch(ctx.enter_context(nc.semaphore("s_pe")))
        s_act = Ch(ctx.enter_context(nc.semaphore("s_act")))
        s_dve = Ch(ctx.enter_context(nc.semaphore("s_dve")))
        s_gq = Ch(ctx.enter_context(nc.semaphore("s_gq")), 16)
        s_gx1 = Ch(ctx.enter_context(nc.semaphore("s_gx1")), 16)
        s_gsc = [Ch(ctx.enter_context(nc.semaphore(f"s_gsc{j}")), 16) for j in range(2)]
        s_q2 = [Ch(ctx.enter_context(nc.semaphore(f"s_q2{j}")), 16) for j in range(2)]

        with nc.psum_tensor("p_bg", [1, 8], F32) as p_bg, nc.Block(no_gpsimd_drain=True) as blk:
            @blk.sync
            def _(sy):
                for fc in range(8):
                    s_ld.bump(sy.dma_start(out=wq[:, fc * D:(fc + 1) * D],
                                           in_=wqkv_in[128 * fc:128 * (fc + 1), 0:D]))
                    s_ld.bump(sy.dma_start(out=wkv[:, fc * 2048:(fc + 1) * 2048],
                                           in_=wqkv_in[128 * fc:128 * (fc + 1), D:3 * D]))
                    s_ld.bump(sy.dma_start(out=wout[:, fc * D:(fc + 1) * D],
                                           in_=wout_in[128 * fc:128 * (fc + 1), :]))
                for dst, src in [(boutr, bout_in), (gt, gt_in), (gamv, gam_in),
                                 (betv, bet_in), (ident, ident_in), (cst, cst_in),
                                 (onesr, ones_in)]:
                    s_ld.bump(sy.dma_start(out=dst[:, :], in_=src[:, :]))

            @blk.vector
            def _(v):
                v.wait_ge(s_ld.sem, s_ld.n)
                for fc in range(8):
                    s_dve.bump(v.tensor_scalar_mul(ggam[:, 8 * fc:8 * fc + 8],
                                                   gt[:, 8 * fc:8 * fc + 8],
                                                   gamv[:, fc:fc + 1]))

            @blk.tensor
            def _(t):
                t.wait_ge(s_dve.sem, s_dve.n)
                for fc in range(8):
                    ins = t.matmul(p_bg[:, :], betv[:, fc:fc + 1],
                                   gt[:, 8 * fc:8 * fc + 8],
                                   start=(fc == 0), stop=(fc == 7))
                s_pe.bump(ins)

            @blk.vector
            def _(v):
                v.wait_ge(s_pe.sem, s_pe.n)
                s_dve.bump(v.tensor_copy(bgrow[:, :], p_bg[:, :]))

            @blk.scalar
            def _(s):
                s.add_instruction(mybir.InstLoadActFuncSet(
                    name=nc.get_next_instruction_name(),
                    act_func_set_id=6, ins=[], outs=[]))

        m = {}

        for seq in range(n_seq):
            t0 = seq * NT_SEQ

            if not with_p1:
                continue
            with contextlib.ExitStack() as ps1:
                p_kv = _ps(ps1, nc, f"p_kv{seq}", [64, 1024])
                p_ks = _ps(ps1, nc, f"p_ks{seq}", [64, 16])
                p_q = _ps(ps1, nc, f"p_q{seq}", [128, 1024])
                p_pj = _ps(ps1, nc, f"p_pj{seq}", [128, 1024])
                blk = ps1.enter_context(nc.Block(no_gpsimd_drain=True))

                for i in range(NT_SEQ):
                    g = t0 + i

                    @blk.sync
                    def _(sy, i=i, g=g, seq=seq):
                        if ("p1xn1", seq, i - 2) in m:
                            sy.wait_ge(s_dve.sem, m[("p1xn1", seq, i - 2)])
                        s_x[i % 2].bump(sy.dma_start(out=xt[i % 2][:, :],
                                                     in_=x_in[128 * g:128 * (g + 1), :]))
                        m[("p1x", seq, i)] = s_x[i % 2].n

                    @blk.vector
                    def _(v, i=i, seq=seq):
                        v.wait_ge(s_x[i % 2].sem, m[("p1x", seq, i)])
                        if ("p1r1", seq, i - 1) in m:
                            v.wait_ge(s_act.sem, m[("p1r1", seq, i - 1)])
                        v.bn_stats(bnst[:, 0:6], xt[i % 2][:, 0:512])
                        s_dve.bump(v.bn_stats(bnst[:, 6:12], xt[i % 2][:, 512:1024]))
                        v.wait_ge(s_dve.sem, s_dve.n)
                        s_dve.bump(v.bn_aggr(mv1[:, :], bnst[:, :]))
                        v.wait_ge(s_dve.sem, s_dve.n)
                        s_dve.bump(v.tensor_copy(dumm[:, :], mv1[:, :]))
                        m[("p1var", seq, i)] = s_dve.n

                    @blk.scalar
                    def _(s, i=i, seq=seq):
                        s.wait_ge(s_dve.sem, m[("p1var", seq, i)])
                        if ("p1xn1", seq, i - 1) in m:
                            s.wait_ge(s_dve.sem, m[("p1xn1", seq, i - 1)])
                        s.wait_ge(s_act.sem, m.get("t1s", 0))
                        s_act.bump(s.activation(sc["t1s"][:, :], mv1[:, 1:2],
                                                AF.Ln, bias=cst[:, 18:19], scale=1.0))
                        s.wait_ge(s_act.sem, s_act.n)
                        s_act.bump(s.activation(sc["r1"][:, :], sc["t1s"][:, :],
                                                AF.Exp, bias=0.0, scale=-0.5))
                        m[("p1r1", seq, i)] = s_act.n
                        m["t1s"] = s_act.n

                    @blk.vector
                    def _(v, i=i, seq=seq):
                        v.wait_ge(s_act.sem, m[("p1r1", seq, i)])
                        if ("p1xp", seq, i - 1) in m:
                            v.wait_ge(s_pe.sem, m[("p1xp", seq, i - 1)])
                        s_dve.bump(v.tensor_scalar(xn1[:, :], xt[i % 2][:, :],
                                                   mv1[:, 0:1], sc["r1"][:, :],
                                                   ALU.subtract, ALU.mult))
                        m[("p1xn1", seq, i)] = s_dve.n

                    @blk.tensor
                    def _(t, i=i, seq=seq):
                        t.wait_ge(s_dve.sem, m[("p1xn1", seq, i)])
                        if ("p1vev", seq, i - 1) in m:
                            t.wait_ge(s_act.sem, m[("p1vev", seq, i - 1)])
                        for fc in range(8):
                            ins = t.transpose(p_pj[:, 128 * fc:128 * (fc + 1)],
                                              xn1[:, 128 * fc:128 * (fc + 1)],
                                              ident[:, :])
                        s_pe.bump(ins)
                        m[("p1xp", seq, i)] = s_pe.n

                    @blk.scalar
                    def _(s, i=i, seq=seq):
                        s.wait_ge(s_pe.sem, m[("p1xp", seq, i)])
                        if ("p1vmm", seq, i - 1) in m:
                            s.wait_ge(s_pe.sem, m[("p1vmm", seq, i - 1)])
                        s_act.bump(s.activation(xn1t[:, :], p_pj[:, :], AF.Copy,
                                                bias=0.0, scale=1.0))
                        m[("p1xn1t", seq, i)] = s_act.n

                    @blk.tensor
                    def _(t, i=i, seq=seq):
                        t.wait_ge(s_act.sem, m[("p1xn1t", seq, i)])
                        if ("p1qev", seq, i - 1) in m:
                            t.wait_ge(s_act.sem, m[("p1qev", seq, i - 1)])
                        for qc in range(8):
                            for fc in range(8):
                                ins = t.matmul(
                                    p_q[:, 128 * qc:128 * (qc + 1)],
                                    wq[:, fc * D + 128 * qc: fc * D + 128 * (qc + 1)],
                                    xn1t[:, 128 * fc:128 * (fc + 1)],
                                    start=(fc == 0 and qc % 4 == 0),
                                    stop=(fc == 7 and qc % 4 == 3))
                        s_pe.bump(ins)
                        m[("p1qmm", seq, i)] = s_pe.n

                    @blk.scalar
                    def _(s, i=i, seq=seq):
                        s.wait_ge(s_pe.sem, m[("p1qmm", seq, i)])
                        s.wait_ge(s_gq.sem, m.get("p1qst_last", 0))
                        s_act.bump(s.activation(qsb[:, :], p_q[:, :], AF.Relu,
                                                bias=0.0, scale=1.0))
                        m[("p1qev", seq, i)] = s_act.n

                    @blk.gpsimd
                    def _(gp, i=i, g=g, seq=seq):
                        gp.wait_ge(s_act.sem, m[("p1qev", seq, i)])
                        s_gq.bump(gp.dma_start(out=qscr[128 * g:128 * (g + 1), :],
                                               in_=qsb[:, :]))
                        m[("p1qst", seq, i)] = s_gq.n
                        m["p1qst_last"] = s_gq.n

                    @blk.tensor
                    def _(t, i=i, seq=seq):
                        for h4 in range(2):
                            for fc in range(8):
                                ins = t.matmul(
                                    p_pj[:, 512 * h4:512 * (h4 + 1)],
                                    xn1t[:, 128 * fc:128 * (fc + 1)],
                                    wkv[:, fc * 2048 + 512 * h4: fc * 2048 + 512 * (h4 + 1)],
                                    start=(fc == 0), stop=(fc == 7))
                        s_pe.bump(ins)
                        m[("p1kmm", seq, i)] = s_pe.n

                    @blk.scalar
                    def _(s, i=i, seq=seq):
                        s.wait_ge(s_pe.sem, m[("p1kmm", seq, i)])
                        if ("p1kvmm", seq, i - 1) in m:
                            s.wait_ge(s_pe.sem, m[("p1kvmm", seq, i - 1)])
                        s_act.bump(s.activation(ktile[:, :], p_pj[:, :], AF.Relu,
                                                bias=0.0, scale=1.0))
                        m[("p1kev", seq, i)] = s_act.n

                    @blk.tensor
                    def _(t, i=i, seq=seq):
                        t.wait_ge(s_act.sem, m[("p1kev", seq, i)])
                        for h4 in range(2):
                            for fc in range(8):
                                ins = t.matmul(
                                    p_pj[:, 512 * h4:512 * (h4 + 1)],
                                    xn1t[:, 128 * fc:128 * (fc + 1)],
                                    wkv[:, fc * 2048 + 1024 + 512 * h4: fc * 2048 + 1024 + 512 * (h4 + 1)],
                                    start=(fc == 0), stop=(fc == 7))
                        s_pe.bump(ins)
                        m[("p1vmm", seq, i)] = s_pe.n

                    @blk.scalar
                    def _(s, i=i, seq=seq):
                        s.wait_ge(s_pe.sem, m[("p1vmm", seq, i)])
                        s_act.bump(s.activation(vtile[:, :], p_pj[:, :], AF.Copy,
                                                bias=0.0, scale=1.0))
                        m[("p1vev", seq, i)] = s_act.n

                    @blk.tensor
                    def _(t, i=i, seq=seq):
                        t.wait_ge(s_act.sem, m[("p1vev", seq, i)])
                        for h in range(16):
                            t.matmul(p_kv[:, 64 * h:64 * (h + 1)],
                                     ktile[:, 64 * h:64 * (h + 1)],
                                     vtile[:, 64 * h:64 * (h + 1)],
                                     start=(i == 0 and h % 8 == 0),
                                     stop=(i == NT_SEQ - 1 and h % 8 == 7))
                            ins = t.matmul(p_ks[:, h:h + 1],
                                           ktile[:, 64 * h:64 * (h + 1)],
                                           cst[:, 0:1],
                                           start=(i == 0 and h == 0),
                                           stop=(i == NT_SEQ - 1 and h == 15))
                        s_pe.bump(ins)
                        m[("p1kvmm", seq, i)] = s_pe.n

                @blk.vector
                def _(v, seq=seq):
                    v.wait_ge(s_pe.sem, m[("p1kvmm", seq, NT_SEQ - 1)])
                    s_dve.bump(v.tensor_copy(kv2[0:64, :], p_kv[:, :]))
                    v.wait_ge(s_dve.sem, s_dve.n)
                    s_dve.bump(v.tensor_copy(kv2[64:128, :], kv2[0:64, :]))
                    s_dve.bump(v.tensor_scalar_add(ksum2[0:64, :], p_ks[:, :], EPS_A))
                    v.wait_ge(s_dve.sem, s_dve.n)
                    s_dve.bump(v.tensor_copy(ksum2[64:128, :], ksum2[0:64, :]))
                    s_dve.bump(v.memset(ksblk[:, :], 0.0))
                    v.wait_ge(s_dve.sem, s_dve.n)
                    for h in range(16):
                        half = 64 * (h % 2)
                        col = 16 * (h // 2) + h
                        ins = v.tensor_copy(ksblk[half:half + 64, col:col + 1],
                                            ksum2[half:half + 64, h:h + 1])
                    s_dve.bump(ins)
                    m[("ksblk", seq)] = s_dve.n

            if not with_p2:
                continue
            with contextlib.ExitStack() as ps2:
                p_zd = _ps(ps2, nc, f"p_zd{seq}", [128, 16])
                p_o = _ps(ps2, nc, f"p_o{seq}", [128, 1024])
                p_w = _ps(ps2, nc, f"p_w{seq}", [128, 1024])
                p_lg = _ps(ps2, nc, f"p_lg{seq}", [128, 8])
                blk = ps2.enter_context(nc.Block(no_gpsimd_drain=True))

                def em_t3p(j, seq=seq):
                    if j < 0:
                        return

                    @blk.tensor
                    def _(t, j=j, seq=seq):
                        t.wait_ge(s_dve.sem, m[("p2t3", seq, j)])
                        for fc in range(8):
                            ins = t.transpose(
                                p_w[:, 512 * (fc // 4) + 128 * (fc % 4):
                                    512 * (fc // 4) + 128 * (fc % 4) + 128],
                                t3d[j % 2][:, 128 * fc:128 * (fc + 1)],
                                ident[:, :])
                        s_pe.bump(ins)
                        m[("p2t3p", seq, j)] = s_pe.n

                    @blk.vector
                    def _(v, j=j, seq=seq):
                        v.wait_ge(s_pe.sem, m[("p2t3p", seq, j)])
                        if ("p2lgmm", seq, j - 2) in m:
                            v.wait_ge(s_pe.sem, m[("p2lgmm", seq, j - 2)])
                        v.tensor_copy(t3td[j % 2][:, 0:512], p_w[:, 0:512])
                        s_dve.bump(v.tensor_copy(t3td[j % 2][:, 512:1024],
                                                 p_w[:, 512:1024]))
                        m[("p2t3te", seq, j)] = s_dve.n

                    @blk.tensor
                    def _(t, j=j, seq=seq):
                        t.wait_ge(s_dve.sem, m[("p2t3te", seq, j)])
                        if ("p2lgev", seq, j - 1) in m:
                            t.wait_ge(s_dve.sem, m[("p2lgev", seq, j - 1)])
                        for fc in range(8):
                            t.matmul(p_lg[:, :],
                                     t3td[j % 2][:, 128 * fc:128 * (fc + 1)],
                                     ggam[:, 8 * fc:8 * fc + 8],
                                     start=(fc == 0), stop=False)
                        ins = t.matmul(p_lg[:, :], onesr[:, 0:128], bgrow[:, :],
                                       start=False, stop=True)
                        s_pe.bump(ins)
                        m[("p2lgmm", seq, j)] = s_pe.n

                def em_argmax(j, seq=seq):
                    if j < 0:
                        return

                    @blk.vector
                    def _(v, j=j, seq=seq):
                        v.wait_ge(s_pe.sem, m[("p2lgmm", seq, j)])
                        if ("p2exp", seq, j - 1) in m:
                            v.wait_ge(s_act.sem, m[("p2exp", seq, j - 1)])
                        s_dve.bump(v.tensor_copy(lgt[:, :], p_lg[:, :]))
                        m[("p2lgev", seq, j)] = s_dve.n
                        v.wait_ge(s_dve.sem, s_dve.n)
                        s_dve.bump(v.reduce_max(sc["mx"][:, :], lgt[:, :], axis=AX.X))
                        v.wait_ge(s_dve.sem, s_dve.n)
                        v.tensor_scalar_mul(sc["nmx"][:, :], sc["mx"][:, :], -1.0)
                        s_dve.bump(v.tensor_scalar(eqt[:, :], lgt[:, :], sc["mx"][:, :],
                                                   None, ALU.is_ge))
                        m[("p2nmx", seq, j)] = s_dve.n
                        v.wait_ge(s_dve.sem, s_dve.n)
                        s_dve.bump(v.tensor_mul(cndt[:, :], eqt[:, :], cst[:, 2:10]))
                        v.wait_ge(s_dve.sem, s_dve.n)
                        s_dve.bump(v.reduce_max(sc["t1"][:, :], cndt[:, :], axis=AX.X))
                        v.wait_ge(s_dve.sem, s_dve.n)
                        s_dve.bump(v.tensor_scalar(sc["idx"][:, :], sc["t1"][:, :],
                                                   -1.0, 8.0, ALU.mult, ALU.add))
                        m[("p2idx", seq, j)] = s_dve.n

                    @blk.scalar
                    def _(s, j=j, seq=seq):
                        s.wait_ge(s_dve.sem, m[("p2nmx", seq, j)])
                        s_act.bump(s.activation(ext[:, :], lgt[:, :], AF.Exp,
                                                bias=sc["nmx"][:, :], scale=1.0,
                                                accum_out=sc["smx"][:, :]))
                        m[("p2exp", seq, j)] = s_act.n

                    @blk.vector
                    def _(v, j=j, seq=seq):
                        v.wait_ge(s_act.sem, m[("p2exp", seq, j)])
                        s_dve.bump(v.reciprocal(sc["wgt"][:, :], sc["smx"][:, :]))
                        v.wait_ge(s_dve.sem, s_dve.n)
                        if ("p2cnt", seq, j - 1) in m:
                            v.wait_ge(s_pe.sem, m[("p2cnt", seq, j - 1)])
                        s_dve.bump(v.tensor_scalar(oht[:, :], cst[:, 10:18],
                                                   sc["idx"][:, :], None, ALU.is_equal))
                        m[("p2oh", seq, j)] = s_dve.n
                        v.wait_ge(s_dve.sem, s_dve.n)
                        v.tensor_copy(scpkd[j % 2][:, 0:1], sc["idx"][:, :])
                        s_dve.bump(v.tensor_copy(scpkd[j % 2][:, 1:2], sc["wgt"][:, :]))
                        m[("p2scpk", seq, j)] = s_dve.n

                    @blk.gpsimd
                    def _(gp, j=j, seq=seq):
                        gg = seq * NT_SEQ + j
                        gp.wait_ge(s_dve.sem, m[("p2scpk", seq, j)])
                        s_gsc[j % 2].bump(gp.dma_start(out=scal_out[128 * gg:128 * (gg + 1), :],
                                                       in_=scpkd[j % 2][:, :]))
                        m[("p2scst", seq, j)] = s_gsc[j % 2].n
                        m[("scst_last", j % 2)] = s_gsc[j % 2].n

                def em_cnt(j, seq=seq):
                    if j < 0:
                        return

                    @blk.tensor
                    def _(t, j=j, seq=seq):
                        t.wait_ge(s_dve.sem, m[("p2oh", seq, j)])
                        ins = t.matmul(p_cnt[:, seq:seq + 1], oht[:, :], cst[:, 0:1],
                                       start=(seq == 0 and j == 0),
                                       stop=(seq == NSEQ - 1 and j == NT_SEQ - 1),
                                       skip_group_check=True)
                        s_pe.bump(ins)
                        m[("p2cnt", seq, j)] = s_pe.n

                for i in range(NT_SEQ):
                    g = t0 + i

                    @blk.sync
                    def _(sy, i=i, g=g, seq=seq):
                        if ("p2x1", seq, i - 2) in m:
                            sy.wait_ge(s_dve.sem, m[("p2x1", seq, i - 2)])
                        s_x[i % 2].bump(sy.dma_start(out=xt[i % 2][:, :],
                                                     in_=x_in[128 * g:128 * (g + 1), :]))
                        m[("p2x", seq, i)] = s_x[i % 2].n
                        if i == 0:
                            sy.wait_ge(s_gq.sem, m[("p1qst", seq, NT_SEQ - 1)])
                        if ("p2omm", seq, i - 2) in m:
                            sy.wait_ge(s_pe.sem, m[("p2omm", seq, i - 2)])
                        s_q2[i % 2].bump(sy.dma_start(out=qt[i % 2][:, :],
                                                      in_=qscr[128 * g:128 * (g + 1), :]))
                        m[("p2q", seq, i)] = s_q2[i % 2].n

                    @blk.tensor
                    def _(t, i=i, seq=seq):
                        t.wait_ge(s_q2[i % 2].sem, m[("p2q", seq, i)])
                        if i == 0:
                            t.wait_ge(s_dve.sem, m[("ksblk", seq)])
                        if ("p2z", seq, i - 1) in m:
                            t.wait_ge(s_dve.sem, m[("p2z", seq, i - 1)])
                        for fc in range(8):
                            ins = t.matmul(p_zd[:, :],
                                           qt[i % 2][:, 128 * fc:128 * (fc + 1)],
                                           ksblk[:, 16 * fc:16 * (fc + 1)],
                                           start=(fc == 0), stop=(fc == 7))
                        s_pe.bump(ins)
                        m[("p2zmm", seq, i)] = s_pe.n

                    @blk.scalar
                    def _(s, i=i, seq=seq):
                        s.wait_ge(s_pe.sem, m[("p2zmm", seq, i)])
                        if ("p2ozs", seq, i - 1) in m:
                            s.wait_ge(s_dve.sem, m[("p2ozs", seq, i - 1)])
                        s_act.bump(s.activation(zt[:, :], p_zd[:, :], AF.Copy,
                                                bias=EPS_A, scale=1.0))
                        m[("p2zev", seq, i)] = s_act.n

                    @blk.vector
                    def _(v, i=i, seq=seq):
                        v.wait_ge(s_act.sem, m[("p2zev", seq, i)])
                        s_dve.bump(v.reciprocal(z2[:, :], zt[:, :]))
                        m[("p2z", seq, i)] = s_dve.n

                    @blk.tensor
                    def _(t, i=i, seq=seq):
                        if ("p2ozte", seq, i - 1) in m:
                            t.wait_ge(s_dve.sem, m[("p2ozte", seq, i - 1)])
                        for h in range(16):
                            half = 64 * (h % 2)
                            qc = h // 2
                            col = 64 * (h // 2) + 512 * (h % 2)
                            ins = t.matmul(
                                p_o[:, col:col + 64],
                                qt[i % 2][half:half + 64, 128 * qc:128 * (qc + 1)],
                                kv2[half:half + 64, 64 * h:64 * (h + 1)],
                                start=True, stop=True)
                        s_pe.bump(ins)
                        m[("p2omm", seq, i)] = s_pe.n

                    em_t3p(i - 1)
                    em_cnt(i - 2)

                    @blk.vector
                    def _(v, i=i, seq=seq):
                        v.wait_ge(s_pe.sem, m[("p2omm", seq, i)])
                        v.wait_ge(s_dve.sem, m[("p2z", seq, i)])
                        if ("p2ozp", seq, i - 1) in m:
                            v.wait_ge(s_pe.sem, m[("p2ozp", seq, i - 1)])
                        for h in range(16):
                            col = 64 * (h // 2) + 512 * (h % 2)
                            ins = v.tensor_scalar_mul(oz[:, 64 * h:64 * (h + 1)],
                                                      p_o[:, col:col + 64],
                                                      z2[:, h:h + 1])
                        s_dve.bump(ins)
                        m[("p2ozs", seq, i)] = s_dve.n

                    @blk.tensor
                    def _(t, i=i, seq=seq):
                        t.wait_ge(s_dve.sem, m[("p2ozs", seq, i)])
                        for fc in range(8):
                            ins = t.transpose(
                                p_o[:, 512 * (fc // 4) + 128 * (fc % 4):
                                    512 * (fc // 4) + 128 * (fc % 4) + 128],
                                oz[:, 128 * fc:128 * (fc + 1)],
                                ident[:, :])
                        s_pe.bump(ins)
                        m[("p2ozp", seq, i)] = s_pe.n

                    @blk.vector
                    def _(v, i=i, seq=seq):
                        v.wait_ge(s_pe.sem, m[("p2ozp", seq, i)])
                        if ("p2wmm", seq, i - 1) in m:
                            v.wait_ge(s_pe.sem, m[("p2wmm", seq, i - 1)])
                        v.tensor_copy(ozt[:, 0:512], p_o[:, 0:512])
                        s_dve.bump(v.tensor_copy(ozt[:, 512:1024], p_o[:, 512:1024]))
                        m[("p2ozte", seq, i)] = s_dve.n

                    @blk.tensor
                    def _(t, i=i, seq=seq):
                        t.wait_ge(s_dve.sem, m[("p2ozte", seq, i)])
                        if ("p2x1", seq, i - 1) in m:
                            t.wait_ge(s_dve.sem, m[("p2x1", seq, i - 1)])
                        for oc in range(2):
                            for fc in range(8):
                                t.matmul(
                                    p_w[:, 512 * oc:512 * (oc + 1)],
                                    ozt[:, 128 * fc:128 * (fc + 1)],
                                    wout[:, fc * D + 512 * oc: fc * D + 512 * (oc + 1)],
                                    start=(fc == 0), stop=False)
                            ins = t.matmul(p_w[:, 512 * oc:512 * (oc + 1)],
                                           onesr[:, 0:128],
                                           boutr[:, 512 * oc:512 * (oc + 1)],
                                           start=False, stop=True)
                        s_pe.bump(ins)
                        m[("p2wmm", seq, i)] = s_pe.n

                    @blk.vector
                    def _(v, i=i, seq=seq):
                        v.wait_ge(s_pe.sem, m[("p2wmm", seq, i)])
                        v.wait_ge(s_x[i % 2].sem, m[("p2x", seq, i)])
                        v.wait_ge(s_gx1.sem, m.get("p2x1st_last", 0))
                        if ("p2t3", seq, i - 1) in m:
                            v.wait_ge(s_dve.sem, m[("p2t3", seq, i - 1)])
                        s_dve.bump(v.tensor_add(x1b[:, :], p_w[:, :], xt[i % 2][:, :]))
                        m[("p2x1", seq, i)] = s_dve.n

                    @blk.gpsimd
                    def _(gp, i=i, g=g, seq=seq):
                        gp.wait_ge(s_dve.sem, m[("p2x1", seq, i)])
                        s_gx1.bump(gp.dma_start(out=x1_out[128 * g:128 * (g + 1), :],
                                                in_=x1b[:, :]))
                        m[("p2x1st", seq, i)] = s_gx1.n
                        m["p2x1st_last"] = s_gx1.n

                    @blk.vector
                    def _(v, i=i, seq=seq):
                        v.wait_ge(s_dve.sem, m[("p2x1", seq, i)])
                        if ("p2r2", seq, i - 1) in m:
                            v.wait_ge(s_act.sem, m[("p2r2", seq, i - 1)])
                        v.bn_stats(bnst[:, 0:6], x1b[:, 0:512])
                        s_dve.bump(v.bn_stats(bnst[:, 6:12], x1b[:, 512:1024]))
                        v.wait_ge(s_dve.sem, s_dve.n)
                        s_dve.bump(v.bn_aggr(mv2[:, :], bnst[:, :]))
                        m[("p2var2", seq, i)] = s_dve.n

                    @blk.scalar
                    def _(s, i=i, seq=seq):
                        s.wait_ge(s_dve.sem, m[("p2var2", seq, i)])
                        if ("p2t3", seq, i - 1) in m:
                            s.wait_ge(s_dve.sem, m[("p2t3", seq, i - 1)])
                        s.wait_ge(s_act.sem, m.get("t1s", 0))
                        s_act.bump(s.activation(sc["t1s"][:, :], mv2[:, 1:2],
                                                AF.Ln, bias=cst[:, 18:19], scale=1.0))
                        s.wait_ge(s_act.sem, s_act.n)
                        s_act.bump(s.activation(sc["r2"][:, :], sc["t1s"][:, :],
                                                AF.Exp, bias=0.0, scale=-0.5))
                        m[("p2r2", seq, i)] = s_act.n
                        m["t1s"] = s_act.n

                    @blk.vector
                    def _(v, i=i, seq=seq):
                        v.wait_ge(s_act.sem, m[("p2r2", seq, i)])
                        if ("p2r3", seq, i - 1) in m:
                            v.wait_ge(s_act.sem, m[("p2r3", seq, i - 1)])
                        s_dve.bump(v.tensor_mul(sc["t0"][:, :], sc["r2"][:, :],
                                                sc["r2"][:, :]))
                        v.wait_ge(s_dve.sem, s_dve.n)
                        s_dve.bump(v.tensor_mul(sc["var3"][:, :], mv2[:, 1:2],
                                                sc["t0"][:, :]))
                        m[("p2v3", seq, i)] = s_dve.n

                    @blk.scalar
                    def _(s, i=i, seq=seq):
                        s.wait_ge(s_dve.sem, m[("p2v3", seq, i)])
                        s.wait_ge(s_act.sem, m.get("t2s", 0))
                        s_act.bump(s.activation(sc["sq3"][:, :], sc["var3"][:, :],
                                                AF.Ln, bias=cst[:, 18:19], scale=1.0))
                        s.wait_ge(s_act.sem, s_act.n)
                        s_act.bump(s.activation(sc["r3"][:, :], sc["sq3"][:, :],
                                                AF.Exp, bias=0.0, scale=-0.5))
                        m[("p2r3", seq, i)] = s_act.n
                        m["t2s"] = s_act.n

                    @blk.vector
                    def _(v, i=i, seq=seq):
                        v.wait_ge(s_act.sem, m[("p2r3", seq, i)])
                        s_dve.bump(v.tensor_mul(sc["ss"][:, :], sc["r2"][:, :],
                                                sc["r3"][:, :]))
                        v.wait_ge(s_dve.sem, s_dve.n)
                        v.wait_ge(s_gsc[i % 2].sem, m.get(("scst_last", i % 2), 0))
                        v.tensor_copy(scpkd[i % 2][:, 2:3], mv2[:, 0:1])
                        v.tensor_copy(scpkd[i % 2][:, 3:4], sc["r2"][:, :])
                        v.tensor_copy(scpkd[i % 2][:, 4:5], mv2[:, 0:1])
                        v.tensor_copy(scpkd[i % 2][:, 5:6], sc["ss"][:, :])
                        s_dve.bump(v.memset(scpkd[i % 2][:, 6:8], 0.0))
                        m[("p2scpk1", seq, i)] = s_dve.n
                        v.wait_ge(s_dve.sem, s_dve.n)
                        if ("p2t3p", seq, i - 2) in m:
                            v.wait_ge(s_pe.sem, m[("p2t3p", seq, i - 2)])
                        s_dve.bump(v.tensor_scalar(t3d[i % 2][:, :], x1b[:, :],
                                                   mv2[:, 0:1], sc["ss"][:, :],
                                                   ALU.subtract, ALU.mult))
                        m[("p2t3", seq, i)] = s_dve.n

                    em_argmax(i - 1)

                em_t3p(NT_SEQ - 1)
                em_cnt(NT_SEQ - 2)
                em_argmax(NT_SEQ - 1)
                em_cnt(NT_SEQ - 1)


        with nc.Block(no_gpsimd_drain=True) as blk:
            @blk.vector
            def _(v):
                v.wait_ge(s_pe.sem, s_pe.n)
                v.wait_ge(s_act.sem, s_act.n)
                s_dve.bump(v.tensor_copy(cntsb[:, :], p_cnt[:, :]))

            @blk.gpsimd
            def _(gp):
                gp.wait_ge(s_dve.sem, s_dve.n)
                s_gq.bump(gp.dma_start(out=cnt_out[:, :], in_=cntsb[:, :]))
                gp.wait_ge(s_gq.sem, s_gq.n)
                gp.wait_ge(s_gx1.sem, s_gx1.n)
                gp.wait_ge(s_gsc[0].sem, s_gsc[0].n)
                gp.wait_ge(s_gsc[1].sem, s_gsc[1].n)

    return nc


def build_b():
    nc = bass.Bass("TRN2", target_bir_lowering=False)
    x1_in = nc.dram_tensor("x1", [TOK, D], F32, kind="ExternalInput")
    scal_in = nc.dram_tensor("scal", [TOK, 8], F32, kind="ExternalInput")
    cnt_in = nc.dram_tensor("cntall", [16, 8], F32, kind="ExternalInput")
    mask_in = nc.dram_tensor("maskcol", [16, 2], F32, kind="ExternalInput")
    wi_in = nc.dram_tensor("wi", [NE * D, I_FF], F32, kind="ExternalInput")
    wo_in = nc.dram_tensor("wo", [NE * I_FF, D], F32, kind="ExternalInput")
    gam_in = nc.dram_tensor("gam", [128, 8], F32, kind="ExternalInput")
    bet_in = nc.dram_tensor("bet", [128, 8], F32, kind="ExternalInput")
    ident_in = nc.dram_tensor("ident", [128, 128], F32, kind="ExternalInput")
    cst_in = nc.dram_tensor("cst", [128, 32], F32, kind="ExternalInput")
    ones_in = nc.dram_tensor("onesrow", [1, 128], F32, kind="ExternalInput")
    toku_in = nc.dram_tensor("toku", [128, NT], U32, kind="ExternalInput")
    tokf_in = nc.dram_tensor("tokf", [128, NT], F32, kind="ExternalInput")
    out_dram = nc.dram_tensor("out", [TOK, D], F32, kind="ExternalOutput")

    gidx = nc.dram_tensor("gidx_d", [GIDX_ROWS, 1], U32)
    eobuf = nc.dram_tensor("eobuf", [NSLOT, D], F32)

    ctx = contextlib.ExitStack()
    with ctx:
        gamv = _sb(ctx, nc, "gamv_sb", [128, 8])
        betv = _sb(ctx, nc, "betv_sb", [128, 8])
        ident = _sb(ctx, nc, "ident_sb", [128, 128])
        cst = _sb(ctx, nc, "cst_sb", [128, 32])
        onesr = _sb(ctx, nc, "onesr", [1, 128])
        toku = _sb(ctx, nc, "toku_sb", [128, NT], U32)
        tokf = _sb(ctx, nc, "tokf_sb", [128, NT])
        cntsb = _sb(ctx, nc, "cntsb", [16, 8])
        masksb = _sb(ctx, nc, "masksb", [16, 2])
        offs = _sb(ctx, nc, "offs", [8, 2])
        scal = _sb(ctx, nc, "scal_sb", [128, 8 * NT])
        rowt = _sb(ctx, nc, "rowt", [1, 128])
        keepc = _sb(ctx, nc, "keepc", [128, NT])
        slotc = _sb(ctx, nc, "slotc", [128, NT])
        posu = _sb(ctx, nc, "posu", [128, NT], U32)
        gposu = _sb(ctx, nc, "gposu", [128, NT], U32)
        wkf = _sb(ctx, nc, "wkf", [128, NT])
        zb = _sb(ctx, nc, "zb", [128, 64], U32)
        tp = {k: _sb(ctx, nc, f"tp_{k}", [128, 1]) for k in
              ["a", "b", "c", "k2", "pf", "gf"]}
        gxi = _sb(ctx, nc, "gxi", [128, 1], U32)
        dsc = _sb(ctx, nc, "dsc", [128, 8])

        s_ld = Ch(ctx.enter_context(nc.semaphore("s_ld")), 16)
        s_pe = Ch(ctx.enter_context(nc.semaphore("s_pe")))
        s_act = Ch(ctx.enter_context(nc.semaphore("s_act")))
        s_dve = Ch(ctx.enter_context(nc.semaphore("s_dve")))
        s_gs = Ch(ctx.enter_context(nc.semaphore("s_gs")), 16)
        s_gz = Ch(ctx.enter_context(nc.semaphore("s_gz")))
        s_gx = Ch(ctx.enter_context(nc.semaphore("s_gx")), 16)
        s_gd = Ch(ctx.enter_context(nc.semaphore("s_gd")), 16)
        s_geo = [Ch(ctx.enter_context(nc.semaphore(f"s_geo{j}")), 16) for j in range(2)]
        s_geg = [Ch(ctx.enter_context(nc.semaphore(f"s_geg{j}")), 16) for j in range(4)]
        s_w = [Ch(ctx.enter_context(nc.semaphore(f"s_w{j}")), 16) for j in range(2)]
        s_wp = Ch(ctx.enter_context(nc.semaphore("s_wp")), 16)
        s_st = [Ch(ctx.enter_context(nc.semaphore(f"s_st{j}")), 16) for j in range(4)]
        s_x1 = [Ch(ctx.enter_context(nc.semaphore(f"s_x1{j}")), 16) for j in range(4)]

        m = {}

        with nc.psum_tensor("p_off", [8, 2], F32) as p_off, nc.Block(no_gpsimd_drain=True) as blk:
            @blk.sync
            def _(sy):
                for dst, src in [(gamv, gam_in), (betv, bet_in), (ident, ident_in),
                                 (cst, cst_in), (onesr, ones_in), (toku, toku_in),
                                 (tokf, tokf_in), (cntsb, cnt_in), (masksb, mask_in)]:
                    s_ld.bump(sy.dma_start(out=dst[:, :], in_=src[:, :]))
                for t in range(NT):
                    s_ld.bump(sy.dma_start(out=scal[:, 8 * t:8 * (t + 1)],
                                           in_=scal_in[128 * t:128 * (t + 1), :]))

            @blk.gpsimd
            def _(gp):
                gp.memset(zb[:, :], 0)
                s_gz.bump(gp.engine_nop())
                gp.wait_ge(s_gz.sem, s_gz.n)
                s_gs.bump(gp.dma_start(
                    out=gidx[:, :].rearrange("(a b) c -> a (b c)", a=128),
                    in_=zb[:, :]))
                m["gzero"] = s_gs.n

            @blk.tensor
            def _(t):
                t.wait_ge(s_ld.sem, s_ld.n)
                s_pe.bump(t.matmul(p_off[:, :], cntsb[:, :], masksb[:, :],
                                   start=True, stop=True))

            @blk.vector
            def _(v):
                v.wait_ge(s_pe.sem, s_pe.n)
                s_dve.bump(v.tensor_copy(offs[:, :], p_off[:, :]))
                m["offs"] = s_dve.n

            @blk.scalar
            def _(s):
                s.preload_act_table(AF.Gelu)

        with contextlib.ExitStack() as ps:
            ohall = _sb(ps, nc, "ohall", [8, TOK])
            scn = _sb(ps, nc, "scn", [8, TOK])
            rke = _sb(ps, nc, "rke", [8, TOK])
            k8 = _sb(ps, nc, "k8", [8, TOK])
            tmp8 = scn
            sr8 = ohall
            p_r = _ps(ps, nc, "p_r", [1, 128])
            p_b = _ps(ps, nc, "p_b", [8, 128])
            p_k = _ps(ps, nc, "p_k", [128, 2])
            blk = ps.enter_context(nc.Block(no_gpsimd_drain=True))

            for i in range(NT):
                @blk.tensor
                def _(t, i=i):
                    if i == 0:
                        t.wait_ge(s_dve.sem, m["offs"])
                    else:
                        t.wait_ge(s_dve.sem, m[("row", i - 1)])
                    s_pe.bump(t.transpose(p_r[:, :], scal[:, 8 * i:8 * i + 1],
                                          ident[:, :]))
                    m[("rt", i)] = s_pe.n

                @blk.vector
                def _(v, i=i):
                    v.wait_ge(s_pe.sem, m[("rt", i)])
                    if ("bc", i - 1) in m:
                        v.wait_ge(s_pe.sem, m[("bc", i - 1)])
                    s_dve.bump(v.tensor_copy(rowt[:, :], p_r[:, :]))
                    m[("row", i)] = s_dve.n

                @blk.tensor
                def _(t, i=i):
                    t.wait_ge(s_dve.sem, m[("row", i)])
                    if ("ohc", i - 1) in m:
                        t.wait_ge(s_dve.sem, m[("ohc", i - 1)])
                    s_pe.bump(t.matmul(p_b[:, :], onesr[:, 0:8], rowt[:, :],
                                       start=True, stop=True))
                    m[("bc", i)] = s_pe.n

                @blk.vector
                def _(v, i=i):
                    v.wait_ge(s_pe.sem, m[("bc", i)])
                    s_dve.bump(v.tensor_scalar(ohall[:, 128 * i:128 * (i + 1)],
                                               p_b[:, :], cst[0:8, 1:2], None,
                                               ALU.is_equal))
                    m[("ohc", i)] = s_dve.n

            @blk.vector
            def _(v):
                v.wait_ge(s_dve.sem, m[("ohc", NT - 1)])
                s_dve.bump(v.tensor_tensor_scan(scn[:, :], ohall[:, :], ohall[:, :],
                                                0.0, ALU.add, ALU.bypass))
                v.wait_ge(s_dve.sem, s_dve.n)
                s_dve.bump(v.tensor_sub(rke[:, :], scn[:, :], ohall[:, :]))
                v.wait_ge(s_dve.sem, s_dve.n)
                s_dve.bump(v.tensor_scalar(tmp8[:, 0:SEQLEN], rke[:, 0:SEQLEN],
                                           offs[:, 0:1], CAP, ALU.add, ALU.is_lt))
                s_dve.bump(v.tensor_scalar(tmp8[:, SEQLEN:TOK], rke[:, SEQLEN:TOK],
                                           offs[:, 1:2], CAP, ALU.add, ALU.is_lt))
                v.wait_ge(s_dve.sem, s_dve.n)
                s_dve.bump(v.tensor_mul(k8[:, :], tmp8[:, :], ohall[:, :]))
                v.wait_ge(s_dve.sem, s_dve.n)
                s_dve.bump(v.tensor_mul(sr8[:, :], rke[:, :], k8[:, :]))
                m["scan"] = s_dve.n

            for i in range(NT):
                @blk.tensor
                def _(t, i=i):
                    t.wait_ge(s_dve.sem, m["scan"])
                    if ("kse", i - 1) in m:
                        t.wait_ge(s_dve.sem, m[("kse", i - 1)])
                    t.matmul(p_k[:, 0:1], k8[:, 128 * i:128 * (i + 1)],
                             cst[0:8, 0:1], start=True, stop=True)
                    s_pe.bump(t.matmul(p_k[:, 1:2], sr8[:, 128 * i:128 * (i + 1)],
                                       cst[0:8, 0:1], start=True, stop=True,
                                       skip_group_check=True))
                    m[("ks", i)] = s_pe.n

                @blk.vector
                def _(v, i=i):
                    v.wait_ge(s_pe.sem, m[("ks", i)])
                    v.tensor_copy(keepc[:, i:i + 1], p_k[:, 0:1])
                    s_dve.bump(v.tensor_copy(slotc[:, i:i + 1], p_k[:, 1:2]))
                    m[("kse", i)] = s_dve.n
                    v.wait_ge(s_dve.sem, s_dve.n)
                    v.scalar_tensor_tensor(tp["a"][:, :], scal[:, 8 * i:8 * i + 1],
                                           float(C_PAD), slotc[:, i:i + 1],
                                           ALU.mult, ALU.add)
                    s_dve.bump(v.tensor_scalar(tp["b"][:, :], slotc[:, i:i + 1],
                                               float(C_PAD), None, ALU.is_lt))
                    v.wait_ge(s_dve.sem, s_dve.n)
                    s_dve.bump(v.tensor_mul(tp["k2"][:, :], keepc[:, i:i + 1],
                                            tp["b"][:, :]))
                    v.wait_ge(s_dve.sem, s_dve.n)
                    v.tensor_scalar(tp["c"][:, :], tokf[:, i:i + 1], 1.0, TRASH0,
                                    ALU.mult, ALU.add)
                    s_dve.bump(v.tensor_mul(tp["gf"][:, :], tp["a"][:, :], tp["k2"][:, :]))
                    v.wait_ge(s_dve.sem, s_dve.n)
                    s_dve.bump(v.tensor_mul(wkf[:, i:i + 1],
                                            scal[:, 8 * i + 1:8 * i + 2], tp["k2"][:, :]))
                    v.wait_ge(s_dve.sem, s_dve.n)
                    v.tensor_mul(tp["pf"][:, :], tp["c"][:, :], tp["k2"][:, :])
                    s_dve.bump(v.engine_nop())
                    v.wait_ge(s_dve.sem, s_dve.n)
                    s_dve.bump(v.tensor_sub(tp["b"][:, :], tp["c"][:, :], tp["pf"][:, :]))
                    v.wait_ge(s_dve.sem, s_dve.n)
                    s_dve.bump(v.tensor_add(tp["pf"][:, :], tp["gf"][:, :], tp["b"][:, :]))
                    v.wait_ge(s_dve.sem, s_dve.n)
                    v.tensor_copy(posu[:, i:i + 1], tp["pf"][:, :])
                    s_dve.bump(v.tensor_copy(gposu[:, i:i + 1], tp["gf"][:, :]))
                    m[("pos", i)] = s_dve.n

            @blk.gpsimd
            def _(gp):
                gp.wait_ge(s_gs.sem, m["gzero"])
                for i in range(NT):
                    gp.wait_ge(s_dve.sem, m[("pos", i)])
                    s_gs.bump(gp.indirect_dma_start(
                        out=gidx[:, :],
                        out_offset=bass.IndirectOffsetOnAxis(ap=posu[:, i:i + 1], axis=0),
                        in_=toku[:, i:i + 1], in_offset=None))
                m["scat"] = s_gs.n

        with contextlib.ExitStack() as fstack:
            drow = _sb(fstack, nc, "drow", [128, D])
            dt3 = _sb(fstack, nc, "dt3", [128, D])
            dspt = [_sb(fstack, nc, f"dspt{i}", [128, ST_E * D]) for i in range(2)]
            wmov = [_sb(fstack, nc, f"wmov{i}", [128, 8 * 512]) for i in range(2)]
            wipre = _sb(fstack, nc, "wipre", [128, 8 * 512])
            hsb = [_sb(fstack, nc, f"hsb{i}", [128, I_FF]) for i in range(ST_E)]
            eosb = [_sb(fstack, nc, f"eosb{i}", [128, D]) for i in range(2)]
            p_h = [_ps(fstack, nc, f"p_h{j}", [128, 512]) for j in range(2)]
            p_t2 = _ps(fstack, nc, "p_t2", [128, 1024])
            p_eo = [_ps(fstack, nc, f"p_eo{j}", [128, 512]) for j in range(4)]

            def em_dgather(blk, de, j):
                st = ST_E * de + j

                @blk.gpsimd
                def _(gp, j=j, st=st, de=de):
                    if de == 0 and j == 0:
                        gp.wait_ge(s_gs.sem, m["scat"])
                    prev = (de, j - 1) if j > 0 else (de - 1, ST_E - 1)
                    if ("dt3",) + prev in m:
                        gp.wait_ge(s_dve.sem, m[("dt3",) + prev])
                    s_gx.bump(gp.dma_start(out=gxi[:, :],
                                           in_=gidx[128 * st:128 * (st + 1), :]))
                    gp.wait_ge(s_gx.sem, s_gx.n)
                    s_gd.bump(gp.indirect_dma_start(
                        out=drow[:, :], out_offset=None, in_=x1_in[:, :],
                        in_offset=bass.IndirectOffsetOnAxis(ap=gxi[:, :], axis=0)))
                    s_gd.bump(gp.indirect_dma_start(
                        out=dsc[:, :], out_offset=None, in_=scal_in[:, :],
                        in_offset=bass.IndirectOffsetOnAxis(ap=gxi[:, :], axis=0)))
                    m[("dg", de, j)] = s_gd.n

                @blk.vector
                def _(v, j=j, de=de):
                    v.wait_ge(s_gd.sem, m[("dg", de, j)])
                    s_dve.bump(v.tensor_scalar_sub(dt3[:, :], drow[:, :],
                                                   dsc[:, 4:5]))
                    v.wait_ge(s_dve.sem, s_dve.n)
                    s_dve.bump(v.tensor_scalar_mul(dt3[:, :], dt3[:, :],
                                                   dsc[:, 5:6]))
                    m[("dt3", de, j)] = s_dve.n

            def em_dtp(blk, de, j, pc, host_e):
                @blk.tensor
                def _(t, j=j, pc=pc, de=de, host_e=host_e):
                    if pc == 0:
                        t.wait_ge(s_dve.sem, m[("dt3", de, j)])
                    if host_e is not None and j == 0 and pc == 0:
                        t.wait_ge(s_act.sem,
                                  m[("hge", host_e, 7, ST_E - 1)])
                    prev = (de, j - 1, pc) if j > 0 else (de - 1, ST_E - 1, pc)
                    if ("dte",) + prev in m:
                        t.wait_ge(s_dve.sem, m[("dte",) + prev])
                    for k in range(4):
                        fc = 4 * pc + k
                        ins = t.transpose(
                            p_h[pc][:, 128 * k:128 * (k + 1)],
                            dt3[:, 128 * fc:128 * (fc + 1)],
                            ident[:, :])
                    s_pe.bump(ins)
                    m[("dtp", de, j, pc)] = s_pe.n

                @blk.vector
                def _(v, j=j, pc=pc, de=de):
                    v.wait_ge(s_pe.sem, m[("dtp", de, j, pc)])
                    for k in range(4):
                        fc = 4 * pc + k
                        ins = v.tensor_scalar(
                            dspt[de % 2][:, j * D + 128 * fc: j * D + 128 * (fc + 1)],
                            p_h[pc][:, 128 * k:128 * (k + 1)],
                            gamv[:, fc:fc + 1], betv[:, fc:fc + 1],
                            ALU.mult, ALU.add)
                    s_dve.bump(ins)
                    m[("dte", de, j, pc)] = s_dve.n

            def em_dispatch(blk, de, host_e):
                for j in range(ST_E):
                    em_dgather(blk, de, j)
                    for pc in range(2):
                        em_dtp(blk, de, j, pc, host_e)

            for e in range(NE):
                    with nc.Block(no_gpsimd_drain=True) as blk:
                        if e == 0:
                            em_dispatch(blk, 0, None)

                        for ic in range(8):
                            if not (ic == 0 and e > 0):
                                @blk.sync
                                def _(sy, e=e, ic=ic):
                                    if ("wim", e, ic - 2) in m:
                                        sy.wait_ge(s_pe.sem, m[("wim", e, ic - 2)])
                                    if e > 0 and ic == 1:
                                        sy.wait_ge(s_pe.sem, m[("wom", e - 1, 1, 31)])
                                    if e > 0 and ic == 0:
                                        sy.wait_ge(s_pe.sem, m[("wom", e - 1, 1, 30)])
                                    for fc in range(8):
                                        s_w[ic % 2].bump(sy.dma_start(
                                            out=wmov[ic % 2][:, 512 * fc:512 * (fc + 1)],
                                            in_=wi_in[D * e + 128 * fc: D * e + 128 * (fc + 1),
                                                      512 * ic:512 * (ic + 1)]))
                                    m[("wil", e, ic)] = s_w[ic % 2].n

                            for j in range(ST_E):
                                @blk.tensor
                                def _(t, e=e, ic=ic, j=j):
                                    pre = (ic == 0 and e > 0)
                                    if pre:
                                        t.wait_ge(s_wp.sem, m[("wipre", e)])
                                    else:
                                        t.wait_ge(s_w[ic % 2].sem, m[("wil", e, ic)])
                                    if ic == 0 and j == 0:
                                        t.wait_ge(s_dve.sem, m[("dte", e, ST_E - 1, 1)])
                                    pj = (ic * ST_E + j) % 2
                                    prev = ic * ST_E + j - 2
                                    if prev >= 0:
                                        pic, pjj = divmod(prev, ST_E)
                                        t.wait_ge(s_act.sem, m[("hge", e, pic, pjj)])
                                    src = wipre if pre else wmov[ic % 2]
                                    for fc in range(8):
                                        ins = t.matmul(
                                            p_h[pj][:, :],
                                            dspt[e % 2][:, j * D + 128 * fc: j * D + 128 * (fc + 1)],
                                            src[:, 512 * fc:512 * (fc + 1)],
                                            start=(fc == 0), stop=(fc == 7))
                                    s_pe.bump(ins)
                                    m[("wim", e, ic, j)] = s_pe.n
                                    if j == ST_E - 1:
                                        m[("wim", e, ic)] = s_pe.n

                                @blk.scalar
                                def _(s, e=e, ic=ic, j=j):
                                    s.wait_ge(s_pe.sem, m[("wim", e, ic, j)])
                                    pj = (ic * ST_E + j) % 2
                                    s_act.bump(s.activation(
                                        hsb[j][:, 512 * ic:512 * (ic + 1)],
                                        p_h[pj][:, :], AF.Gelu, bias=0.0, scale=1.0))
                                    m[("hge", e, ic, j)] = s_act.n

                        if e + 1 < NE:
                            em_dgather(blk, e + 1, 0)
                        for j in range(ST_E):
                            for gg in range(8):
                                if gg == 4 and e + 1 < NE and j < ST_E:
                                    for pc in range(2):
                                        em_dtp(blk, e + 1, j, pc, e)
                                    if j + 1 < ST_E:
                                        em_dgather(blk, e + 1, j + 1)

                                @blk.tensor
                                def _(t, e=e, j=j, gg=gg):
                                    if gg == 0:
                                        t.wait_ge(s_act.sem, m[("hge", e, 7, j)])
                                    cc = j * 8 + gg
                                    if cc >= 2:
                                        pcc = cc - 2
                                        t.wait_ge(s_dve.sem, m[("hte", e, pcc // 8, pcc % 8)])
                                    bank = 512 * (gg % 2)
                                    for k in range(4):
                                        c = 4 * gg + k
                                        ins = t.transpose(
                                            p_t2[:, bank + 128 * k:bank + 128 * (k + 1)],
                                            hsb[j][:, 128 * c:128 * (c + 1)], ident[:, :])
                                    s_pe.bump(ins)
                                    m[("htp", e, j, gg)] = s_pe.n

                                @blk.vector
                                def _(v, e=e, j=j, gg=gg):
                                    v.wait_ge(s_pe.sem, m[("htp", e, j, gg)])
                                    bank = 512 * (gg % 2)
                                    s_dve.bump(v.tensor_copy(
                                        hsb[j][:, 512 * gg:512 * (gg + 1)],
                                        p_t2[:, bank:bank + 512]))
                                    m[("hte", e, j, gg)] = s_dve.n

                        for hc in range(2):
                            for ic in range(32):
                                @blk.sync
                                def _(sy, e=e, hc=hc, ic=ic):
                                    k = hc * 32 + ic
                                    if k >= 2:
                                        pk = k - 2
                                        t_pe = m[("wom", e, pk // 32, pk % 32)]
                                        sy.wait_ge(s_pe.sem, t_pe)
                                    if k < 2:
                                        sy.wait_ge(s_pe.sem, m[("wim", e, 6 + k)])
                                    s_w[ic % 2].bump(sy.dma_start(
                                        out=wmov[ic % 2][:, 0:512],
                                        in_=wo_in[I_FF * e + 128 * ic: I_FF * e + 128 * (ic + 1),
                                                  512 * hc:512 * (hc + 1)]))
                                    m[("wol", e, hc, ic)] = s_w[ic % 2].n

                                if e + 1 < NE and hc == 1 and ic < 8:
                                    @blk.sync
                                    def _(sy, e=e, fc=ic):
                                        if fc == 0:
                                            sy.wait_ge(s_pe.sem, m[("wim", e, 0)])
                                        s_wp.bump(sy.dma_start(
                                            out=wipre[:, 512 * fc:512 * (fc + 1)],
                                            in_=wi_in[D * (e + 1) + 128 * fc:
                                                      D * (e + 1) + 128 * (fc + 1),
                                                      0:512]))
                                        if fc == 7:
                                            m[("wipre", e + 1)] = s_wp.n

                                @blk.tensor
                                def _(t, e=e, hc=hc, ic=ic):
                                    t.wait_ge(s_w[ic % 2].sem, m[("wol", e, hc, ic)])
                                    if ic == 0:
                                        t.wait_ge(s_dve.sem, m[("hte", e, ST_E - 1, 7)])
                                        if hc == 1:
                                            for j in range(ST_E):
                                                t.wait_ge(s_dve.sem, m[("eoe", e, 0, j)])
                                    for j in range(ST_E):
                                        ins = t.matmul(
                                            p_eo[j][:, :],
                                            hsb[j][:, 128 * ic:128 * (ic + 1)],
                                            wmov[ic % 2][:, 0:512],
                                            start=(ic == 0), stop=(ic == 31))
                                    s_pe.bump(ins)
                                    m[("wom", e, hc, ic)] = s_pe.n

                            for j in range(ST_E):
                                @blk.vector
                                def _(v, e=e, hc=hc, j=j):
                                    v.wait_ge(s_pe.sem, m[("wom", e, hc, 31)])
                                    v.wait_ge(s_geo[j % 2].sem, m.get(("eost_last", j % 2), 0))
                                    s_dve.bump(v.tensor_copy(
                                        eosb[j % 2][:, 512 * hc:512 * (hc + 1)],
                                        p_eo[j][:, :]))
                                    m[("eoe", e, hc, j)] = s_dve.n

                                @blk.gpsimd
                                def _(gp, e=e, hc=hc, j=j):
                                    gp.wait_ge(s_dve.sem, m[("eoe", e, hc, j)])
                                    st = ST_E * e + j
                                    s_geo[j % 2].bump(gp.dma_start(
                                        out=eobuf[128 * st:128 * (st + 1),
                                                  512 * hc:512 * (hc + 1)],
                                        in_=eosb[j % 2][:, 512 * hc:512 * (hc + 1)]))
                                    m[("eost", e, hc, j)] = s_geo[j % 2].n
                                    m[("eost_last", j % 2)] = s_geo[j % 2].n

        with contextlib.ExitStack() as fin:
            x1t = [_sb(fin, nc, f"x1t{i}", [128, D]) for i in range(4)]
            eog = [_sb(fin, nc, f"eog{i}", [128, D]) for i in range(4)]
            xn2f = _sb(fin, nc, "xn2f", [128, D])
            res2 = _sb(fin, nc, "res2", [128, D])
            outf = [_sb(fin, nc, f"outf{i}", [128, D]) for i in range(4)]
            with nc.Block(no_gpsimd_drain=True) as blk:
                for i in range(NT):
                    @blk.gpsimd
                    def _(gp, i=i):
                        if i == 0:
                            for j in range(2):
                                gp.wait_ge(s_geo[j].sem, s_geo[j].n)
                        if ("comb", i - 4) in m:
                            gp.wait_ge(s_dve.sem, m[("comb", i - 4)])
                        s_geg[i % 4].bump(gp.indirect_dma_start(
                            out=eog[i % 4][:, :], out_offset=None, in_=eobuf[:, :],
                            in_offset=bass.IndirectOffsetOnAxis(ap=gposu[:, i:i + 1], axis=0)))
                        m[("eogth", i)] = s_geg[i % 4].n

                    @blk.scalar
                    def _(s, i=i):
                        if ("comb", i - 4) in m:
                            s.wait_ge(s_dve.sem, m[("comb", i - 4)])
                        s_x1[i % 4].bump(s.dma_start(out=x1t[i % 4][:, :],
                                                     in_=x1_in[128 * i:128 * (i + 1), :]))
                        m[("x1l", i)] = s_x1[i % 4].n

                    @blk.vector
                    def _(v, i=i):
                        v.wait_ge(s_geg[i % 4].sem, m[("eogth", i)])
                        v.wait_ge(s_x1[i % 4].sem, m[("x1l", i)])
                        if ("comb", i - 1) in m:
                            v.wait_ge(s_dve.sem, m[("comb", i - 1)])
                        if ("outst", i - 4) in m:
                            v.wait_ge(s_st[i % 4].sem, m[("outst", i - 4)])
                        s_dve.bump(v.tensor_scalar(xn2f[:, :], x1t[i % 4][:, :],
                                                   scal[:, 8 * i + 2:8 * i + 3],
                                                   scal[:, 8 * i + 3:8 * i + 4],
                                                   ALU.subtract, ALU.mult))
                        v.wait_ge(s_dve.sem, s_dve.n)
                        s_dve.bump(v.scalar_tensor_tensor(res2[:, :], eog[i % 4][:, :],
                                                          wkf[:, i:i + 1], x1t[i % 4][:, :],
                                                          ALU.mult, ALU.add))
                        v.wait_ge(s_dve.sem, s_dve.n)
                        s_dve.bump(v.tensor_add(outf[i % 4][:, :], res2[:, :],
                                                xn2f[:, :]))
                        m[("comb", i)] = s_dve.n

                    @blk.sync
                    def _(sy, i=i):
                        sy.wait_ge(s_dve.sem, m[("comb", i)])
                        s_st[i % 4].bump(sy.dma_start(out=out_dram[128 * i:128 * (i + 1), :],
                                                      in_=outf[i % 4][:, :]))
                        m[("outst", i)] = s_st[i % 4].n

                @blk.gpsimd
                def _(gp):
                    for j in range(4):
                        gp.wait_ge(s_st[j].sem, s_st[j].n)

    return nc


def _consts():
    ident = np.eye(128, dtype=np.float32)
    cst = np.zeros((128, 32), np.float32)
    cst[:, 0] = 1.0
    cst[:, 1] = np.arange(128, dtype=np.float32)
    cst[:, 18] = EPS_LN
    for j in range(8):
        cst[:, 2 + j] = 8.0 - j
        cst[:, 10 + j] = float(j)
    onesrow = np.ones((1, 128), np.float32)
    tok = (np.arange(NT, dtype=np.uint32)[None, :] * 128
           + np.arange(128, dtype=np.uint32)[:, None])
    return ident, cst, onesrow, tok.astype(np.uint32), tok.astype(np.float32)


def _prep_a_inputs(x_full, Wqkv, Wout, bout, gate, gamma, beta):
    ident, cst, onesrow, _, _ = _consts()
    gateT = np.ascontiguousarray(np.transpose(gate))
    gt = np.ascontiguousarray(
        gateT.reshape(8, 128, 8).transpose(1, 0, 2).reshape(128, 64))
    gam = np.ascontiguousarray(gamma.reshape(8, 128).T)
    bet = np.ascontiguousarray(beta.reshape(8, 128).T)
    maps = []
    for c in range(NCORE):
        xs = np.ascontiguousarray(
            np.concatenate([x_full[c], x_full[c + 8]], axis=0))
        maps.append({
            "x": xs, "wqkv": Wqkv, "wout": Wout,
            "bout": np.ascontiguousarray(bout.reshape(1, D)),
            "gt": gt, "gam": gam, "bet": bet,
            "ident": ident, "cst": cst, "onesrow": onesrow,
        })
    return maps


def kernel(L0_attn_x, L0_attn_Wqkv, L0_attn_Wout, L0_attn_bout,
           L0_ffn_gate_weight, L0_ffn_ln_gamma, L0_ffn_ln_beta,
           L0_ffn_wi, L0_ffn_wo, L0_ffn_conf_w1, L0_ffn_conf_b1,
           L0_ffn_conf_w2, L0_ffn_conf_b2, L0_ffn_comp_w, L0_ffn_comp_b):
    x = np.ascontiguousarray(np.asarray(L0_attn_x, dtype=np.float32))
    Wqkv = np.ascontiguousarray(np.asarray(L0_attn_Wqkv, dtype=np.float32))
    Wout = np.ascontiguousarray(np.asarray(L0_attn_Wout, dtype=np.float32))
    bout = np.asarray(L0_attn_bout, dtype=np.float32)
    gate = np.asarray(L0_ffn_gate_weight, dtype=np.float32)
    gamma = np.asarray(L0_ffn_ln_gamma, dtype=np.float32)
    beta = np.asarray(L0_ffn_ln_beta, dtype=np.float32)
    wi = np.ascontiguousarray(np.asarray(L0_ffn_wi, dtype=np.float32)
                              .reshape(NE * D, I_FF))
    wo = np.ascontiguousarray(np.asarray(L0_ffn_wo, dtype=np.float32)
                              .reshape(NE * I_FF, D))

    ident, cst, onesrow, toku, tokf = _consts()

    nc_a = build_a()
    maps_a = _prep_a_inputs(x, Wqkv, Wout, bout, gate, gamma, beta)
    res_a = run_bass_kernel_spmd(nc_a, maps_a, core_ids=list(range(NCORE)))

    cnt_all = np.zeros((16, 8), np.float32)
    for c in range(NCORE):
        cnt_all[c] = res_a.results[c]["cnto"][:, 0]
        cnt_all[c + 8] = res_a.results[c]["cnto"][:, 1]
    cnt_all = np.ascontiguousarray(cnt_all)

    gam = np.ascontiguousarray(gamma.reshape(8, 128).T)
    bet = np.ascontiguousarray(beta.reshape(8, 128).T)
    nc_b = build_b()
    maps_b = []
    for c in range(NCORE):
        b = np.arange(16)
        mask1 = (b < c).astype(np.float32)
        mask2 = ((b < 8 + c) & (b != c)).astype(np.float32)
        maskcol = np.ascontiguousarray(np.stack([mask1, mask2], axis=1))
        maps_b.append({
            "x1": np.ascontiguousarray(res_a.results[c]["x1o"]),
            "scal": np.ascontiguousarray(res_a.results[c]["scalo"]),
            "cntall": cnt_all, "maskcol": maskcol,
            "wi": wi, "wo": wo, "gam": gam, "bet": bet,
            "ident": ident, "cst": cst, "onesrow": onesrow,
            "toku": toku, "tokf": tokf,
        })
    res_b = run_bass_kernel_spmd(nc_b, maps_b, core_ids=list(range(NCORE)))

    out = np.empty((16, SEQLEN, D), np.float32)
    for c in range(NCORE):
        o = res_b.results[c]["out"]
        out[c] = o[:SEQLEN]
        out[c + 8] = o[SEQLEN:]
    return out


# version 08
# speedup vs baseline: 1.0268x; 1.0268x over previous
import contextlib
import numpy as np
import concourse.bass as bass
import concourse.mybir as mybir
from concourse.bass_utils import run_bass_kernel_spmd

F32 = mybir.dt.float32
U32 = mybir.dt.uint32
AF = mybir.ActivationFunctionType
ALU = mybir.AluOpType
AX = mybir.AxisListType

D = 1024
NCORE = 8
SEQLEN = 2048
NSEQ = 2
TOK = NSEQ * SEQLEN
NT_SEQ = SEQLEN // 128
NT = TOK // 128
HEADS = 16
NE = 8
CAP = 2048.0
C_PAD = 512
NSLOT = NE * C_PAD
ST_E = C_PAD // 128
TRASH0 = float(NSLOT)
GIDX_ROWS = NSLOT + TOK
I_FF = 4096
EPS_A = 1e-6
EPS_LN = 1e-5
INV_D = 1.0 / D


class Ch:
    def __init__(self, sem, step=1):
        self.sem, self.step, self.n = sem, step, 0

    def bump(self, instr):
        instr.then_inc(self.sem, self.step)
        self.n += self.step
        return self.n


def _sb(ctx, nc, name, shape, dt=F32):
    return ctx.enter_context(nc.sbuf_tensor(name, shape, dt))


def _ps(ctx, nc, name, shape):
    return ctx.enter_context(nc.psum_tensor(name, shape, F32))


def build_a(n_seq=NSEQ, with_p1=True, with_p2=True, p2s=9):
    nc = bass.Bass("TRN2", target_bir_lowering=False)
    x_in = nc.dram_tensor("x", [TOK, D], F32, kind="ExternalInput")
    wqkv_in = nc.dram_tensor("wqkv", [D, 3 * D], F32, kind="ExternalInput")
    wout_in = nc.dram_tensor("wout", [D, D], F32, kind="ExternalInput")
    bout_in = nc.dram_tensor("bout", [1, D], F32, kind="ExternalInput")
    gt_in = nc.dram_tensor("gt", [128, 64], F32, kind="ExternalInput")
    gam_in = nc.dram_tensor("gam", [128, 8], F32, kind="ExternalInput")
    bet_in = nc.dram_tensor("bet", [128, 8], F32, kind="ExternalInput")
    ident_in = nc.dram_tensor("ident", [128, 128], F32, kind="ExternalInput")
    cst_in = nc.dram_tensor("cst", [128, 32], F32, kind="ExternalInput")
    ones_in = nc.dram_tensor("onesrow", [1, 128], F32, kind="ExternalInput")

    x1_out = nc.dram_tensor("x1o", [TOK, D], F32, kind="ExternalOutput")
    scal_out = nc.dram_tensor("scalo", [TOK, 8], F32, kind="ExternalOutput")
    cnt_out = nc.dram_tensor("cnto", [8, 2], F32, kind="ExternalOutput")
    qscr = nc.dram_tensor("qscr", [TOK, D], F32)

    ctx = contextlib.ExitStack()
    with ctx:
        wq = _sb(ctx, nc, "wq", [128, 8 * D])
        wkv = _sb(ctx, nc, "wkv", [128, 8 * 2048])
        wout = _sb(ctx, nc, "wout_sb", [128, 8 * D])
        boutr = _sb(ctx, nc, "boutr", [1, D])
        gt = _sb(ctx, nc, "gt_sb", [128, 64])
        gamv = _sb(ctx, nc, "gamv_sb", [128, 8])
        betv = _sb(ctx, nc, "betv_sb", [128, 8])
        ident = _sb(ctx, nc, "ident_sb", [128, 128])
        cst = _sb(ctx, nc, "cst_sb", [128, 32])
        onesr = _sb(ctx, nc, "onesr", [1, 128])
        ggam = _sb(ctx, nc, "ggam", [128, 64])
        bgrow = _sb(ctx, nc, "bgrow", [1, 8])

        xt = [_sb(ctx, nc, f"xt{i}", [128, D]) for i in range(2)]
        xn1 = _sb(ctx, nc, "xn1", [128, D])
        xn1t = _sb(ctx, nc, "xn1t", [128, D])
        qsb = _sb(ctx, nc, "qsb", [128, D])
        ktile = _sb(ctx, nc, "ktile", [128, D])
        vtile = _sb(ctx, nc, "vtile", [128, D])
        kv2 = _sb(ctx, nc, "kv2", [128, D])
        ksum2 = _sb(ctx, nc, "ksum2", [128, 16])
        ksblk = _sb(ctx, nc, "ksblk", [128, 128])
        qt = [_sb(ctx, nc, f"qt{i}", [128, D]) for i in range(2)]
        zt = _sb(ctx, nc, "zt", [128, 16])
        z2 = _sb(ctx, nc, "z2", [128, 16])
        oz = _sb(ctx, nc, "oz", [128, D])
        ozt = _sb(ctx, nc, "ozt", [128, D])
        x1b = _sb(ctx, nc, "x1b", [128, D])
        t3d = [_sb(ctx, nc, f"t3d{i}", [128, D]) for i in range(2)]
        t3td = [_sb(ctx, nc, f"t3td{i}", [128, D]) for i in range(2)]
        lgt = _sb(ctx, nc, "lgt", [128, 8])
        eqt = _sb(ctx, nc, "eqt", [128, 8])
        ext = _sb(ctx, nc, "ext", [128, 8])
        cndt = _sb(ctx, nc, "cndt", [128, 8])
        oht = _sb(ctx, nc, "oht", [128, 8])
        scpkd = [_sb(ctx, nc, f"scpk{i}", [128, 8]) for i in range(2)]
        cntsb = _sb(ctx, nc, "cntsb", [8, 2])
        SC = ["s1c", "s1q", "m1", "msq", "ex2", "var", "sq1", "r1",
              "s2c", "s2q", "m2", "m2sq", "ex22", "var2", "sq2", "r2",
              "s3c", "s3q", "m3", "m3sq", "ex23", "var3", "sq3", "r3",
              "cc", "ss", "mx", "nmx", "idx", "wgt", "smx", "t0", "t1", "t1s"]
        sc = {k: _sb(ctx, nc, f"sc_{k}", [128, 1]) for k in SC}
        bnst = _sb(ctx, nc, "bnst", [128, 12])
        mv1 = _sb(ctx, nc, "mv1", [128, 2])
        mv2 = _sb(ctx, nc, "mv2", [128, 2])
        dumm = _sb(ctx, nc, "dumm", [128, 2])

        p_cnt = _ps(ctx, nc, "p_cnt", [8, 2])

        s_ld = Ch(ctx.enter_context(nc.semaphore("s_ld")), 16)
        s_x = [Ch(ctx.enter_context(nc.semaphore(f"s_x{j}")), 16) for j in range(2)]
        s_pe = Ch(ctx.enter_context(nc.semaphore("s_pe")))
        s_act = Ch(ctx.enter_context(nc.semaphore("s_act")))
        s_dve = Ch(ctx.enter_context(nc.semaphore("s_dve")))
        s_gq = Ch(ctx.enter_context(nc.semaphore("s_gq")), 16)
        s_gx1 = Ch(ctx.enter_context(nc.semaphore("s_gx1")), 16)
        s_gsc = [Ch(ctx.enter_context(nc.semaphore(f"s_gsc{j}")), 16) for j in range(2)]
        s_q2 = [Ch(ctx.enter_context(nc.semaphore(f"s_q2{j}")), 16) for j in range(2)]

        with nc.psum_tensor("p_bg", [1, 8], F32) as p_bg, nc.Block(no_gpsimd_drain=True) as blk:
            @blk.sync
            def _(sy):
                for fc in range(8):
                    s_ld.bump(sy.dma_start(out=wq[:, fc * D:(fc + 1) * D],
                                           in_=wqkv_in[128 * fc:128 * (fc + 1), 0:D]))
                    s_ld.bump(sy.dma_start(out=wkv[:, fc * 2048:(fc + 1) * 2048],
                                           in_=wqkv_in[128 * fc:128 * (fc + 1), D:3 * D]))
                    s_ld.bump(sy.dma_start(out=wout[:, fc * D:(fc + 1) * D],
                                           in_=wout_in[128 * fc:128 * (fc + 1), :]))
                for dst, src in [(boutr, bout_in), (gt, gt_in), (gamv, gam_in),
                                 (betv, bet_in), (ident, ident_in), (cst, cst_in),
                                 (onesr, ones_in)]:
                    s_ld.bump(sy.dma_start(out=dst[:, :], in_=src[:, :]))

            @blk.vector
            def _(v):
                v.wait_ge(s_ld.sem, s_ld.n)
                for fc in range(8):
                    s_dve.bump(v.tensor_scalar_mul(ggam[:, 8 * fc:8 * fc + 8],
                                                   gt[:, 8 * fc:8 * fc + 8],
                                                   gamv[:, fc:fc + 1]))

            @blk.tensor
            def _(t):
                t.wait_ge(s_dve.sem, s_dve.n)
                for fc in range(8):
                    ins = t.matmul(p_bg[:, :], betv[:, fc:fc + 1],
                                   gt[:, 8 * fc:8 * fc + 8],
                                   start=(fc == 0), stop=(fc == 7))
                s_pe.bump(ins)

            @blk.vector
            def _(v):
                v.wait_ge(s_pe.sem, s_pe.n)
                s_dve.bump(v.tensor_copy(bgrow[:, :], p_bg[:, :]))

            @blk.scalar
            def _(s):
                s.add_instruction(mybir.InstLoadActFuncSet(
                    name=nc.get_next_instruction_name(),
                    act_func_set_id=6, ins=[], outs=[]))

        m = {}

        for seq in range(n_seq):
            t0 = seq * NT_SEQ

            if not with_p1:
                continue
            with contextlib.ExitStack() as ps1:
                p_kv = _ps(ps1, nc, f"p_kv{seq}", [64, 1024])
                p_ks = _ps(ps1, nc, f"p_ks{seq}", [64, 16])
                p_q = _ps(ps1, nc, f"p_q{seq}", [128, 1024])
                p_pj = _ps(ps1, nc, f"p_pj{seq}", [128, 1024])
                blk = ps1.enter_context(nc.Block(no_gpsimd_drain=True))

                for i in range(NT_SEQ):
                    g = t0 + i

                    @blk.sync
                    def _(sy, i=i, g=g, seq=seq):
                        if ("p1xn1", seq, i - 2) in m:
                            sy.wait_ge(s_dve.sem, m[("p1xn1", seq, i - 2)])
                        s_x[i % 2].bump(sy.dma_start(out=xt[i % 2][:, :],
                                                     in_=x_in[128 * g:128 * (g + 1), :]))
                        m[("p1x", seq, i)] = s_x[i % 2].n

                    @blk.vector
                    def _(v, i=i, seq=seq):
                        v.wait_ge(s_x[i % 2].sem, m[("p1x", seq, i)])
                        if ("p1r1", seq, i - 1) in m:
                            v.wait_ge(s_act.sem, m[("p1r1", seq, i - 1)])
                        v.bn_stats(bnst[:, 0:6], xt[i % 2][:, 0:512])
                        s_dve.bump(v.bn_stats(bnst[:, 6:12], xt[i % 2][:, 512:1024]))
                        v.wait_ge(s_dve.sem, s_dve.n)
                        s_dve.bump(v.bn_aggr(mv1[:, :], bnst[:, :]))
                        v.wait_ge(s_dve.sem, s_dve.n)
                        s_dve.bump(v.tensor_copy(dumm[:, :], mv1[:, :]))
                        m[("p1var", seq, i)] = s_dve.n

                    @blk.scalar
                    def _(s, i=i, seq=seq):
                        s.wait_ge(s_dve.sem, m[("p1var", seq, i)])
                        if ("p1xn1", seq, i - 1) in m:
                            s.wait_ge(s_dve.sem, m[("p1xn1", seq, i - 1)])
                        s.wait_ge(s_act.sem, m.get("t1s", 0))
                        s_act.bump(s.activation(sc["t1s"][:, :], mv1[:, 1:2],
                                                AF.Ln, bias=cst[:, 18:19], scale=1.0))
                        s.wait_ge(s_act.sem, s_act.n)
                        s_act.bump(s.activation(sc["r1"][:, :], sc["t1s"][:, :],
                                                AF.Exp, bias=0.0, scale=-0.5))
                        m[("p1r1", seq, i)] = s_act.n
                        m["t1s"] = s_act.n

                    @blk.vector
                    def _(v, i=i, seq=seq):
                        v.wait_ge(s_act.sem, m[("p1r1", seq, i)])
                        if ("p1xp", seq, i - 1) in m:
                            v.wait_ge(s_pe.sem, m[("p1xp", seq, i - 1)])
                        s_dve.bump(v.tensor_scalar(xn1[:, :], xt[i % 2][:, :],
                                                   mv1[:, 0:1], sc["r1"][:, :],
                                                   ALU.subtract, ALU.mult))
                        m[("p1xn1", seq, i)] = s_dve.n

                    @blk.tensor
                    def _(t, i=i, seq=seq):
                        t.wait_ge(s_dve.sem, m[("p1xn1", seq, i)])
                        if ("p1vev", seq, i - 1) in m:
                            t.wait_ge(s_act.sem, m[("p1vev", seq, i - 1)])
                        for fc in range(8):
                            ins = t.transpose(p_pj[:, 128 * fc:128 * (fc + 1)],
                                              xn1[:, 128 * fc:128 * (fc + 1)],
                                              ident[:, :])
                        s_pe.bump(ins)
                        m[("p1xp", seq, i)] = s_pe.n

                    @blk.scalar
                    def _(s, i=i, seq=seq):
                        s.wait_ge(s_pe.sem, m[("p1xp", seq, i)])
                        if ("p1vmm", seq, i - 1) in m:
                            s.wait_ge(s_pe.sem, m[("p1vmm", seq, i - 1)])
                        s_act.bump(s.activation(xn1t[:, :], p_pj[:, :], AF.Copy,
                                                bias=0.0, scale=1.0))
                        m[("p1xn1t", seq, i)] = s_act.n

                    @blk.tensor
                    def _(t, i=i, seq=seq):
                        t.wait_ge(s_act.sem, m[("p1xn1t", seq, i)])
                        if ("p1qev", seq, i - 1) in m:
                            t.wait_ge(s_act.sem, m[("p1qev", seq, i - 1)])
                        for qc in range(8):
                            for fc in range(8):
                                ins = t.matmul(
                                    p_q[:, 128 * qc:128 * (qc + 1)],
                                    wq[:, fc * D + 128 * qc: fc * D + 128 * (qc + 1)],
                                    xn1t[:, 128 * fc:128 * (fc + 1)],
                                    start=(fc == 0 and qc % 4 == 0),
                                    stop=(fc == 7 and qc % 4 == 3))
                        s_pe.bump(ins)
                        m[("p1qmm", seq, i)] = s_pe.n

                    @blk.scalar
                    def _(s, i=i, seq=seq):
                        s.wait_ge(s_pe.sem, m[("p1qmm", seq, i)])
                        s.wait_ge(s_gq.sem, m.get("p1qst_last", 0))
                        s_act.bump(s.activation(qsb[:, :], p_q[:, :], AF.Relu,
                                                bias=0.0, scale=1.0))
                        m[("p1qev", seq, i)] = s_act.n

                    @blk.gpsimd
                    def _(gp, i=i, g=g, seq=seq):
                        gp.wait_ge(s_act.sem, m[("p1qev", seq, i)])
                        s_gq.bump(gp.dma_start(out=qscr[128 * g:128 * (g + 1), :],
                                               in_=qsb[:, :]))
                        m[("p1qst", seq, i)] = s_gq.n
                        m["p1qst_last"] = s_gq.n

                    @blk.tensor
                    def _(t, i=i, seq=seq):
                        for h4 in range(2):
                            for fc in range(8):
                                ins = t.matmul(
                                    p_pj[:, 512 * h4:512 * (h4 + 1)],
                                    xn1t[:, 128 * fc:128 * (fc + 1)],
                                    wkv[:, fc * 2048 + 512 * h4: fc * 2048 + 512 * (h4 + 1)],
                                    start=(fc == 0), stop=(fc == 7))
                        s_pe.bump(ins)
                        m[("p1kmm", seq, i)] = s_pe.n

                    @blk.scalar
                    def _(s, i=i, seq=seq):
                        s.wait_ge(s_pe.sem, m[("p1kmm", seq, i)])
                        if ("p1kvmm", seq, i - 1) in m:
                            s.wait_ge(s_pe.sem, m[("p1kvmm", seq, i - 1)])
                        s_act.bump(s.activation(ktile[:, :], p_pj[:, :], AF.Relu,
                                                bias=0.0, scale=1.0))
                        m[("p1kev", seq, i)] = s_act.n

                    @blk.tensor
                    def _(t, i=i, seq=seq):
                        t.wait_ge(s_act.sem, m[("p1kev", seq, i)])
                        for h4 in range(2):
                            for fc in range(8):
                                ins = t.matmul(
                                    p_pj[:, 512 * h4:512 * (h4 + 1)],
                                    xn1t[:, 128 * fc:128 * (fc + 1)],
                                    wkv[:, fc * 2048 + 1024 + 512 * h4: fc * 2048 + 1024 + 512 * (h4 + 1)],
                                    start=(fc == 0), stop=(fc == 7))
                        s_pe.bump(ins)
                        m[("p1vmm", seq, i)] = s_pe.n

                    @blk.scalar
                    def _(s, i=i, seq=seq):
                        s.wait_ge(s_pe.sem, m[("p1vmm", seq, i)])
                        s_act.bump(s.activation(vtile[:, :], p_pj[:, :], AF.Copy,
                                                bias=0.0, scale=1.0))
                        m[("p1vev", seq, i)] = s_act.n

                    @blk.tensor
                    def _(t, i=i, seq=seq):
                        t.wait_ge(s_act.sem, m[("p1vev", seq, i)])
                        for h in range(16):
                            t.matmul(p_kv[:, 64 * h:64 * (h + 1)],
                                     ktile[:, 64 * h:64 * (h + 1)],
                                     vtile[:, 64 * h:64 * (h + 1)],
                                     start=(i == 0 and h % 8 == 0),
                                     stop=(i == NT_SEQ - 1 and h % 8 == 7))
                            ins = t.matmul(p_ks[:, h:h + 1],
                                           ktile[:, 64 * h:64 * (h + 1)],
                                           cst[:, 0:1],
                                           start=(i == 0 and h == 0),
                                           stop=(i == NT_SEQ - 1 and h == 15))
                        s_pe.bump(ins)
                        m[("p1kvmm", seq, i)] = s_pe.n

                @blk.vector
                def _(v, seq=seq):
                    v.wait_ge(s_pe.sem, m[("p1kvmm", seq, NT_SEQ - 1)])
                    s_dve.bump(v.tensor_copy(kv2[0:64, :], p_kv[:, :]))
                    v.wait_ge(s_dve.sem, s_dve.n)
                    s_dve.bump(v.tensor_copy(kv2[64:128, :], kv2[0:64, :]))
                    s_dve.bump(v.tensor_scalar_add(ksum2[0:64, :], p_ks[:, :], EPS_A))
                    v.wait_ge(s_dve.sem, s_dve.n)
                    s_dve.bump(v.tensor_copy(ksum2[64:128, :], ksum2[0:64, :]))
                    s_dve.bump(v.memset(ksblk[:, :], 0.0))
                    v.wait_ge(s_dve.sem, s_dve.n)
                    for h in range(16):
                        half = 64 * (h % 2)
                        col = 16 * (h // 2) + h
                        ins = v.tensor_copy(ksblk[half:half + 64, col:col + 1],
                                            ksum2[half:half + 64, h:h + 1])
                    s_dve.bump(ins)
                    m[("ksblk", seq)] = s_dve.n

            if not with_p2:
                continue
            with contextlib.ExitStack() as ps2:
                p_zd = _ps(ps2, nc, f"p_zd{seq}", [128, 16])
                p_o = _ps(ps2, nc, f"p_o{seq}", [128, 1024])
                p_w = _ps(ps2, nc, f"p_w{seq}", [128, 1024])
                p_lg = _ps(ps2, nc, f"p_lg{seq}", [128, 8])
                blk = ps2.enter_context(nc.Block(no_gpsimd_drain=True))

                def em_t3p(j, seq=seq):
                    if j < 0:
                        return

                    @blk.tensor
                    def _(t, j=j, seq=seq):
                        t.wait_ge(s_dve.sem, m[("p2t3", seq, j)])
                        for fc in range(8):
                            ins = t.transpose(
                                p_w[:, 512 * (fc // 4) + 128 * (fc % 4):
                                    512 * (fc // 4) + 128 * (fc % 4) + 128],
                                t3d[j % 2][:, 128 * fc:128 * (fc + 1)],
                                ident[:, :])
                        s_pe.bump(ins)
                        m[("p2t3p", seq, j)] = s_pe.n

                    @blk.scalar
                    def _(s, j=j, seq=seq):
                        s.wait_ge(s_pe.sem, m[("p2t3p", seq, j)])
                        if ("p2lgmm", seq, j - 2) in m:
                            s.wait_ge(s_pe.sem, m[("p2lgmm", seq, j - 2)])
                        s.activation(t3td[j % 2][:, 0:512], p_w[:, 0:512],
                                     AF.Copy, bias=0.0, scale=1.0)
                        s_act.bump(s.activation(t3td[j % 2][:, 512:1024],
                                                p_w[:, 512:1024],
                                                AF.Copy, bias=0.0, scale=1.0))
                        m[("p2t3te", seq, j)] = s_act.n

                    @blk.tensor
                    def _(t, j=j, seq=seq):
                        t.wait_ge(s_act.sem, m[("p2t3te", seq, j)])
                        if ("p2lgev", seq, j - 1) in m:
                            t.wait_ge(s_dve.sem, m[("p2lgev", seq, j - 1)])
                        for fc in range(8):
                            t.matmul(p_lg[:, :],
                                     t3td[j % 2][:, 128 * fc:128 * (fc + 1)],
                                     ggam[:, 8 * fc:8 * fc + 8],
                                     start=(fc == 0), stop=False)
                        ins = t.matmul(p_lg[:, :], onesr[:, 0:128], bgrow[:, :],
                                       start=False, stop=True)
                        s_pe.bump(ins)
                        m[("p2lgmm", seq, j)] = s_pe.n

                def em_argmax(j, seq=seq):
                    if j < 0:
                        return

                    @blk.vector
                    def _(v, j=j, seq=seq):
                        v.wait_ge(s_pe.sem, m[("p2lgmm", seq, j)])
                        if ("p2exp", seq, j - 1) in m:
                            v.wait_ge(s_act.sem, m[("p2exp", seq, j - 1)])
                        s_dve.bump(v.tensor_copy(lgt[:, :], p_lg[:, :]))
                        m[("p2lgev", seq, j)] = s_dve.n
                        v.wait_ge(s_dve.sem, s_dve.n)
                        s_dve.bump(v.reduce_max(sc["mx"][:, :], lgt[:, :], axis=AX.X))
                        v.wait_ge(s_dve.sem, s_dve.n)
                        v.tensor_scalar_mul(sc["nmx"][:, :], sc["mx"][:, :], -1.0)
                        s_dve.bump(v.tensor_scalar(eqt[:, :], lgt[:, :], sc["mx"][:, :],
                                                   None, ALU.is_ge))
                        m[("p2nmx", seq, j)] = s_dve.n
                        v.wait_ge(s_dve.sem, s_dve.n)
                        s_dve.bump(v.tensor_mul(cndt[:, :], eqt[:, :], cst[:, 2:10]))
                        v.wait_ge(s_dve.sem, s_dve.n)
                        s_dve.bump(v.reduce_max(sc["t1"][:, :], cndt[:, :], axis=AX.X))
                        v.wait_ge(s_dve.sem, s_dve.n)
                        s_dve.bump(v.tensor_scalar(sc["idx"][:, :], sc["t1"][:, :],
                                                   -1.0, 8.0, ALU.mult, ALU.add))
                        m[("p2idx", seq, j)] = s_dve.n

                    @blk.scalar
                    def _(s, j=j, seq=seq):
                        s.wait_ge(s_dve.sem, m[("p2nmx", seq, j)])
                        s_act.bump(s.activation(ext[:, :], lgt[:, :], AF.Exp,
                                                bias=sc["nmx"][:, :], scale=1.0,
                                                accum_out=sc["smx"][:, :]))
                        m[("p2exp", seq, j)] = s_act.n

                    @blk.vector
                    def _(v, j=j, seq=seq):
                        v.wait_ge(s_act.sem, m[("p2exp", seq, j)])
                        s_dve.bump(v.reciprocal(sc["wgt"][:, :], sc["smx"][:, :]))
                        v.wait_ge(s_dve.sem, s_dve.n)
                        if ("p2cnt", seq, j - 1) in m:
                            v.wait_ge(s_pe.sem, m[("p2cnt", seq, j - 1)])
                        s_dve.bump(v.tensor_scalar(oht[:, :], cst[:, 10:18],
                                                   sc["idx"][:, :], None, ALU.is_equal))
                        m[("p2oh", seq, j)] = s_dve.n
                        v.wait_ge(s_dve.sem, s_dve.n)
                        v.tensor_copy(scpkd[j % 2][:, 0:1], sc["idx"][:, :])
                        s_dve.bump(v.tensor_copy(scpkd[j % 2][:, 1:2], sc["wgt"][:, :]))
                        m[("p2scpk", seq, j)] = s_dve.n

                    @blk.gpsimd
                    def _(gp, j=j, seq=seq):
                        gg = seq * NT_SEQ + j
                        gp.wait_ge(s_dve.sem, m[("p2scpk", seq, j)])
                        s_gsc[j % 2].bump(gp.dma_start(out=scal_out[128 * gg:128 * (gg + 1), :],
                                                       in_=scpkd[j % 2][:, :]))
                        m[("p2scst", seq, j)] = s_gsc[j % 2].n
                        m[("scst_last", j % 2)] = s_gsc[j % 2].n

                def em_cnt(j, seq=seq):
                    if j < 0:
                        return

                    @blk.tensor
                    def _(t, j=j, seq=seq):
                        t.wait_ge(s_dve.sem, m[("p2oh", seq, j)])
                        ins = t.matmul(p_cnt[:, seq:seq + 1], oht[:, :], cst[:, 0:1],
                                       start=(seq == 0 and j == 0),
                                       stop=(seq == NSEQ - 1 and j == NT_SEQ - 1),
                                       skip_group_check=True)
                        s_pe.bump(ins)
                        m[("p2cnt", seq, j)] = s_pe.n

                for i in range(NT_SEQ):
                    g = t0 + i

                    @blk.sync
                    def _(sy, i=i, g=g, seq=seq):
                        if ("p2x1", seq, i - 2) in m:
                            sy.wait_ge(s_dve.sem, m[("p2x1", seq, i - 2)])
                        s_x[i % 2].bump(sy.dma_start(out=xt[i % 2][:, :],
                                                     in_=x_in[128 * g:128 * (g + 1), :]))
                        m[("p2x", seq, i)] = s_x[i % 2].n
                        if i == 0:
                            sy.wait_ge(s_gq.sem, m[("p1qst", seq, NT_SEQ - 1)])
                        if ("p2omm", seq, i - 2) in m:
                            sy.wait_ge(s_pe.sem, m[("p2omm", seq, i - 2)])
                        s_q2[i % 2].bump(sy.dma_start(out=qt[i % 2][:, :],
                                                      in_=qscr[128 * g:128 * (g + 1), :]))
                        m[("p2q", seq, i)] = s_q2[i % 2].n

                    @blk.tensor
                    def _(t, i=i, seq=seq):
                        t.wait_ge(s_q2[i % 2].sem, m[("p2q", seq, i)])
                        if i == 0:
                            t.wait_ge(s_dve.sem, m[("ksblk", seq)])
                        if ("p2z", seq, i - 1) in m:
                            t.wait_ge(s_dve.sem, m[("p2z", seq, i - 1)])
                        for fc in range(8):
                            ins = t.matmul(p_zd[:, :],
                                           qt[i % 2][:, 128 * fc:128 * (fc + 1)],
                                           ksblk[:, 16 * fc:16 * (fc + 1)],
                                           start=(fc == 0), stop=(fc == 7))
                        s_pe.bump(ins)
                        m[("p2zmm", seq, i)] = s_pe.n

                    @blk.scalar
                    def _(s, i=i, seq=seq):
                        s.wait_ge(s_pe.sem, m[("p2zmm", seq, i)])
                        if ("p2ozs", seq, i - 1) in m:
                            s.wait_ge(s_dve.sem, m[("p2ozs", seq, i - 1)])
                        s_act.bump(s.activation(zt[:, :], p_zd[:, :], AF.Copy,
                                                bias=EPS_A, scale=1.0))
                        m[("p2zev", seq, i)] = s_act.n

                    @blk.vector
                    def _(v, i=i, seq=seq):
                        v.wait_ge(s_act.sem, m[("p2zev", seq, i)])
                        s_dve.bump(v.reciprocal(z2[:, :], zt[:, :]))
                        m[("p2z", seq, i)] = s_dve.n

                    @blk.tensor
                    def _(t, i=i, seq=seq):
                        if ("p2ozte", seq, i - 1) in m:
                            t.wait_ge(s_act.sem, m[("p2ozte", seq, i - 1)])
                        for h in range(16):
                            half = 64 * (h % 2)
                            qc = h // 2
                            col = 64 * (h // 2) + 512 * (h % 2)
                            ins = t.matmul(
                                p_o[:, col:col + 64],
                                qt[i % 2][half:half + 64, 128 * qc:128 * (qc + 1)],
                                kv2[half:half + 64, 64 * h:64 * (h + 1)],
                                start=True, stop=True)
                        s_pe.bump(ins)
                        m[("p2omm", seq, i)] = s_pe.n

                    em_t3p(i - 1)
                    em_cnt(i - 2)

                    @blk.vector
                    def _(v, i=i, seq=seq):
                        v.wait_ge(s_pe.sem, m[("p2omm", seq, i)])
                        v.wait_ge(s_dve.sem, m[("p2z", seq, i)])
                        if ("p2ozp", seq, i - 1) in m:
                            v.wait_ge(s_pe.sem, m[("p2ozp", seq, i - 1)])
                        out_ap = oz[:, :].rearrange("p (a b c) -> p a b c",
                                                    a=8, b=2, c=64)
                        in_ap = p_o[:, :].rearrange("p (b a c) -> p a b c",
                                                    b=2, a=8, c=64)
                        z_ap = z2[:, :].rearrange("p (a b) -> p a b",
                                                  a=8, b=2).unsqueeze(3)
                        z_ap = z_ap.broadcast_to([128, 8, 2, 64])
                        s_dve.bump(v.tensor_tensor(out_ap, in_ap, z_ap, ALU.mult))
                        m[("p2ozs", seq, i)] = s_dve.n

                    @blk.tensor
                    def _(t, i=i, seq=seq):
                        t.wait_ge(s_dve.sem, m[("p2ozs", seq, i)])
                        for fc in range(8):
                            ins = t.transpose(
                                p_o[:, 512 * (fc // 4) + 128 * (fc % 4):
                                    512 * (fc // 4) + 128 * (fc % 4) + 128],
                                oz[:, 128 * fc:128 * (fc + 1)],
                                ident[:, :])
                        s_pe.bump(ins)
                        m[("p2ozp", seq, i)] = s_pe.n

                    @blk.scalar
                    def _(s, i=i, seq=seq):
                        s.wait_ge(s_pe.sem, m[("p2ozp", seq, i)])
                        if ("p2wmm", seq, i - 1) in m:
                            s.wait_ge(s_pe.sem, m[("p2wmm", seq, i - 1)])
                        s.activation(ozt[:, 0:512], p_o[:, 0:512], AF.Copy,
                                     bias=0.0, scale=1.0)
                        s_act.bump(s.activation(ozt[:, 512:1024], p_o[:, 512:1024],
                                                AF.Copy, bias=0.0, scale=1.0))
                        m[("p2ozte", seq, i)] = s_act.n

                    @blk.tensor
                    def _(t, i=i, seq=seq):
                        t.wait_ge(s_act.sem, m[("p2ozte", seq, i)])
                        if ("p2x1", seq, i - 1) in m:
                            t.wait_ge(s_dve.sem, m[("p2x1", seq, i - 1)])
                        for oc in range(2):
                            for fc in range(8):
                                t.matmul(
                                    p_w[:, 512 * oc:512 * (oc + 1)],
                                    ozt[:, 128 * fc:128 * (fc + 1)],
                                    wout[:, fc * D + 512 * oc: fc * D + 512 * (oc + 1)],
                                    start=(fc == 0), stop=False)
                            ins = t.matmul(p_w[:, 512 * oc:512 * (oc + 1)],
                                           onesr[:, 0:128],
                                           boutr[:, 512 * oc:512 * (oc + 1)],
                                           start=False, stop=True)
                        s_pe.bump(ins)
                        m[("p2wmm", seq, i)] = s_pe.n

                    @blk.vector
                    def _(v, i=i, seq=seq):
                        v.wait_ge(s_pe.sem, m[("p2wmm", seq, i)])
                        v.wait_ge(s_x[i % 2].sem, m[("p2x", seq, i)])
                        v.wait_ge(s_gx1.sem, m.get("p2x1st_last", 0))
                        if ("p2t3", seq, i - 1) in m:
                            v.wait_ge(s_dve.sem, m[("p2t3", seq, i - 1)])
                        s_dve.bump(v.tensor_add(x1b[:, :], p_w[:, :], xt[i % 2][:, :]))
                        m[("p2x1", seq, i)] = s_dve.n

                    @blk.gpsimd
                    def _(gp, i=i, g=g, seq=seq):
                        gp.wait_ge(s_dve.sem, m[("p2x1", seq, i)])
                        s_gx1.bump(gp.dma_start(out=x1_out[128 * g:128 * (g + 1), :],
                                                in_=x1b[:, :]))
                        m[("p2x1st", seq, i)] = s_gx1.n
                        m["p2x1st_last"] = s_gx1.n

                    @blk.vector
                    def _(v, i=i, seq=seq):
                        v.wait_ge(s_dve.sem, m[("p2x1", seq, i)])
                        if ("p2r2", seq, i - 1) in m:
                            v.wait_ge(s_act.sem, m[("p2r2", seq, i - 1)])
                        v.bn_stats(bnst[:, 0:6], x1b[:, 0:512])
                        s_dve.bump(v.bn_stats(bnst[:, 6:12], x1b[:, 512:1024]))
                        v.wait_ge(s_dve.sem, s_dve.n)
                        s_dve.bump(v.bn_aggr(mv2[:, :], bnst[:, :]))
                        m[("p2var2", seq, i)] = s_dve.n

                    @blk.scalar
                    def _(s, i=i, seq=seq):
                        s.wait_ge(s_dve.sem, m[("p2var2", seq, i)])
                        if ("p2t3", seq, i - 1) in m:
                            s.wait_ge(s_dve.sem, m[("p2t3", seq, i - 1)])
                        s.wait_ge(s_act.sem, m.get("t1s", 0))
                        s_act.bump(s.activation(sc["t1s"][:, :], mv2[:, 1:2],
                                                AF.Ln, bias=cst[:, 18:19], scale=1.0))
                        s.wait_ge(s_act.sem, s_act.n)
                        s_act.bump(s.activation(sc["r2"][:, :], sc["t1s"][:, :],
                                                AF.Exp, bias=0.0, scale=-0.5))
                        m[("p2r2", seq, i)] = s_act.n
                        m["t1s"] = s_act.n

                    @blk.vector
                    def _(v, i=i, seq=seq):
                        v.wait_ge(s_act.sem, m[("p2r2", seq, i)])
                        if ("p2r3", seq, i - 1) in m:
                            v.wait_ge(s_act.sem, m[("p2r3", seq, i - 1)])
                        s_dve.bump(v.tensor_mul(sc["t0"][:, :], sc["r2"][:, :],
                                                sc["r2"][:, :]))
                        v.wait_ge(s_dve.sem, s_dve.n)
                        s_dve.bump(v.tensor_mul(sc["var3"][:, :], mv2[:, 1:2],
                                                sc["t0"][:, :]))
                        m[("p2v3", seq, i)] = s_dve.n

                    @blk.scalar
                    def _(s, i=i, seq=seq):
                        s.wait_ge(s_dve.sem, m[("p2v3", seq, i)])
                        s.wait_ge(s_act.sem, m.get("t2s", 0))
                        s_act.bump(s.activation(sc["sq3"][:, :], sc["var3"][:, :],
                                                AF.Ln, bias=cst[:, 18:19], scale=1.0))
                        s.wait_ge(s_act.sem, s_act.n)
                        s_act.bump(s.activation(sc["r3"][:, :], sc["sq3"][:, :],
                                                AF.Exp, bias=0.0, scale=-0.5))
                        m[("p2r3", seq, i)] = s_act.n
                        m["t2s"] = s_act.n

                    @blk.vector
                    def _(v, i=i, seq=seq):
                        v.wait_ge(s_act.sem, m[("p2r3", seq, i)])
                        s_dve.bump(v.tensor_mul(sc["ss"][:, :], sc["r2"][:, :],
                                                sc["r3"][:, :]))
                        v.wait_ge(s_dve.sem, s_dve.n)
                        v.wait_ge(s_gsc[i % 2].sem, m.get(("scst_last", i % 2), 0))
                        v.tensor_copy(scpkd[i % 2][:, 2:3], mv2[:, 0:1])
                        v.tensor_copy(scpkd[i % 2][:, 3:4], sc["r2"][:, :])
                        v.tensor_copy(scpkd[i % 2][:, 4:5], mv2[:, 0:1])
                        v.tensor_copy(scpkd[i % 2][:, 5:6], sc["ss"][:, :])
                        s_dve.bump(v.memset(scpkd[i % 2][:, 6:8], 0.0))
                        m[("p2scpk1", seq, i)] = s_dve.n
                        v.wait_ge(s_dve.sem, s_dve.n)
                        if ("p2t3p", seq, i - 2) in m:
                            v.wait_ge(s_pe.sem, m[("p2t3p", seq, i - 2)])
                        s_dve.bump(v.tensor_scalar(t3d[i % 2][:, :], x1b[:, :],
                                                   mv2[:, 0:1], sc["ss"][:, :],
                                                   ALU.subtract, ALU.mult))
                        m[("p2t3", seq, i)] = s_dve.n

                    em_argmax(i - 1)

                em_t3p(NT_SEQ - 1)
                em_cnt(NT_SEQ - 2)
                em_argmax(NT_SEQ - 1)
                em_cnt(NT_SEQ - 1)


        with nc.Block(no_gpsimd_drain=True) as blk:
            @blk.vector
            def _(v):
                v.wait_ge(s_pe.sem, s_pe.n)
                v.wait_ge(s_act.sem, s_act.n)
                s_dve.bump(v.tensor_copy(cntsb[:, :], p_cnt[:, :]))

            @blk.gpsimd
            def _(gp):
                gp.wait_ge(s_dve.sem, s_dve.n)
                s_gq.bump(gp.dma_start(out=cnt_out[:, :], in_=cntsb[:, :]))
                gp.wait_ge(s_gq.sem, s_gq.n)
                gp.wait_ge(s_gx1.sem, s_gx1.n)
                gp.wait_ge(s_gsc[0].sem, s_gsc[0].n)
                gp.wait_ge(s_gsc[1].sem, s_gsc[1].n)

    return nc


def build_b():
    nc = bass.Bass("TRN2", target_bir_lowering=False)
    x1_in = nc.dram_tensor("x1", [TOK, D], F32, kind="ExternalInput")
    scal_in = nc.dram_tensor("scal", [TOK, 8], F32, kind="ExternalInput")
    cnt_in = nc.dram_tensor("cntall", [16, 8], F32, kind="ExternalInput")
    mask_in = nc.dram_tensor("maskcol", [16, 2], F32, kind="ExternalInput")
    wi_in = nc.dram_tensor("wi", [NE * D, I_FF], F32, kind="ExternalInput")
    wo_in = nc.dram_tensor("wo", [NE * I_FF, D], F32, kind="ExternalInput")
    gam_in = nc.dram_tensor("gam", [128, 8], F32, kind="ExternalInput")
    bet_in = nc.dram_tensor("bet", [128, 8], F32, kind="ExternalInput")
    ident_in = nc.dram_tensor("ident", [128, 128], F32, kind="ExternalInput")
    cst_in = nc.dram_tensor("cst", [128, 32], F32, kind="ExternalInput")
    ones_in = nc.dram_tensor("onesrow", [1, 128], F32, kind="ExternalInput")
    toku_in = nc.dram_tensor("toku", [128, NT], U32, kind="ExternalInput")
    tokf_in = nc.dram_tensor("tokf", [128, NT], F32, kind="ExternalInput")
    out_dram = nc.dram_tensor("out", [TOK, D], F32, kind="ExternalOutput")

    gidx = nc.dram_tensor("gidx_d", [GIDX_ROWS, 1], U32)
    eobuf = nc.dram_tensor("eobuf", [NSLOT, D], F32)

    ctx = contextlib.ExitStack()
    with ctx:
        gamv = _sb(ctx, nc, "gamv_sb", [128, 8])
        betv = _sb(ctx, nc, "betv_sb", [128, 8])
        ident = _sb(ctx, nc, "ident_sb", [128, 128])
        cst = _sb(ctx, nc, "cst_sb", [128, 32])
        onesr = _sb(ctx, nc, "onesr", [1, 128])
        toku = _sb(ctx, nc, "toku_sb", [128, NT], U32)
        tokf = _sb(ctx, nc, "tokf_sb", [128, NT])
        cntsb = _sb(ctx, nc, "cntsb", [16, 8])
        masksb = _sb(ctx, nc, "masksb", [16, 2])
        offs = _sb(ctx, nc, "offs", [8, 2])
        scal = _sb(ctx, nc, "scal_sb", [128, 8 * NT])
        rowt = _sb(ctx, nc, "rowt", [1, 128])
        keepc = _sb(ctx, nc, "keepc", [128, NT])
        slotc = _sb(ctx, nc, "slotc", [128, NT])
        posu = _sb(ctx, nc, "posu", [128, NT], U32)
        gposu = _sb(ctx, nc, "gposu", [128, NT], U32)
        wkf = _sb(ctx, nc, "wkf", [128, NT])
        zb = _sb(ctx, nc, "zb", [128, 64], U32)
        tp = {k: _sb(ctx, nc, f"tp_{k}", [128, 1]) for k in
              ["a", "b", "c", "k2", "pf", "gf"]}
        gxi = _sb(ctx, nc, "gxi", [128, 1], U32)
        dsc = _sb(ctx, nc, "dsc", [128, 8])

        s_ld = Ch(ctx.enter_context(nc.semaphore("s_ld")), 16)
        s_pe = Ch(ctx.enter_context(nc.semaphore("s_pe")))
        s_act = Ch(ctx.enter_context(nc.semaphore("s_act")))
        s_dve = Ch(ctx.enter_context(nc.semaphore("s_dve")))
        s_gs = Ch(ctx.enter_context(nc.semaphore("s_gs")), 16)
        s_gz = Ch(ctx.enter_context(nc.semaphore("s_gz")))
        s_gx = Ch(ctx.enter_context(nc.semaphore("s_gx")), 16)
        s_gd = Ch(ctx.enter_context(nc.semaphore("s_gd")), 16)
        s_geo = [Ch(ctx.enter_context(nc.semaphore(f"s_geo{j}")), 16) for j in range(2)]
        s_geg = [Ch(ctx.enter_context(nc.semaphore(f"s_geg{j}")), 16) for j in range(4)]
        s_w = [Ch(ctx.enter_context(nc.semaphore(f"s_w{j}")), 16) for j in range(2)]
        s_wp = Ch(ctx.enter_context(nc.semaphore("s_wp")), 16)
        s_st = [Ch(ctx.enter_context(nc.semaphore(f"s_st{j}")), 16) for j in range(4)]
        s_x1 = [Ch(ctx.enter_context(nc.semaphore(f"s_x1{j}")), 16) for j in range(4)]

        m = {}

        with nc.psum_tensor("p_off", [8, 2], F32) as p_off, nc.Block(no_gpsimd_drain=True) as blk:
            @blk.sync
            def _(sy):
                for dst, src in [(gamv, gam_in), (betv, bet_in), (ident, ident_in),
                                 (cst, cst_in), (onesr, ones_in), (toku, toku_in),
                                 (tokf, tokf_in), (cntsb, cnt_in), (masksb, mask_in)]:
                    s_ld.bump(sy.dma_start(out=dst[:, :], in_=src[:, :]))
                for t in range(NT):
                    s_ld.bump(sy.dma_start(out=scal[:, 8 * t:8 * (t + 1)],
                                           in_=scal_in[128 * t:128 * (t + 1), :]))

            @blk.gpsimd
            def _(gp):
                gp.memset(zb[:, :], 0)
                s_gz.bump(gp.engine_nop())
                gp.wait_ge(s_gz.sem, s_gz.n)
                s_gs.bump(gp.dma_start(
                    out=gidx[:, :].rearrange("(a b) c -> a (b c)", a=128),
                    in_=zb[:, :]))
                m["gzero"] = s_gs.n

            @blk.tensor
            def _(t):
                t.wait_ge(s_ld.sem, s_ld.n)
                s_pe.bump(t.matmul(p_off[:, :], cntsb[:, :], masksb[:, :],
                                   start=True, stop=True))

            @blk.vector
            def _(v):
                v.wait_ge(s_pe.sem, s_pe.n)
                s_dve.bump(v.tensor_copy(offs[:, :], p_off[:, :]))
                m["offs"] = s_dve.n

            @blk.scalar
            def _(s):
                s.preload_act_table(AF.Gelu)

        with contextlib.ExitStack() as ps:
            ohall = _sb(ps, nc, "ohall", [8, TOK])
            scn = _sb(ps, nc, "scn", [8, TOK])
            rke = _sb(ps, nc, "rke", [8, TOK])
            k8 = _sb(ps, nc, "k8", [8, TOK])
            tmp8 = scn
            sr8 = ohall
            p_r = _ps(ps, nc, "p_r", [1, 128])
            p_b = _ps(ps, nc, "p_b", [8, 128])
            p_k = _ps(ps, nc, "p_k", [128, 2])
            blk = ps.enter_context(nc.Block(no_gpsimd_drain=True))

            for i in range(NT):
                @blk.tensor
                def _(t, i=i):
                    if i == 0:
                        t.wait_ge(s_dve.sem, m["offs"])
                    else:
                        t.wait_ge(s_dve.sem, m[("row", i - 1)])
                    s_pe.bump(t.transpose(p_r[:, :], scal[:, 8 * i:8 * i + 1],
                                          ident[:, :]))
                    m[("rt", i)] = s_pe.n

                @blk.vector
                def _(v, i=i):
                    v.wait_ge(s_pe.sem, m[("rt", i)])
                    if ("bc", i - 1) in m:
                        v.wait_ge(s_pe.sem, m[("bc", i - 1)])
                    s_dve.bump(v.tensor_copy(rowt[:, :], p_r[:, :]))
                    m[("row", i)] = s_dve.n

                @blk.tensor
                def _(t, i=i):
                    t.wait_ge(s_dve.sem, m[("row", i)])
                    if ("ohc", i - 1) in m:
                        t.wait_ge(s_dve.sem, m[("ohc", i - 1)])
                    s_pe.bump(t.matmul(p_b[:, :], onesr[:, 0:8], rowt[:, :],
                                       start=True, stop=True))
                    m[("bc", i)] = s_pe.n

                @blk.vector
                def _(v, i=i):
                    v.wait_ge(s_pe.sem, m[("bc", i)])
                    s_dve.bump(v.tensor_scalar(ohall[:, 128 * i:128 * (i + 1)],
                                               p_b[:, :], cst[0:8, 1:2], None,
                                               ALU.is_equal))
                    m[("ohc", i)] = s_dve.n

            @blk.vector
            def _(v):
                v.wait_ge(s_dve.sem, m[("ohc", NT - 1)])
                s_dve.bump(v.tensor_tensor_scan(scn[:, :], ohall[:, :], ohall[:, :],
                                                0.0, ALU.add, ALU.bypass))
                v.wait_ge(s_dve.sem, s_dve.n)
                s_dve.bump(v.tensor_sub(rke[:, :], scn[:, :], ohall[:, :]))
                v.wait_ge(s_dve.sem, s_dve.n)
                s_dve.bump(v.tensor_scalar(tmp8[:, 0:SEQLEN], rke[:, 0:SEQLEN],
                                           offs[:, 0:1], CAP, ALU.add, ALU.is_lt))
                s_dve.bump(v.tensor_scalar(tmp8[:, SEQLEN:TOK], rke[:, SEQLEN:TOK],
                                           offs[:, 1:2], CAP, ALU.add, ALU.is_lt))
                v.wait_ge(s_dve.sem, s_dve.n)
                s_dve.bump(v.tensor_mul(k8[:, :], tmp8[:, :], ohall[:, :]))
                v.wait_ge(s_dve.sem, s_dve.n)
                s_dve.bump(v.tensor_mul(sr8[:, :], rke[:, :], k8[:, :]))
                m["scan"] = s_dve.n

            for i in range(NT):
                @blk.tensor
                def _(t, i=i):
                    t.wait_ge(s_dve.sem, m["scan"])
                    if ("kse", i - 1) in m:
                        t.wait_ge(s_dve.sem, m[("kse", i - 1)])
                    t.matmul(p_k[:, 0:1], k8[:, 128 * i:128 * (i + 1)],
                             cst[0:8, 0:1], start=True, stop=True)
                    s_pe.bump(t.matmul(p_k[:, 1:2], sr8[:, 128 * i:128 * (i + 1)],
                                       cst[0:8, 0:1], start=True, stop=True,
                                       skip_group_check=True))
                    m[("ks", i)] = s_pe.n

                @blk.vector
                def _(v, i=i):
                    v.wait_ge(s_pe.sem, m[("ks", i)])
                    v.tensor_copy(keepc[:, i:i + 1], p_k[:, 0:1])
                    s_dve.bump(v.tensor_copy(slotc[:, i:i + 1], p_k[:, 1:2]))
                    m[("kse", i)] = s_dve.n
                    v.wait_ge(s_dve.sem, s_dve.n)
                    v.scalar_tensor_tensor(tp["a"][:, :], scal[:, 8 * i:8 * i + 1],
                                           float(C_PAD), slotc[:, i:i + 1],
                                           ALU.mult, ALU.add)
                    s_dve.bump(v.tensor_scalar(tp["b"][:, :], slotc[:, i:i + 1],
                                               float(C_PAD), None, ALU.is_lt))
                    v.wait_ge(s_dve.sem, s_dve.n)
                    s_dve.bump(v.tensor_mul(tp["k2"][:, :], keepc[:, i:i + 1],
                                            tp["b"][:, :]))
                    v.wait_ge(s_dve.sem, s_dve.n)
                    v.tensor_scalar(tp["c"][:, :], tokf[:, i:i + 1], 1.0, TRASH0,
                                    ALU.mult, ALU.add)
                    s_dve.bump(v.tensor_mul(tp["gf"][:, :], tp["a"][:, :], tp["k2"][:, :]))
                    v.wait_ge(s_dve.sem, s_dve.n)
                    s_dve.bump(v.tensor_mul(wkf[:, i:i + 1],
                                            scal[:, 8 * i + 1:8 * i + 2], tp["k2"][:, :]))
                    v.wait_ge(s_dve.sem, s_dve.n)
                    v.tensor_mul(tp["pf"][:, :], tp["c"][:, :], tp["k2"][:, :])
                    s_dve.bump(v.engine_nop())
                    v.wait_ge(s_dve.sem, s_dve.n)
                    s_dve.bump(v.tensor_sub(tp["b"][:, :], tp["c"][:, :], tp["pf"][:, :]))
                    v.wait_ge(s_dve.sem, s_dve.n)
                    s_dve.bump(v.tensor_add(tp["pf"][:, :], tp["gf"][:, :], tp["b"][:, :]))
                    v.wait_ge(s_dve.sem, s_dve.n)
                    v.tensor_copy(posu[:, i:i + 1], tp["pf"][:, :])
                    s_dve.bump(v.tensor_copy(gposu[:, i:i + 1], tp["gf"][:, :]))
                    m[("pos", i)] = s_dve.n

            @blk.gpsimd
            def _(gp):
                gp.wait_ge(s_gs.sem, m["gzero"])
                for i in range(NT):
                    gp.wait_ge(s_dve.sem, m[("pos", i)])
                    s_gs.bump(gp.indirect_dma_start(
                        out=gidx[:, :],
                        out_offset=bass.IndirectOffsetOnAxis(ap=posu[:, i:i + 1], axis=0),
                        in_=toku[:, i:i + 1], in_offset=None))
                m["scat"] = s_gs.n

        with contextlib.ExitStack() as fstack:
            drow = _sb(fstack, nc, "drow", [128, D])
            dt3 = _sb(fstack, nc, "dt3", [128, D])
            dspt = [_sb(fstack, nc, f"dspt{i}", [128, ST_E * D]) for i in range(2)]
            wmov = [_sb(fstack, nc, f"wmov{i}", [128, 8 * 512]) for i in range(2)]
            wipre = _sb(fstack, nc, "wipre", [128, 8 * 512])
            hsb = [_sb(fstack, nc, f"hsb{i}", [128, I_FF]) for i in range(ST_E)]
            eosb = [_sb(fstack, nc, f"eosb{i}", [128, D]) for i in range(2)]
            p_h = [_ps(fstack, nc, f"p_h{j}", [128, 512]) for j in range(2)]
            p_t2 = _ps(fstack, nc, "p_t2", [128, 1024])
            p_eo = [_ps(fstack, nc, f"p_eo{j}", [128, 512]) for j in range(4)]

            def em_dgather(blk, de, j):
                st = ST_E * de + j

                @blk.gpsimd
                def _(gp, j=j, st=st, de=de):
                    if de == 0 and j == 0:
                        gp.wait_ge(s_gs.sem, m["scat"])
                    prev = (de, j - 1) if j > 0 else (de - 1, ST_E - 1)
                    if ("dt3",) + prev in m:
                        gp.wait_ge(s_dve.sem, m[("dt3",) + prev])
                    s_gx.bump(gp.dma_start(out=gxi[:, :],
                                           in_=gidx[128 * st:128 * (st + 1), :]))
                    gp.wait_ge(s_gx.sem, s_gx.n)
                    s_gd.bump(gp.indirect_dma_start(
                        out=drow[:, :], out_offset=None, in_=x1_in[:, :],
                        in_offset=bass.IndirectOffsetOnAxis(ap=gxi[:, :], axis=0)))
                    s_gd.bump(gp.indirect_dma_start(
                        out=dsc[:, :], out_offset=None, in_=scal_in[:, :],
                        in_offset=bass.IndirectOffsetOnAxis(ap=gxi[:, :], axis=0)))
                    m[("dg", de, j)] = s_gd.n

                @blk.vector
                def _(v, j=j, de=de):
                    v.wait_ge(s_gd.sem, m[("dg", de, j)])
                    s_dve.bump(v.tensor_scalar_sub(dt3[:, :], drow[:, :],
                                                   dsc[:, 4:5]))
                    v.wait_ge(s_dve.sem, s_dve.n)
                    s_dve.bump(v.tensor_scalar_mul(dt3[:, :], dt3[:, :],
                                                   dsc[:, 5:6]))
                    m[("dt3", de, j)] = s_dve.n

            def em_dtp(blk, de, j, pc, host_e):
                @blk.tensor
                def _(t, j=j, pc=pc, de=de, host_e=host_e):
                    if pc == 0:
                        t.wait_ge(s_dve.sem, m[("dt3", de, j)])
                    if host_e is not None and j == 0 and pc == 0:
                        t.wait_ge(s_act.sem,
                                  m[("hge", host_e, 7, ST_E - 1)])
                    prev = (de, j - 1, pc) if j > 0 else (de - 1, ST_E - 1, pc)
                    if ("dte",) + prev in m:
                        t.wait_ge(s_dve.sem, m[("dte",) + prev])
                    for k in range(4):
                        fc = 4 * pc + k
                        ins = t.transpose(
                            p_h[pc][:, 128 * k:128 * (k + 1)],
                            dt3[:, 128 * fc:128 * (fc + 1)],
                            ident[:, :])
                    s_pe.bump(ins)
                    m[("dtp", de, j, pc)] = s_pe.n

                @blk.vector
                def _(v, j=j, pc=pc, de=de):
                    v.wait_ge(s_pe.sem, m[("dtp", de, j, pc)])
                    for k in range(4):
                        fc = 4 * pc + k
                        ins = v.tensor_scalar(
                            dspt[de % 2][:, j * D + 128 * fc: j * D + 128 * (fc + 1)],
                            p_h[pc][:, 128 * k:128 * (k + 1)],
                            gamv[:, fc:fc + 1], betv[:, fc:fc + 1],
                            ALU.mult, ALU.add)
                    s_dve.bump(ins)
                    m[("dte", de, j, pc)] = s_dve.n

            def em_dispatch(blk, de, host_e):
                for j in range(ST_E):
                    em_dgather(blk, de, j)
                    for pc in range(2):
                        em_dtp(blk, de, j, pc, host_e)

            for e in range(NE):
                    with nc.Block(no_gpsimd_drain=True) as blk:
                        if e == 0:
                            em_dispatch(blk, 0, None)

                        for ic in range(8):
                            if not (ic == 0 and e > 0):
                                @blk.sync
                                def _(sy, e=e, ic=ic):
                                    if ("wim", e, ic - 2) in m:
                                        sy.wait_ge(s_pe.sem, m[("wim", e, ic - 2)])
                                    if e > 0 and ic == 1:
                                        sy.wait_ge(s_pe.sem, m[("wom", e - 1, 1, 31)])
                                    if e > 0 and ic == 0:
                                        sy.wait_ge(s_pe.sem, m[("wom", e - 1, 1, 30)])
                                    for fc in range(8):
                                        s_w[ic % 2].bump(sy.dma_start(
                                            out=wmov[ic % 2][:, 512 * fc:512 * (fc + 1)],
                                            in_=wi_in[D * e + 128 * fc: D * e + 128 * (fc + 1),
                                                      512 * ic:512 * (ic + 1)]))
                                    m[("wil", e, ic)] = s_w[ic % 2].n

                            for j in range(ST_E):
                                @blk.tensor
                                def _(t, e=e, ic=ic, j=j):
                                    pre = (ic == 0 and e > 0)
                                    if pre:
                                        t.wait_ge(s_wp.sem, m[("wipre", e)])
                                    else:
                                        t.wait_ge(s_w[ic % 2].sem, m[("wil", e, ic)])
                                    if ic == 0 and j == 0:
                                        t.wait_ge(s_dve.sem, m[("dte", e, ST_E - 1, 1)])
                                    pj = (ic * ST_E + j) % 2
                                    prev = ic * ST_E + j - 2
                                    if prev >= 0:
                                        pic, pjj = divmod(prev, ST_E)
                                        t.wait_ge(s_act.sem, m[("hge", e, pic, pjj)])
                                    src = wipre if pre else wmov[ic % 2]
                                    for fc in range(8):
                                        ins = t.matmul(
                                            p_h[pj][:, :],
                                            dspt[e % 2][:, j * D + 128 * fc: j * D + 128 * (fc + 1)],
                                            src[:, 512 * fc:512 * (fc + 1)],
                                            start=(fc == 0), stop=(fc == 7))
                                    s_pe.bump(ins)
                                    m[("wim", e, ic, j)] = s_pe.n
                                    if j == ST_E - 1:
                                        m[("wim", e, ic)] = s_pe.n

                                @blk.scalar
                                def _(s, e=e, ic=ic, j=j):
                                    s.wait_ge(s_pe.sem, m[("wim", e, ic, j)])
                                    pj = (ic * ST_E + j) % 2
                                    s_act.bump(s.activation(
                                        hsb[j][:, 512 * ic:512 * (ic + 1)],
                                        p_h[pj][:, :], AF.Gelu, bias=0.0, scale=1.0))
                                    m[("hge", e, ic, j)] = s_act.n

                        if e + 1 < NE:
                            em_dgather(blk, e + 1, 0)
                        for j in range(ST_E):
                            for gg in range(8):
                                if gg == 4 and e + 1 < NE and j < ST_E:
                                    for pc in range(2):
                                        em_dtp(blk, e + 1, j, pc, e)
                                    if j + 1 < ST_E:
                                        em_dgather(blk, e + 1, j + 1)

                                @blk.tensor
                                def _(t, e=e, j=j, gg=gg):
                                    if gg == 0:
                                        t.wait_ge(s_act.sem, m[("hge", e, 7, j)])
                                    cc = j * 8 + gg
                                    if cc >= 2:
                                        pcc = cc - 2
                                        t.wait_ge(s_dve.sem, m[("hte", e, pcc // 8, pcc % 8)])
                                    bank = 512 * (gg % 2)
                                    for k in range(4):
                                        c = 4 * gg + k
                                        ins = t.transpose(
                                            p_t2[:, bank + 128 * k:bank + 128 * (k + 1)],
                                            hsb[j][:, 128 * c:128 * (c + 1)], ident[:, :])
                                    s_pe.bump(ins)
                                    m[("htp", e, j, gg)] = s_pe.n

                                @blk.vector
                                def _(v, e=e, j=j, gg=gg):
                                    v.wait_ge(s_pe.sem, m[("htp", e, j, gg)])
                                    bank = 512 * (gg % 2)
                                    s_dve.bump(v.tensor_copy(
                                        hsb[j][:, 512 * gg:512 * (gg + 1)],
                                        p_t2[:, bank:bank + 512]))
                                    m[("hte", e, j, gg)] = s_dve.n

                        for hc in range(2):
                            for ic in range(32):
                                @blk.sync
                                def _(sy, e=e, hc=hc, ic=ic):
                                    k = hc * 32 + ic
                                    if k >= 2:
                                        pk = k - 2
                                        t_pe = m[("wom", e, pk // 32, pk % 32)]
                                        sy.wait_ge(s_pe.sem, t_pe)
                                    if k < 2:
                                        sy.wait_ge(s_pe.sem, m[("wim", e, 6 + k)])
                                    s_w[ic % 2].bump(sy.dma_start(
                                        out=wmov[ic % 2][:, 0:512],
                                        in_=wo_in[I_FF * e + 128 * ic: I_FF * e + 128 * (ic + 1),
                                                  512 * hc:512 * (hc + 1)]))
                                    m[("wol", e, hc, ic)] = s_w[ic % 2].n

                                if e + 1 < NE and hc == 1 and ic < 8:
                                    @blk.sync
                                    def _(sy, e=e, fc=ic):
                                        if fc == 0:
                                            sy.wait_ge(s_pe.sem, m[("wim", e, 0)])
                                        s_wp.bump(sy.dma_start(
                                            out=wipre[:, 512 * fc:512 * (fc + 1)],
                                            in_=wi_in[D * (e + 1) + 128 * fc:
                                                      D * (e + 1) + 128 * (fc + 1),
                                                      0:512]))
                                        if fc == 7:
                                            m[("wipre", e + 1)] = s_wp.n

                                @blk.tensor
                                def _(t, e=e, hc=hc, ic=ic):
                                    t.wait_ge(s_w[ic % 2].sem, m[("wol", e, hc, ic)])
                                    if ic == 0:
                                        t.wait_ge(s_dve.sem, m[("hte", e, ST_E - 1, 7)])
                                        if hc == 1:
                                            for j in range(ST_E):
                                                t.wait_ge(s_dve.sem, m[("eoe", e, 0, j)])
                                    for j in range(ST_E):
                                        ins = t.matmul(
                                            p_eo[j][:, :],
                                            hsb[j][:, 128 * ic:128 * (ic + 1)],
                                            wmov[ic % 2][:, 0:512],
                                            start=(ic == 0), stop=(ic == 31))
                                    s_pe.bump(ins)
                                    m[("wom", e, hc, ic)] = s_pe.n

                            for j in range(ST_E):
                                @blk.vector
                                def _(v, e=e, hc=hc, j=j):
                                    v.wait_ge(s_pe.sem, m[("wom", e, hc, 31)])
                                    v.wait_ge(s_geo[j % 2].sem, m.get(("eost_last", j % 2), 0))
                                    s_dve.bump(v.tensor_copy(
                                        eosb[j % 2][:, 512 * hc:512 * (hc + 1)],
                                        p_eo[j][:, :]))
                                    m[("eoe", e, hc, j)] = s_dve.n

                                @blk.gpsimd
                                def _(gp, e=e, hc=hc, j=j):
                                    gp.wait_ge(s_dve.sem, m[("eoe", e, hc, j)])
                                    st = ST_E * e + j
                                    s_geo[j % 2].bump(gp.dma_start(
                                        out=eobuf[128 * st:128 * (st + 1),
                                                  512 * hc:512 * (hc + 1)],
                                        in_=eosb[j % 2][:, 512 * hc:512 * (hc + 1)]))
                                    m[("eost", e, hc, j)] = s_geo[j % 2].n
                                    m[("eost_last", j % 2)] = s_geo[j % 2].n

        with contextlib.ExitStack() as fin:
            x1t = [_sb(fin, nc, f"x1t{i}", [128, D]) for i in range(4)]
            eog = [_sb(fin, nc, f"eog{i}", [128, D]) for i in range(4)]
            xn2f = _sb(fin, nc, "xn2f", [128, D])
            res2 = _sb(fin, nc, "res2", [128, D])
            outf = [_sb(fin, nc, f"outf{i}", [128, D]) for i in range(4)]
            with nc.Block(no_gpsimd_drain=True) as blk:
                for i in range(NT):
                    @blk.gpsimd
                    def _(gp, i=i):
                        if i == 0:
                            for j in range(2):
                                gp.wait_ge(s_geo[j].sem, s_geo[j].n)
                        if ("comb", i - 4) in m:
                            gp.wait_ge(s_dve.sem, m[("comb", i - 4)])
                        s_geg[i % 4].bump(gp.indirect_dma_start(
                            out=eog[i % 4][:, :], out_offset=None, in_=eobuf[:, :],
                            in_offset=bass.IndirectOffsetOnAxis(ap=gposu[:, i:i + 1], axis=0)))
                        m[("eogth", i)] = s_geg[i % 4].n

                    @blk.scalar
                    def _(s, i=i):
                        if ("comb", i - 4) in m:
                            s.wait_ge(s_dve.sem, m[("comb", i - 4)])
                        s_x1[i % 4].bump(s.dma_start(out=x1t[i % 4][:, :],
                                                     in_=x1_in[128 * i:128 * (i + 1), :]))
                        m[("x1l", i)] = s_x1[i % 4].n

                    @blk.vector
                    def _(v, i=i):
                        v.wait_ge(s_geg[i % 4].sem, m[("eogth", i)])
                        v.wait_ge(s_x1[i % 4].sem, m[("x1l", i)])
                        if ("comb", i - 1) in m:
                            v.wait_ge(s_dve.sem, m[("comb", i - 1)])
                        if ("outst", i - 4) in m:
                            v.wait_ge(s_st[i % 4].sem, m[("outst", i - 4)])
                        s_dve.bump(v.tensor_scalar(xn2f[:, :], x1t[i % 4][:, :],
                                                   scal[:, 8 * i + 2:8 * i + 3],
                                                   scal[:, 8 * i + 3:8 * i + 4],
                                                   ALU.subtract, ALU.mult))
                        v.wait_ge(s_dve.sem, s_dve.n)
                        s_dve.bump(v.scalar_tensor_tensor(res2[:, :], eog[i % 4][:, :],
                                                          wkf[:, i:i + 1], x1t[i % 4][:, :],
                                                          ALU.mult, ALU.add))
                        v.wait_ge(s_dve.sem, s_dve.n)
                        s_dve.bump(v.tensor_add(outf[i % 4][:, :], res2[:, :],
                                                xn2f[:, :]))
                        m[("comb", i)] = s_dve.n

                    @blk.sync
                    def _(sy, i=i):
                        sy.wait_ge(s_dve.sem, m[("comb", i)])
                        s_st[i % 4].bump(sy.dma_start(out=out_dram[128 * i:128 * (i + 1), :],
                                                      in_=outf[i % 4][:, :]))
                        m[("outst", i)] = s_st[i % 4].n

                @blk.gpsimd
                def _(gp):
                    for j in range(4):
                        gp.wait_ge(s_st[j].sem, s_st[j].n)

    return nc


def _consts():
    ident = np.eye(128, dtype=np.float32)
    cst = np.zeros((128, 32), np.float32)
    cst[:, 0] = 1.0
    cst[:, 1] = np.arange(128, dtype=np.float32)
    cst[:, 18] = EPS_LN
    for j in range(8):
        cst[:, 2 + j] = 8.0 - j
        cst[:, 10 + j] = float(j)
    onesrow = np.ones((1, 128), np.float32)
    tok = (np.arange(NT, dtype=np.uint32)[None, :] * 128
           + np.arange(128, dtype=np.uint32)[:, None])
    return ident, cst, onesrow, tok.astype(np.uint32), tok.astype(np.float32)


def _prep_a_inputs(x_full, Wqkv, Wout, bout, gate, gamma, beta):
    ident, cst, onesrow, _, _ = _consts()
    gateT = np.ascontiguousarray(np.transpose(gate))
    gt = np.ascontiguousarray(
        gateT.reshape(8, 128, 8).transpose(1, 0, 2).reshape(128, 64))
    gam = np.ascontiguousarray(gamma.reshape(8, 128).T)
    bet = np.ascontiguousarray(beta.reshape(8, 128).T)
    maps = []
    for c in range(NCORE):
        xs = np.ascontiguousarray(
            np.concatenate([x_full[c], x_full[c + 8]], axis=0))
        maps.append({
            "x": xs, "wqkv": Wqkv, "wout": Wout,
            "bout": np.ascontiguousarray(bout.reshape(1, D)),
            "gt": gt, "gam": gam, "bet": bet,
            "ident": ident, "cst": cst, "onesrow": onesrow,
        })
    return maps


def kernel(L0_attn_x, L0_attn_Wqkv, L0_attn_Wout, L0_attn_bout,
           L0_ffn_gate_weight, L0_ffn_ln_gamma, L0_ffn_ln_beta,
           L0_ffn_wi, L0_ffn_wo, L0_ffn_conf_w1, L0_ffn_conf_b1,
           L0_ffn_conf_w2, L0_ffn_conf_b2, L0_ffn_comp_w, L0_ffn_comp_b):
    x = np.ascontiguousarray(np.asarray(L0_attn_x, dtype=np.float32))
    Wqkv = np.ascontiguousarray(np.asarray(L0_attn_Wqkv, dtype=np.float32))
    Wout = np.ascontiguousarray(np.asarray(L0_attn_Wout, dtype=np.float32))
    bout = np.asarray(L0_attn_bout, dtype=np.float32)
    gate = np.asarray(L0_ffn_gate_weight, dtype=np.float32)
    gamma = np.asarray(L0_ffn_ln_gamma, dtype=np.float32)
    beta = np.asarray(L0_ffn_ln_beta, dtype=np.float32)
    wi = np.ascontiguousarray(np.asarray(L0_ffn_wi, dtype=np.float32)
                              .reshape(NE * D, I_FF))
    wo = np.ascontiguousarray(np.asarray(L0_ffn_wo, dtype=np.float32)
                              .reshape(NE * I_FF, D))

    ident, cst, onesrow, toku, tokf = _consts()

    nc_a = build_a()
    maps_a = _prep_a_inputs(x, Wqkv, Wout, bout, gate, gamma, beta)
    res_a = run_bass_kernel_spmd(nc_a, maps_a, core_ids=list(range(NCORE)))

    cnt_all = np.zeros((16, 8), np.float32)
    for c in range(NCORE):
        cnt_all[c] = res_a.results[c]["cnto"][:, 0]
        cnt_all[c + 8] = res_a.results[c]["cnto"][:, 1]
    cnt_all = np.ascontiguousarray(cnt_all)

    gam = np.ascontiguousarray(gamma.reshape(8, 128).T)
    bet = np.ascontiguousarray(beta.reshape(8, 128).T)
    nc_b = build_b()
    maps_b = []
    for c in range(NCORE):
        b = np.arange(16)
        mask1 = (b < c).astype(np.float32)
        mask2 = ((b < 8 + c) & (b != c)).astype(np.float32)
        maskcol = np.ascontiguousarray(np.stack([mask1, mask2], axis=1))
        maps_b.append({
            "x1": np.ascontiguousarray(res_a.results[c]["x1o"]),
            "scal": np.ascontiguousarray(res_a.results[c]["scalo"]),
            "cntall": cnt_all, "maskcol": maskcol,
            "wi": wi, "wo": wo, "gam": gam, "bet": bet,
            "ident": ident, "cst": cst, "onesrow": onesrow,
            "toku": toku, "tokf": tokf,
        })
    res_b = run_bass_kernel_spmd(nc_b, maps_b, core_ids=list(range(NCORE)))

    out = np.empty((16, SEQLEN, D), np.float32)
    for c in range(NCORE):
        o = res_b.results[c]["out"]
        out[c] = o[:SEQLEN]
        out[c + 8] = o[SEQLEN:]
    return out


# version 09
# speedup vs baseline: 1.0971x; 1.0685x over previous
import contextlib
import numpy as np
import concourse.bass as bass
import concourse.mybir as mybir
from concourse.bass_utils import run_bass_kernel_spmd

F32 = mybir.dt.float32
U32 = mybir.dt.uint32
AF = mybir.ActivationFunctionType
ALU = mybir.AluOpType
AX = mybir.AxisListType

D = 1024
NCORE = 8
SEQLEN = 2048
NSEQ = 2
TOK = NSEQ * SEQLEN
NT_SEQ = SEQLEN // 128
NT = TOK // 128
HEADS = 16
NE = 8
CAP = 2048.0
C_PAD = 512
NSLOT = NE * C_PAD
ST_E = C_PAD // 128
TRASH0 = float(NSLOT)
GIDX_ROWS = NSLOT + TOK
I_FF = 4096
EPS_A = 1e-6
EPS_LN = 1e-5
INV_D = 1.0 / D


class Ch:
    def __init__(self, sem, step=1):
        self.sem, self.step, self.n = sem, step, 0

    def bump(self, instr):
        instr.then_inc(self.sem, self.step)
        self.n += self.step
        return self.n


def _sb(ctx, nc, name, shape, dt=F32):
    return ctx.enter_context(nc.sbuf_tensor(name, shape, dt))


def _ps(ctx, nc, name, shape):
    return ctx.enter_context(nc.psum_tensor(name, shape, F32))


def build_a(n_seq=NSEQ, with_p1=True, with_p2=True, p2s=9):
    nc = bass.Bass("TRN2", target_bir_lowering=False)
    x_in = nc.dram_tensor("x", [TOK, D], F32, kind="ExternalInput")
    wqkv_in = nc.dram_tensor("wqkv", [D, 3 * D], F32, kind="ExternalInput")
    wout_in = nc.dram_tensor("wout", [D, D], F32, kind="ExternalInput")
    bout_in = nc.dram_tensor("bout", [1, D], F32, kind="ExternalInput")
    gt_in = nc.dram_tensor("gt", [128, 64], F32, kind="ExternalInput")
    gam_in = nc.dram_tensor("gam", [128, 8], F32, kind="ExternalInput")
    bet_in = nc.dram_tensor("bet", [128, 8], F32, kind="ExternalInput")
    ident_in = nc.dram_tensor("ident", [128, 128], F32, kind="ExternalInput")
    cst_in = nc.dram_tensor("cst", [128, 32], F32, kind="ExternalInput")
    ones_in = nc.dram_tensor("onesrow", [1, 128], F32, kind="ExternalInput")

    x1_out = nc.dram_tensor("x1o", [TOK, D], F32, kind="ExternalOutput")
    scal_out = nc.dram_tensor("scalo", [TOK, 8], F32, kind="ExternalOutput")
    cnt_out = nc.dram_tensor("cnto", [8, 2], F32, kind="ExternalOutput")
    qscr = nc.dram_tensor("qscr", [TOK, D], F32)

    ctx = contextlib.ExitStack()
    with ctx:
        wq = _sb(ctx, nc, "wq", [128, 8 * D])
        wkv = _sb(ctx, nc, "wkv", [128, 8 * 2048])
        wout = _sb(ctx, nc, "wout_sb", [128, 8 * D])
        boutr = _sb(ctx, nc, "boutr", [1, D])
        gt = _sb(ctx, nc, "gt_sb", [128, 64])
        gamv = _sb(ctx, nc, "gamv_sb", [128, 8])
        betv = _sb(ctx, nc, "betv_sb", [128, 8])
        ident = _sb(ctx, nc, "ident_sb", [128, 128])
        cst = _sb(ctx, nc, "cst_sb", [128, 32])
        onesr = _sb(ctx, nc, "onesr", [1, 128])
        ggam = _sb(ctx, nc, "ggam", [128, 64])
        bgrow = _sb(ctx, nc, "bgrow", [1, 8])

        xt = [_sb(ctx, nc, f"xt{i}", [128, D]) for i in range(2)]
        xn1 = _sb(ctx, nc, "xn1", [128, D])
        xn1t = _sb(ctx, nc, "xn1t", [128, D])
        qsb = _sb(ctx, nc, "qsb", [128, D])
        ktile = _sb(ctx, nc, "ktile", [128, D])
        vtile = _sb(ctx, nc, "vtile", [128, D])
        kv2 = _sb(ctx, nc, "kv2", [128, D])
        ksum2 = _sb(ctx, nc, "ksum2", [128, 16])
        ksblk = _sb(ctx, nc, "ksblk", [128, 128])
        qt = [_sb(ctx, nc, f"qt{i}", [128, D]) for i in range(2)]
        zt = _sb(ctx, nc, "zt", [128, 16])
        z2 = _sb(ctx, nc, "z2", [128, 16])
        oz = _sb(ctx, nc, "oz", [128, D])
        ozt = _sb(ctx, nc, "ozt", [128, D])
        x1b = _sb(ctx, nc, "x1b", [128, D])
        t3d = [_sb(ctx, nc, f"t3d{i}", [128, D]) for i in range(2)]
        t3td = [_sb(ctx, nc, f"t3td{i}", [128, D]) for i in range(2)]
        lgt = _sb(ctx, nc, "lgt", [128, 8])
        eqt = _sb(ctx, nc, "eqt", [128, 8])
        ext = _sb(ctx, nc, "ext", [128, 8])
        cndt = _sb(ctx, nc, "cndt", [128, 8])
        oht = _sb(ctx, nc, "oht", [128, 8])
        scpkd = [_sb(ctx, nc, f"scpk{i}", [128, 8]) for i in range(2)]
        cntsb = _sb(ctx, nc, "cntsb", [8, 2])
        SC = ["s1c", "s1q", "m1", "msq", "ex2", "var", "sq1", "r1",
              "s2c", "s2q", "m2", "m2sq", "ex22", "var2", "sq2", "r2",
              "s3c", "s3q", "m3", "m3sq", "ex23", "var3", "sq3", "r3",
              "cc", "ss", "mx", "nmx", "idx", "wgt", "smx", "t0", "t1", "t1s"]
        sc = {k: _sb(ctx, nc, f"sc_{k}", [128, 1]) for k in SC}
        bnst = _sb(ctx, nc, "bnst", [128, 12])
        mv1 = _sb(ctx, nc, "mv1", [128, 2])
        mv2 = _sb(ctx, nc, "mv2", [128, 2])
        dumm = _sb(ctx, nc, "dumm", [128, 2])

        p_cnt = _ps(ctx, nc, "p_cnt", [8, 2])

        s_ld = Ch(ctx.enter_context(nc.semaphore("s_ld")), 16)
        s_ldg = Ch(ctx.enter_context(nc.semaphore("s_ldg")), 16)
        s_x = [Ch(ctx.enter_context(nc.semaphore(f"s_x{j}")), 16) for j in range(2)]
        s_pe = Ch(ctx.enter_context(nc.semaphore("s_pe")))
        s_act = Ch(ctx.enter_context(nc.semaphore("s_act")))
        s_dve = Ch(ctx.enter_context(nc.semaphore("s_dve")))
        s_gq = Ch(ctx.enter_context(nc.semaphore("s_gq")), 16)
        s_gx1 = Ch(ctx.enter_context(nc.semaphore("s_gx1")), 16)
        s_gsc = [Ch(ctx.enter_context(nc.semaphore(f"s_gsc{j}")), 16) for j in range(2)]
        s_q2 = [Ch(ctx.enter_context(nc.semaphore(f"s_q2{j}")), 16) for j in range(2)]

        with nc.psum_tensor("p_bg", [1, 8], F32) as p_bg, nc.Block(no_gpsimd_drain=True) as blk:
            @blk.sync
            def _(sy):
                for fc in range(8):
                    s_ld.bump(sy.dma_start(out=wq[:, fc * D:(fc + 1) * D],
                                           in_=wqkv_in[128 * fc:128 * (fc + 1), 0:D]))
                for dst, src in [(boutr, bout_in), (gt, gt_in), (gamv, gam_in),
                                 (betv, bet_in), (ident, ident_in), (cst, cst_in),
                                 (onesr, ones_in)]:
                    s_ld.bump(sy.dma_start(out=dst[:, :], in_=src[:, :]))

            @blk.gpsimd
            def _(gp):
                for fc in range(8):
                    s_ldg.bump(gp.dma_start(out=wkv[:, fc * 2048:(fc + 1) * 2048],
                                            in_=wqkv_in[128 * fc:128 * (fc + 1), D:3 * D]))

            @blk.scalar
            def _(s):
                for fc in range(8):
                    s_ld.bump(s.dma_start(out=wout[:, fc * D:(fc + 1) * D],
                                          in_=wout_in[128 * fc:128 * (fc + 1), :]))

            @blk.vector
            def _(v):
                v.wait_ge(s_ld.sem, s_ld.n)
                v.wait_ge(s_ldg.sem, s_ldg.n)
                for fc in range(8):
                    s_dve.bump(v.tensor_scalar_mul(ggam[:, 8 * fc:8 * fc + 8],
                                                   gt[:, 8 * fc:8 * fc + 8],
                                                   gamv[:, fc:fc + 1]))

            @blk.tensor
            def _(t):
                t.wait_ge(s_dve.sem, s_dve.n)
                for fc in range(8):
                    ins = t.matmul(p_bg[:, :], betv[:, fc:fc + 1],
                                   gt[:, 8 * fc:8 * fc + 8],
                                   start=(fc == 0), stop=(fc == 7))
                s_pe.bump(ins)

            @blk.vector
            def _(v):
                v.wait_ge(s_pe.sem, s_pe.n)
                s_dve.bump(v.tensor_copy(bgrow[:, :], p_bg[:, :]))

            @blk.scalar
            def _(s):
                s.add_instruction(mybir.InstLoadActFuncSet(
                    name=nc.get_next_instruction_name(),
                    act_func_set_id=6, ins=[], outs=[]))

        m = {}

        for seq in range(n_seq):
            t0 = seq * NT_SEQ

            if not with_p1:
                continue
            with contextlib.ExitStack() as ps1:
                p_kv = _ps(ps1, nc, f"p_kv{seq}", [64, 1024])
                p_ks = _ps(ps1, nc, f"p_ks{seq}", [64, 16])
                p_q = _ps(ps1, nc, f"p_q{seq}", [128, 1024])
                p_pj = _ps(ps1, nc, f"p_pj{seq}", [128, 1024])
                blk = ps1.enter_context(nc.Block(no_gpsimd_drain=True))

                for i in range(NT_SEQ):
                    g = t0 + i

                    @blk.sync
                    def _(sy, i=i, g=g, seq=seq):
                        if ("p1xn1", seq, i - 2) in m:
                            sy.wait_ge(s_dve.sem, m[("p1xn1", seq, i - 2)])
                        s_x[i % 2].bump(sy.dma_start(out=xt[i % 2][:, :],
                                                     in_=x_in[128 * g:128 * (g + 1), :]))
                        m[("p1x", seq, i)] = s_x[i % 2].n

                    @blk.vector
                    def _(v, i=i, seq=seq):
                        v.wait_ge(s_x[i % 2].sem, m[("p1x", seq, i)])
                        if ("p1r1", seq, i - 1) in m:
                            v.wait_ge(s_act.sem, m[("p1r1", seq, i - 1)])
                        v.bn_stats(bnst[:, 0:6], xt[i % 2][:, 0:512])
                        s_dve.bump(v.bn_stats(bnst[:, 6:12], xt[i % 2][:, 512:1024]))
                        v.wait_ge(s_dve.sem, s_dve.n)
                        s_dve.bump(v.bn_aggr(mv1[:, :], bnst[:, :]))
                        v.wait_ge(s_dve.sem, s_dve.n)
                        s_dve.bump(v.tensor_copy(dumm[:, :], mv1[:, :]))
                        m[("p1var", seq, i)] = s_dve.n

                    @blk.scalar
                    def _(s, i=i, seq=seq):
                        s.wait_ge(s_dve.sem, m[("p1var", seq, i)])
                        if ("p1xn1", seq, i - 1) in m:
                            s.wait_ge(s_dve.sem, m[("p1xn1", seq, i - 1)])
                        s.wait_ge(s_act.sem, m.get("t1s", 0))
                        s_act.bump(s.activation(sc["t1s"][:, :], mv1[:, 1:2],
                                                AF.Ln, bias=cst[:, 18:19], scale=1.0))
                        s.wait_ge(s_act.sem, s_act.n)
                        s_act.bump(s.activation(sc["r1"][:, :], sc["t1s"][:, :],
                                                AF.Exp, bias=0.0, scale=-0.5))
                        m[("p1r1", seq, i)] = s_act.n
                        m["t1s"] = s_act.n

                    @blk.vector
                    def _(v, i=i, seq=seq):
                        v.wait_ge(s_act.sem, m[("p1r1", seq, i)])
                        if ("p1xp", seq, i - 1) in m:
                            v.wait_ge(s_pe.sem, m[("p1xp", seq, i - 1)])
                        s_dve.bump(v.tensor_scalar(xn1[:, :], xt[i % 2][:, :],
                                                   mv1[:, 0:1], sc["r1"][:, :],
                                                   ALU.subtract, ALU.mult))
                        m[("p1xn1", seq, i)] = s_dve.n

                    @blk.tensor
                    def _(t, i=i, seq=seq):
                        t.wait_ge(s_dve.sem, m[("p1xn1", seq, i)])
                        if ("p1vev", seq, i - 1) in m:
                            t.wait_ge(s_act.sem, m[("p1vev", seq, i - 1)])
                        for fc in range(8):
                            ins = t.transpose(p_pj[:, 128 * fc:128 * (fc + 1)],
                                              xn1[:, 128 * fc:128 * (fc + 1)],
                                              ident[:, :])
                        s_pe.bump(ins)
                        m[("p1xp", seq, i)] = s_pe.n

                    @blk.scalar
                    def _(s, i=i, seq=seq):
                        s.wait_ge(s_pe.sem, m[("p1xp", seq, i)])
                        if ("p1vmm", seq, i - 1) in m:
                            s.wait_ge(s_pe.sem, m[("p1vmm", seq, i - 1)])
                        s_act.bump(s.activation(xn1t[:, :], p_pj[:, :], AF.Copy,
                                                bias=0.0, scale=1.0))
                        m[("p1xn1t", seq, i)] = s_act.n

                    @blk.tensor
                    def _(t, i=i, seq=seq):
                        t.wait_ge(s_act.sem, m[("p1xn1t", seq, i)])
                        if ("p1qev", seq, i - 1) in m:
                            t.wait_ge(s_act.sem, m[("p1qev", seq, i - 1)])
                        for qc in range(8):
                            for fc in range(8):
                                ins = t.matmul(
                                    p_q[:, 128 * qc:128 * (qc + 1)],
                                    wq[:, fc * D + 128 * qc: fc * D + 128 * (qc + 1)],
                                    xn1t[:, 128 * fc:128 * (fc + 1)],
                                    start=(fc == 0 and qc % 4 == 0),
                                    stop=(fc == 7 and qc % 4 == 3))
                        s_pe.bump(ins)
                        m[("p1qmm", seq, i)] = s_pe.n

                    @blk.scalar
                    def _(s, i=i, seq=seq):
                        s.wait_ge(s_pe.sem, m[("p1qmm", seq, i)])
                        s.wait_ge(s_gq.sem, m.get("p1qst_last", 0))
                        s_act.bump(s.activation(qsb[:, :], p_q[:, :], AF.Relu,
                                                bias=0.0, scale=1.0))
                        m[("p1qev", seq, i)] = s_act.n

                    @blk.gpsimd
                    def _(gp, i=i, g=g, seq=seq):
                        gp.wait_ge(s_act.sem, m[("p1qev", seq, i)])
                        s_gq.bump(gp.dma_start(out=qscr[128 * g:128 * (g + 1), :],
                                               in_=qsb[:, :]))
                        m[("p1qst", seq, i)] = s_gq.n
                        m["p1qst_last"] = s_gq.n

                    @blk.tensor
                    def _(t, i=i, seq=seq):
                        for h4 in range(2):
                            for fc in range(8):
                                ins = t.matmul(
                                    p_pj[:, 512 * h4:512 * (h4 + 1)],
                                    xn1t[:, 128 * fc:128 * (fc + 1)],
                                    wkv[:, fc * 2048 + 512 * h4: fc * 2048 + 512 * (h4 + 1)],
                                    start=(fc == 0), stop=(fc == 7))
                        s_pe.bump(ins)
                        m[("p1kmm", seq, i)] = s_pe.n

                    @blk.scalar
                    def _(s, i=i, seq=seq):
                        s.wait_ge(s_pe.sem, m[("p1kmm", seq, i)])
                        if ("p1kvmm", seq, i - 1) in m:
                            s.wait_ge(s_pe.sem, m[("p1kvmm", seq, i - 1)])
                        s_act.bump(s.activation(ktile[:, :], p_pj[:, :], AF.Relu,
                                                bias=0.0, scale=1.0))
                        m[("p1kev", seq, i)] = s_act.n

                    @blk.tensor
                    def _(t, i=i, seq=seq):
                        t.wait_ge(s_act.sem, m[("p1kev", seq, i)])
                        for h4 in range(2):
                            for fc in range(8):
                                ins = t.matmul(
                                    p_pj[:, 512 * h4:512 * (h4 + 1)],
                                    xn1t[:, 128 * fc:128 * (fc + 1)],
                                    wkv[:, fc * 2048 + 1024 + 512 * h4: fc * 2048 + 1024 + 512 * (h4 + 1)],
                                    start=(fc == 0), stop=(fc == 7))
                        s_pe.bump(ins)
                        m[("p1vmm", seq, i)] = s_pe.n

                    @blk.scalar
                    def _(s, i=i, seq=seq):
                        s.wait_ge(s_pe.sem, m[("p1vmm", seq, i)])
                        s_act.bump(s.activation(vtile[:, :], p_pj[:, :], AF.Copy,
                                                bias=0.0, scale=1.0))
                        m[("p1vev", seq, i)] = s_act.n

                    @blk.tensor
                    def _(t, i=i, seq=seq):
                        t.wait_ge(s_act.sem, m[("p1vev", seq, i)])
                        for h in range(16):
                            t.matmul(p_kv[:, 64 * h:64 * (h + 1)],
                                     ktile[:, 64 * h:64 * (h + 1)],
                                     vtile[:, 64 * h:64 * (h + 1)],
                                     start=(i == 0 and h % 8 == 0),
                                     stop=(i == NT_SEQ - 1 and h % 8 == 7))
                            ins = t.matmul(p_ks[:, h:h + 1],
                                           ktile[:, 64 * h:64 * (h + 1)],
                                           cst[:, 0:1],
                                           start=(i == 0 and h == 0),
                                           stop=(i == NT_SEQ - 1 and h == 15))
                        s_pe.bump(ins)
                        m[("p1kvmm", seq, i)] = s_pe.n

                @blk.vector
                def _(v, seq=seq):
                    v.wait_ge(s_pe.sem, m[("p1kvmm", seq, NT_SEQ - 1)])
                    s_dve.bump(v.tensor_copy(kv2[0:64, :], p_kv[:, :]))
                    v.wait_ge(s_dve.sem, s_dve.n)
                    s_dve.bump(v.tensor_copy(kv2[64:128, :], kv2[0:64, :]))
                    s_dve.bump(v.tensor_scalar_add(ksum2[0:64, :], p_ks[:, :], EPS_A))
                    v.wait_ge(s_dve.sem, s_dve.n)
                    s_dve.bump(v.tensor_copy(ksum2[64:128, :], ksum2[0:64, :]))
                    s_dve.bump(v.memset(ksblk[:, :], 0.0))
                    v.wait_ge(s_dve.sem, s_dve.n)
                    for h in range(16):
                        half = 64 * (h % 2)
                        col = 16 * (h // 2) + h
                        ins = v.tensor_copy(ksblk[half:half + 64, col:col + 1],
                                            ksum2[half:half + 64, h:h + 1])
                    s_dve.bump(ins)
                    m[("ksblk", seq)] = s_dve.n

            if not with_p2:
                continue
            with contextlib.ExitStack() as ps2:
                p_zd = _ps(ps2, nc, f"p_zd{seq}", [128, 16])
                p_o = _ps(ps2, nc, f"p_o{seq}", [128, 1024])
                p_w = _ps(ps2, nc, f"p_w{seq}", [128, 1024])
                p_lg = _ps(ps2, nc, f"p_lg{seq}", [128, 8])
                blk = ps2.enter_context(nc.Block(no_gpsimd_drain=True))

                def em_t3p(j, seq=seq):
                    if j < 0:
                        return

                    @blk.tensor
                    def _(t, j=j, seq=seq):
                        t.wait_ge(s_dve.sem, m[("p2t3", seq, j)])
                        for fc in range(8):
                            ins = t.transpose(
                                p_w[:, 512 * (fc // 4) + 128 * (fc % 4):
                                    512 * (fc // 4) + 128 * (fc % 4) + 128],
                                t3d[j % 2][:, 128 * fc:128 * (fc + 1)],
                                ident[:, :])
                        s_pe.bump(ins)
                        m[("p2t3p", seq, j)] = s_pe.n

                    @blk.scalar
                    def _(s, j=j, seq=seq):
                        s.wait_ge(s_pe.sem, m[("p2t3p", seq, j)])
                        if ("p2lgmm", seq, j - 2) in m:
                            s.wait_ge(s_pe.sem, m[("p2lgmm", seq, j - 2)])
                        s.activation(t3td[j % 2][:, 0:512], p_w[:, 0:512],
                                     AF.Copy, bias=0.0, scale=1.0)
                        s_act.bump(s.activation(t3td[j % 2][:, 512:1024],
                                                p_w[:, 512:1024],
                                                AF.Copy, bias=0.0, scale=1.0))
                        m[("p2t3te", seq, j)] = s_act.n

                    @blk.tensor
                    def _(t, j=j, seq=seq):
                        t.wait_ge(s_act.sem, m[("p2t3te", seq, j)])
                        if ("p2lgev", seq, j - 1) in m:
                            t.wait_ge(s_dve.sem, m[("p2lgev", seq, j - 1)])
                        for fc in range(8):
                            t.matmul(p_lg[:, :],
                                     t3td[j % 2][:, 128 * fc:128 * (fc + 1)],
                                     ggam[:, 8 * fc:8 * fc + 8],
                                     start=(fc == 0), stop=False)
                        ins = t.matmul(p_lg[:, :], onesr[:, 0:128], bgrow[:, :],
                                       start=False, stop=True)
                        s_pe.bump(ins)
                        m[("p2lgmm", seq, j)] = s_pe.n

                def em_argmax(j, seq=seq):
                    if j < 0:
                        return

                    @blk.vector
                    def _(v, j=j, seq=seq):
                        v.wait_ge(s_pe.sem, m[("p2lgmm", seq, j)])
                        if ("p2exp", seq, j - 1) in m:
                            v.wait_ge(s_act.sem, m[("p2exp", seq, j - 1)])
                        s_dve.bump(v.tensor_copy(lgt[:, :], p_lg[:, :]))
                        m[("p2lgev", seq, j)] = s_dve.n
                        v.wait_ge(s_dve.sem, s_dve.n)
                        s_dve.bump(v.reduce_max(sc["mx"][:, :], lgt[:, :], axis=AX.X))
                        v.wait_ge(s_dve.sem, s_dve.n)
                        v.tensor_scalar_mul(sc["nmx"][:, :], sc["mx"][:, :], -1.0)
                        s_dve.bump(v.tensor_scalar(eqt[:, :], lgt[:, :], sc["mx"][:, :],
                                                   None, ALU.is_ge))
                        m[("p2nmx", seq, j)] = s_dve.n
                        v.wait_ge(s_dve.sem, s_dve.n)
                        s_dve.bump(v.tensor_mul(cndt[:, :], eqt[:, :], cst[:, 2:10]))
                        v.wait_ge(s_dve.sem, s_dve.n)
                        s_dve.bump(v.reduce_max(sc["t1"][:, :], cndt[:, :], axis=AX.X))
                        v.wait_ge(s_dve.sem, s_dve.n)
                        s_dve.bump(v.tensor_scalar(sc["idx"][:, :], sc["t1"][:, :],
                                                   -1.0, 8.0, ALU.mult, ALU.add))
                        m[("p2idx", seq, j)] = s_dve.n

                    @blk.scalar
                    def _(s, j=j, seq=seq):
                        s.wait_ge(s_dve.sem, m[("p2nmx", seq, j)])
                        s_act.bump(s.activation(ext[:, :], lgt[:, :], AF.Exp,
                                                bias=sc["nmx"][:, :], scale=1.0,
                                                accum_out=sc["smx"][:, :]))
                        m[("p2exp", seq, j)] = s_act.n

                    @blk.vector
                    def _(v, j=j, seq=seq):
                        v.wait_ge(s_act.sem, m[("p2exp", seq, j)])
                        s_dve.bump(v.reciprocal(sc["wgt"][:, :], sc["smx"][:, :]))
                        v.wait_ge(s_dve.sem, s_dve.n)
                        if ("p2cnt", seq, j - 1) in m:
                            v.wait_ge(s_pe.sem, m[("p2cnt", seq, j - 1)])
                        s_dve.bump(v.tensor_scalar(oht[:, :], cst[:, 10:18],
                                                   sc["idx"][:, :], None, ALU.is_equal))
                        m[("p2oh", seq, j)] = s_dve.n
                        v.wait_ge(s_dve.sem, s_dve.n)
                        v.tensor_copy(scpkd[j % 2][:, 0:1], sc["idx"][:, :])
                        s_dve.bump(v.tensor_copy(scpkd[j % 2][:, 1:2], sc["wgt"][:, :]))
                        m[("p2scpk", seq, j)] = s_dve.n

                    @blk.gpsimd
                    def _(gp, j=j, seq=seq):
                        gg = seq * NT_SEQ + j
                        gp.wait_ge(s_dve.sem, m[("p2scpk", seq, j)])
                        s_gsc[j % 2].bump(gp.dma_start(out=scal_out[128 * gg:128 * (gg + 1), :],
                                                       in_=scpkd[j % 2][:, :]))
                        m[("p2scst", seq, j)] = s_gsc[j % 2].n
                        m[("scst_last", j % 2)] = s_gsc[j % 2].n

                def em_cnt(j, seq=seq):
                    if j < 0:
                        return

                    @blk.tensor
                    def _(t, j=j, seq=seq):
                        t.wait_ge(s_dve.sem, m[("p2oh", seq, j)])
                        ins = t.matmul(p_cnt[:, seq:seq + 1], oht[:, :], cst[:, 0:1],
                                       start=(seq == 0 and j == 0),
                                       stop=(seq == NSEQ - 1 and j == NT_SEQ - 1),
                                       skip_group_check=True)
                        s_pe.bump(ins)
                        m[("p2cnt", seq, j)] = s_pe.n

                for i in range(NT_SEQ):
                    g = t0 + i

                    @blk.sync
                    def _(sy, i=i, g=g, seq=seq):
                        if ("p2x1", seq, i - 2) in m:
                            sy.wait_ge(s_dve.sem, m[("p2x1", seq, i - 2)])
                        s_x[i % 2].bump(sy.dma_start(out=xt[i % 2][:, :],
                                                     in_=x_in[128 * g:128 * (g + 1), :]))
                        m[("p2x", seq, i)] = s_x[i % 2].n
                        if i == 0:
                            sy.wait_ge(s_gq.sem, m[("p1qst", seq, NT_SEQ - 1)])
                        if ("p2omm", seq, i - 2) in m:
                            sy.wait_ge(s_pe.sem, m[("p2omm", seq, i - 2)])
                        s_q2[i % 2].bump(sy.dma_start(out=qt[i % 2][:, :],
                                                      in_=qscr[128 * g:128 * (g + 1), :]))
                        m[("p2q", seq, i)] = s_q2[i % 2].n

                    @blk.tensor
                    def _(t, i=i, seq=seq):
                        t.wait_ge(s_q2[i % 2].sem, m[("p2q", seq, i)])
                        if i == 0:
                            t.wait_ge(s_dve.sem, m[("ksblk", seq)])
                        if ("p2z", seq, i - 1) in m:
                            t.wait_ge(s_dve.sem, m[("p2z", seq, i - 1)])
                        for fc in range(8):
                            ins = t.matmul(p_zd[:, :],
                                           qt[i % 2][:, 128 * fc:128 * (fc + 1)],
                                           ksblk[:, 16 * fc:16 * (fc + 1)],
                                           start=(fc == 0), stop=(fc == 7))
                        s_pe.bump(ins)
                        m[("p2zmm", seq, i)] = s_pe.n

                    @blk.scalar
                    def _(s, i=i, seq=seq):
                        s.wait_ge(s_pe.sem, m[("p2zmm", seq, i)])
                        if ("p2ozs", seq, i - 1) in m:
                            s.wait_ge(s_dve.sem, m[("p2ozs", seq, i - 1)])
                        s_act.bump(s.activation(zt[:, :], p_zd[:, :], AF.Copy,
                                                bias=EPS_A, scale=1.0))
                        m[("p2zev", seq, i)] = s_act.n

                    @blk.vector
                    def _(v, i=i, seq=seq):
                        v.wait_ge(s_act.sem, m[("p2zev", seq, i)])
                        s_dve.bump(v.reciprocal(z2[:, :], zt[:, :]))
                        m[("p2z", seq, i)] = s_dve.n

                    @blk.tensor
                    def _(t, i=i, seq=seq):
                        if ("p2ozte", seq, i - 1) in m:
                            t.wait_ge(s_act.sem, m[("p2ozte", seq, i - 1)])
                        for h in range(16):
                            half = 64 * (h % 2)
                            qc = h // 2
                            col = 64 * (h // 2) + 512 * (h % 2)
                            ins = t.matmul(
                                p_o[:, col:col + 64],
                                qt[i % 2][half:half + 64, 128 * qc:128 * (qc + 1)],
                                kv2[half:half + 64, 64 * h:64 * (h + 1)],
                                start=True, stop=True)
                        s_pe.bump(ins)
                        m[("p2omm", seq, i)] = s_pe.n

                    em_t3p(i - 1)
                    em_cnt(i - 2)

                    @blk.vector
                    def _(v, i=i, seq=seq):
                        v.wait_ge(s_pe.sem, m[("p2omm", seq, i)])
                        v.wait_ge(s_dve.sem, m[("p2z", seq, i)])
                        if ("p2ozp", seq, i - 1) in m:
                            v.wait_ge(s_pe.sem, m[("p2ozp", seq, i - 1)])
                        out_ap = oz[:, :].rearrange("p (a b c) -> p a b c",
                                                    a=8, b=2, c=64)
                        in_ap = p_o[:, :].rearrange("p (b a c) -> p a b c",
                                                    b=2, a=8, c=64)
                        z_ap = z2[:, :].rearrange("p (a b) -> p a b",
                                                  a=8, b=2).unsqueeze(3)
                        z_ap = z_ap.broadcast_to([128, 8, 2, 64])
                        s_dve.bump(v.tensor_tensor(out_ap, in_ap, z_ap, ALU.mult))
                        m[("p2ozs", seq, i)] = s_dve.n

                    @blk.tensor
                    def _(t, i=i, seq=seq):
                        t.wait_ge(s_dve.sem, m[("p2ozs", seq, i)])
                        for fc in range(8):
                            ins = t.transpose(
                                p_o[:, 512 * (fc // 4) + 128 * (fc % 4):
                                    512 * (fc // 4) + 128 * (fc % 4) + 128],
                                oz[:, 128 * fc:128 * (fc + 1)],
                                ident[:, :])
                        s_pe.bump(ins)
                        m[("p2ozp", seq, i)] = s_pe.n

                    @blk.scalar
                    def _(s, i=i, seq=seq):
                        s.wait_ge(s_pe.sem, m[("p2ozp", seq, i)])
                        if ("p2wmm", seq, i - 1) in m:
                            s.wait_ge(s_pe.sem, m[("p2wmm", seq, i - 1)])
                        s.activation(ozt[:, 0:512], p_o[:, 0:512], AF.Copy,
                                     bias=0.0, scale=1.0)
                        s_act.bump(s.activation(ozt[:, 512:1024], p_o[:, 512:1024],
                                                AF.Copy, bias=0.0, scale=1.0))
                        m[("p2ozte", seq, i)] = s_act.n

                    @blk.tensor
                    def _(t, i=i, seq=seq):
                        t.wait_ge(s_act.sem, m[("p2ozte", seq, i)])
                        if ("p2x1", seq, i - 1) in m:
                            t.wait_ge(s_dve.sem, m[("p2x1", seq, i - 1)])
                        for oc in range(2):
                            for fc in range(8):
                                t.matmul(
                                    p_w[:, 512 * oc:512 * (oc + 1)],
                                    ozt[:, 128 * fc:128 * (fc + 1)],
                                    wout[:, fc * D + 512 * oc: fc * D + 512 * (oc + 1)],
                                    start=(fc == 0), stop=False)
                            ins = t.matmul(p_w[:, 512 * oc:512 * (oc + 1)],
                                           onesr[:, 0:128],
                                           boutr[:, 512 * oc:512 * (oc + 1)],
                                           start=False, stop=True)
                        s_pe.bump(ins)
                        m[("p2wmm", seq, i)] = s_pe.n

                    @blk.vector
                    def _(v, i=i, seq=seq):
                        v.wait_ge(s_pe.sem, m[("p2wmm", seq, i)])
                        v.wait_ge(s_x[i % 2].sem, m[("p2x", seq, i)])
                        v.wait_ge(s_gx1.sem, m.get("p2x1st_last", 0))
                        if ("p2t3", seq, i - 1) in m:
                            v.wait_ge(s_dve.sem, m[("p2t3", seq, i - 1)])
                        s_dve.bump(v.tensor_add(x1b[:, :], p_w[:, :], xt[i % 2][:, :]))
                        m[("p2x1", seq, i)] = s_dve.n

                    @blk.gpsimd
                    def _(gp, i=i, g=g, seq=seq):
                        gp.wait_ge(s_dve.sem, m[("p2x1", seq, i)])
                        s_gx1.bump(gp.dma_start(out=x1_out[128 * g:128 * (g + 1), :],
                                                in_=x1b[:, :]))
                        m[("p2x1st", seq, i)] = s_gx1.n
                        m["p2x1st_last"] = s_gx1.n

                    @blk.vector
                    def _(v, i=i, seq=seq):
                        v.wait_ge(s_dve.sem, m[("p2x1", seq, i)])
                        if ("p2r2", seq, i - 1) in m:
                            v.wait_ge(s_act.sem, m[("p2r2", seq, i - 1)])
                        v.bn_stats(bnst[:, 0:6], x1b[:, 0:512])
                        s_dve.bump(v.bn_stats(bnst[:, 6:12], x1b[:, 512:1024]))
                        v.wait_ge(s_dve.sem, s_dve.n)
                        s_dve.bump(v.bn_aggr(mv2[:, :], bnst[:, :]))
                        m[("p2var2", seq, i)] = s_dve.n

                    @blk.scalar
                    def _(s, i=i, seq=seq):
                        s.wait_ge(s_dve.sem, m[("p2var2", seq, i)])
                        if ("p2t3", seq, i - 1) in m:
                            s.wait_ge(s_dve.sem, m[("p2t3", seq, i - 1)])
                        s.wait_ge(s_act.sem, m.get("t1s", 0))
                        s_act.bump(s.activation(sc["t1s"][:, :], mv2[:, 1:2],
                                                AF.Ln, bias=cst[:, 18:19], scale=1.0))
                        s.wait_ge(s_act.sem, s_act.n)
                        s_act.bump(s.activation(sc["r2"][:, :], sc["t1s"][:, :],
                                                AF.Exp, bias=0.0, scale=-0.5))
                        m[("p2r2", seq, i)] = s_act.n
                        m["t1s"] = s_act.n

                    @blk.vector
                    def _(v, i=i, seq=seq):
                        v.wait_ge(s_act.sem, m[("p2r2", seq, i)])
                        if ("p2r3", seq, i - 1) in m:
                            v.wait_ge(s_act.sem, m[("p2r3", seq, i - 1)])
                        s_dve.bump(v.tensor_mul(sc["t0"][:, :], sc["r2"][:, :],
                                                sc["r2"][:, :]))
                        v.wait_ge(s_dve.sem, s_dve.n)
                        s_dve.bump(v.tensor_mul(sc["var3"][:, :], mv2[:, 1:2],
                                                sc["t0"][:, :]))
                        m[("p2v3", seq, i)] = s_dve.n

                    @blk.scalar
                    def _(s, i=i, seq=seq):
                        s.wait_ge(s_dve.sem, m[("p2v3", seq, i)])
                        s.wait_ge(s_act.sem, m.get("t2s", 0))
                        s_act.bump(s.activation(sc["sq3"][:, :], sc["var3"][:, :],
                                                AF.Ln, bias=cst[:, 18:19], scale=1.0))
                        s.wait_ge(s_act.sem, s_act.n)
                        s_act.bump(s.activation(sc["r3"][:, :], sc["sq3"][:, :],
                                                AF.Exp, bias=0.0, scale=-0.5))
                        m[("p2r3", seq, i)] = s_act.n
                        m["t2s"] = s_act.n

                    @blk.vector
                    def _(v, i=i, seq=seq):
                        v.wait_ge(s_act.sem, m[("p2r3", seq, i)])
                        s_dve.bump(v.tensor_mul(sc["ss"][:, :], sc["r2"][:, :],
                                                sc["r3"][:, :]))
                        v.wait_ge(s_dve.sem, s_dve.n)
                        v.wait_ge(s_gsc[i % 2].sem, m.get(("scst_last", i % 2), 0))
                        v.tensor_copy(scpkd[i % 2][:, 2:3], mv2[:, 0:1])
                        v.tensor_copy(scpkd[i % 2][:, 3:4], sc["r2"][:, :])
                        v.tensor_copy(scpkd[i % 2][:, 4:5], mv2[:, 0:1])
                        v.tensor_copy(scpkd[i % 2][:, 5:6], sc["ss"][:, :])
                        s_dve.bump(v.memset(scpkd[i % 2][:, 6:8], 0.0))
                        m[("p2scpk1", seq, i)] = s_dve.n
                        v.wait_ge(s_dve.sem, s_dve.n)
                        if ("p2t3p", seq, i - 2) in m:
                            v.wait_ge(s_pe.sem, m[("p2t3p", seq, i - 2)])
                        s_dve.bump(v.tensor_scalar(t3d[i % 2][:, :], x1b[:, :],
                                                   mv2[:, 0:1], sc["ss"][:, :],
                                                   ALU.subtract, ALU.mult))
                        m[("p2t3", seq, i)] = s_dve.n

                    em_argmax(i - 1)

                em_t3p(NT_SEQ - 1)
                em_cnt(NT_SEQ - 2)
                em_argmax(NT_SEQ - 1)
                em_cnt(NT_SEQ - 1)


        with nc.Block(no_gpsimd_drain=True) as blk:
            @blk.vector
            def _(v):
                v.wait_ge(s_pe.sem, s_pe.n)
                v.wait_ge(s_act.sem, s_act.n)
                s_dve.bump(v.tensor_copy(cntsb[:, :], p_cnt[:, :]))

            @blk.gpsimd
            def _(gp):
                gp.wait_ge(s_dve.sem, s_dve.n)
                s_gq.bump(gp.dma_start(out=cnt_out[:, :], in_=cntsb[:, :]))
                gp.wait_ge(s_gq.sem, s_gq.n)
                gp.wait_ge(s_gx1.sem, s_gx1.n)
                gp.wait_ge(s_gsc[0].sem, s_gsc[0].n)
                gp.wait_ge(s_gsc[1].sem, s_gsc[1].n)

    return nc


def build_b():
    nc = bass.Bass("TRN2", target_bir_lowering=False)
    x1_in = nc.dram_tensor("x1", [TOK, D], F32, kind="ExternalInput")
    scal_in = nc.dram_tensor("scal", [TOK, 8], F32, kind="ExternalInput")
    cnt_in = nc.dram_tensor("cntall", [16, 8], F32, kind="ExternalInput")
    mask_in = nc.dram_tensor("maskcol", [16, 2], F32, kind="ExternalInput")
    wi_in = nc.dram_tensor("wi", [NE * D, I_FF], F32, kind="ExternalInput")
    wo_in = nc.dram_tensor("wo", [NE * I_FF, D], F32, kind="ExternalInput")
    gam_in = nc.dram_tensor("gam", [128, 8], F32, kind="ExternalInput")
    bet_in = nc.dram_tensor("bet", [128, 8], F32, kind="ExternalInput")
    ident_in = nc.dram_tensor("ident", [128, 128], F32, kind="ExternalInput")
    cst_in = nc.dram_tensor("cst", [128, 32], F32, kind="ExternalInput")
    ones_in = nc.dram_tensor("onesrow", [1, 128], F32, kind="ExternalInput")
    toku_in = nc.dram_tensor("toku", [128, NT], U32, kind="ExternalInput")
    tokf_in = nc.dram_tensor("tokf", [128, NT], F32, kind="ExternalInput")
    out_dram = nc.dram_tensor("out", [TOK, D], F32, kind="ExternalOutput")

    gidx = nc.dram_tensor("gidx_d", [GIDX_ROWS, 1], U32)
    eobuf = nc.dram_tensor("eobuf", [NSLOT, D], F32)

    ctx = contextlib.ExitStack()
    with ctx:
        gamv = _sb(ctx, nc, "gamv_sb", [128, 8])
        betv = _sb(ctx, nc, "betv_sb", [128, 8])
        ident = _sb(ctx, nc, "ident_sb", [128, 128])
        cst = _sb(ctx, nc, "cst_sb", [128, 32])
        onesr = _sb(ctx, nc, "onesr", [1, 128])
        toku = _sb(ctx, nc, "toku_sb", [128, NT], U32)
        tokf = _sb(ctx, nc, "tokf_sb", [128, NT])
        cntsb = _sb(ctx, nc, "cntsb", [16, 8])
        masksb = _sb(ctx, nc, "masksb", [16, 2])
        offs = _sb(ctx, nc, "offs", [8, 2])
        scal = _sb(ctx, nc, "scal_sb", [128, 8 * NT])
        rowt = _sb(ctx, nc, "rowt", [1, 128])
        keepc = _sb(ctx, nc, "keepc", [128, NT])
        slotc = _sb(ctx, nc, "slotc", [128, NT])
        posu = _sb(ctx, nc, "posu", [128, NT], U32)
        gposu = _sb(ctx, nc, "gposu", [128, NT], U32)
        wkf = _sb(ctx, nc, "wkf", [128, NT])
        zb = _sb(ctx, nc, "zb", [128, 64], U32)
        tp = {k: _sb(ctx, nc, f"tp_{k}", [128, 1]) for k in
              ["a", "b", "c", "k2", "pf", "gf"]}
        gxi = _sb(ctx, nc, "gxi", [128, 1], U32)
        dsc = _sb(ctx, nc, "dsc", [128, 8])

        s_ld = Ch(ctx.enter_context(nc.semaphore("s_ld")), 16)
        s_pe = Ch(ctx.enter_context(nc.semaphore("s_pe")))
        s_act = Ch(ctx.enter_context(nc.semaphore("s_act")))
        s_dve = Ch(ctx.enter_context(nc.semaphore("s_dve")))
        s_gs = Ch(ctx.enter_context(nc.semaphore("s_gs")), 16)
        s_gz = Ch(ctx.enter_context(nc.semaphore("s_gz")))
        s_gx = Ch(ctx.enter_context(nc.semaphore("s_gx")), 16)
        s_gd = Ch(ctx.enter_context(nc.semaphore("s_gd")), 16)
        s_geo = [Ch(ctx.enter_context(nc.semaphore(f"s_geo{j}")), 16) for j in range(2)]
        s_geg = [Ch(ctx.enter_context(nc.semaphore(f"s_geg{j}")), 16) for j in range(4)]
        s_w = [Ch(ctx.enter_context(nc.semaphore(f"s_w{j}")), 16) for j in range(2)]
        s_wp = Ch(ctx.enter_context(nc.semaphore("s_wp")), 16)
        s_st = [Ch(ctx.enter_context(nc.semaphore(f"s_st{j}")), 16) for j in range(4)]
        s_x1 = [Ch(ctx.enter_context(nc.semaphore(f"s_x1{j}")), 16) for j in range(4)]

        m = {}

        with nc.psum_tensor("p_off", [8, 2], F32) as p_off, nc.Block(no_gpsimd_drain=True) as blk:
            @blk.sync
            def _(sy):
                for dst, src in [(gamv, gam_in), (betv, bet_in), (ident, ident_in),
                                 (cst, cst_in), (onesr, ones_in), (toku, toku_in),
                                 (tokf, tokf_in), (cntsb, cnt_in), (masksb, mask_in)]:
                    s_ld.bump(sy.dma_start(out=dst[:, :], in_=src[:, :]))
                for t in range(NT):
                    s_ld.bump(sy.dma_start(out=scal[:, 8 * t:8 * (t + 1)],
                                           in_=scal_in[128 * t:128 * (t + 1), :]))

            @blk.gpsimd
            def _(gp):
                gp.memset(zb[:, :], 0)
                s_gz.bump(gp.engine_nop())
                gp.wait_ge(s_gz.sem, s_gz.n)
                s_gs.bump(gp.dma_start(
                    out=gidx[:, :].rearrange("(a b) c -> a (b c)", a=128),
                    in_=zb[:, :]))
                m["gzero"] = s_gs.n

            @blk.tensor
            def _(t):
                t.wait_ge(s_ld.sem, s_ld.n)
                s_pe.bump(t.matmul(p_off[:, :], cntsb[:, :], masksb[:, :],
                                   start=True, stop=True))

            @blk.vector
            def _(v):
                v.wait_ge(s_pe.sem, s_pe.n)
                s_dve.bump(v.tensor_copy(offs[:, :], p_off[:, :]))
                m["offs"] = s_dve.n

            @blk.scalar
            def _(s):
                s.preload_act_table(AF.Gelu)

        with contextlib.ExitStack() as ps:
            ohall = _sb(ps, nc, "ohall", [8, TOK])
            scn = _sb(ps, nc, "scn", [8, TOK])
            rke = _sb(ps, nc, "rke", [8, TOK])
            k8 = _sb(ps, nc, "k8", [8, TOK])
            tmp8 = scn
            sr8 = ohall
            p_r = [_ps(ps, nc, f"p_r{j}", [1, 128]) for j in range(2)]
            p_b = [_ps(ps, nc, f"p_b{j}", [8, 128]) for j in range(2)]
            p_k = _ps(ps, nc, "p_k", [128, 2])
            rowt2 = [rowt, _sb(ps, nc, "rowt1", [1, 128])]
            blk = ps.enter_context(nc.Block(no_gpsimd_drain=True))

            def em_bc(i):
                if i < 0:
                    return

                @blk.tensor
                def _(t, i=i):
                    t.wait_ge(s_dve.sem, m[("row", i)])
                    if ("ohc", i - 2) in m:
                        t.wait_ge(s_dve.sem, m[("ohc", i - 2)])
                    s_pe.bump(t.matmul(p_b[i % 2][:, :], onesr[:, 0:8],
                                       rowt2[i % 2][:, :],
                                       start=True, stop=True))
                    m[("bc", i)] = s_pe.n

                @blk.vector
                def _(v, i=i):
                    v.wait_ge(s_pe.sem, m[("bc", i)])
                    s_dve.bump(v.tensor_scalar(ohall[:, 128 * i:128 * (i + 1)],
                                               p_b[i % 2][:, :], cst[0:8, 1:2],
                                               None, ALU.is_equal))
                    m[("ohc", i)] = s_dve.n

            for i in range(NT):
                @blk.tensor
                def _(t, i=i):
                    if i == 0:
                        t.wait_ge(s_dve.sem, m["offs"])
                    if ("row", i - 2) in m:
                        t.wait_ge(s_dve.sem, m[("row", i - 2)])
                    s_pe.bump(t.transpose(p_r[i % 2][:, :],
                                          scal[:, 8 * i:8 * i + 1],
                                          ident[:, :]))
                    m[("rt", i)] = s_pe.n

                @blk.vector
                def _(v, i=i):
                    v.wait_ge(s_pe.sem, m[("rt", i)])
                    if ("bc", i - 2) in m:
                        v.wait_ge(s_pe.sem, m[("bc", i - 2)])
                    s_dve.bump(v.tensor_copy(rowt2[i % 2][:, :], p_r[i % 2][:, :]))
                    m[("row", i)] = s_dve.n

                em_bc(i - 1)
            em_bc(NT - 1)

            @blk.vector
            def _(v):
                v.wait_ge(s_dve.sem, m[("ohc", NT - 1)])
                s_dve.bump(v.tensor_tensor_scan(scn[:, :], ohall[:, :], ohall[:, :],
                                                0.0, ALU.add, ALU.bypass))
                v.wait_ge(s_dve.sem, s_dve.n)
                s_dve.bump(v.tensor_sub(rke[:, :], scn[:, :], ohall[:, :]))
                v.wait_ge(s_dve.sem, s_dve.n)
                s_dve.bump(v.tensor_scalar(tmp8[:, 0:SEQLEN], rke[:, 0:SEQLEN],
                                           offs[:, 0:1], CAP, ALU.add, ALU.is_lt))
                s_dve.bump(v.tensor_scalar(tmp8[:, SEQLEN:TOK], rke[:, SEQLEN:TOK],
                                           offs[:, 1:2], CAP, ALU.add, ALU.is_lt))
                v.wait_ge(s_dve.sem, s_dve.n)
                s_dve.bump(v.tensor_mul(k8[:, :], tmp8[:, :], ohall[:, :]))
                v.wait_ge(s_dve.sem, s_dve.n)
                s_dve.bump(v.tensor_mul(sr8[:, :], rke[:, :], k8[:, :]))
                m["scan"] = s_dve.n

            for i in range(NT):
                @blk.tensor
                def _(t, i=i):
                    t.wait_ge(s_dve.sem, m["scan"])
                    if ("kse", i - 1) in m:
                        t.wait_ge(s_dve.sem, m[("kse", i - 1)])
                    t.matmul(p_k[:, 0:1], k8[:, 128 * i:128 * (i + 1)],
                             cst[0:8, 0:1], start=True, stop=True)
                    s_pe.bump(t.matmul(p_k[:, 1:2], sr8[:, 128 * i:128 * (i + 1)],
                                       cst[0:8, 0:1], start=True, stop=True,
                                       skip_group_check=True))
                    m[("ks", i)] = s_pe.n

                @blk.vector
                def _(v, i=i):
                    v.wait_ge(s_pe.sem, m[("ks", i)])
                    v.tensor_copy(keepc[:, i:i + 1], p_k[:, 0:1])
                    s_dve.bump(v.tensor_copy(slotc[:, i:i + 1], p_k[:, 1:2]))
                    m[("kse", i)] = s_dve.n
                    v.wait_ge(s_dve.sem, s_dve.n)
                    v.scalar_tensor_tensor(tp["a"][:, :], scal[:, 8 * i:8 * i + 1],
                                           float(C_PAD), slotc[:, i:i + 1],
                                           ALU.mult, ALU.add)
                    s_dve.bump(v.tensor_scalar(tp["b"][:, :], slotc[:, i:i + 1],
                                               float(C_PAD), None, ALU.is_lt))
                    v.wait_ge(s_dve.sem, s_dve.n)
                    s_dve.bump(v.tensor_mul(tp["k2"][:, :], keepc[:, i:i + 1],
                                            tp["b"][:, :]))
                    v.wait_ge(s_dve.sem, s_dve.n)
                    v.tensor_scalar(tp["c"][:, :], tokf[:, i:i + 1], 1.0, TRASH0,
                                    ALU.mult, ALU.add)
                    s_dve.bump(v.tensor_mul(tp["gf"][:, :], tp["a"][:, :], tp["k2"][:, :]))
                    v.wait_ge(s_dve.sem, s_dve.n)
                    s_dve.bump(v.tensor_mul(wkf[:, i:i + 1],
                                            scal[:, 8 * i + 1:8 * i + 2], tp["k2"][:, :]))
                    v.wait_ge(s_dve.sem, s_dve.n)
                    v.tensor_mul(tp["pf"][:, :], tp["c"][:, :], tp["k2"][:, :])
                    s_dve.bump(v.engine_nop())
                    v.wait_ge(s_dve.sem, s_dve.n)
                    s_dve.bump(v.tensor_sub(tp["b"][:, :], tp["c"][:, :], tp["pf"][:, :]))
                    v.wait_ge(s_dve.sem, s_dve.n)
                    s_dve.bump(v.tensor_add(tp["pf"][:, :], tp["gf"][:, :], tp["b"][:, :]))
                    v.wait_ge(s_dve.sem, s_dve.n)
                    v.tensor_copy(posu[:, i:i + 1], tp["pf"][:, :])
                    s_dve.bump(v.tensor_copy(gposu[:, i:i + 1], tp["gf"][:, :]))
                    m[("pos", i)] = s_dve.n

            @blk.gpsimd
            def _(gp):
                gp.wait_ge(s_gs.sem, m["gzero"])
                for i in range(NT):
                    gp.wait_ge(s_dve.sem, m[("pos", i)])
                    s_gs.bump(gp.indirect_dma_start(
                        out=gidx[:, :],
                        out_offset=bass.IndirectOffsetOnAxis(ap=posu[:, i:i + 1], axis=0),
                        in_=toku[:, i:i + 1], in_offset=None))
                m["scat"] = s_gs.n

        with contextlib.ExitStack() as fstack:
            drow = _sb(fstack, nc, "drow", [128, D])
            dt3 = _sb(fstack, nc, "dt3", [128, D])
            dspt = [_sb(fstack, nc, f"dspt{i}", [128, ST_E * D]) for i in range(2)]
            wmov = [_sb(fstack, nc, f"wmov{i}", [128, 8 * 512]) for i in range(2)]
            wipre = _sb(fstack, nc, "wipre", [128, 8 * 512])
            hsb = [_sb(fstack, nc, f"hsb{i}", [128, I_FF]) for i in range(ST_E)]
            eosb = [_sb(fstack, nc, f"eosb{i}", [128, D]) for i in range(2)]
            p_h = [_ps(fstack, nc, f"p_h{j}", [128, 512]) for j in range(2)]
            p_t2 = _ps(fstack, nc, "p_t2", [128, 1024])
            p_eo = [_ps(fstack, nc, f"p_eo{j}", [128, 512]) for j in range(4)]

            def em_dgather(blk, de, j):
                st = ST_E * de + j

                @blk.gpsimd
                def _(gp, j=j, st=st, de=de):
                    if de == 0 and j == 0:
                        gp.wait_ge(s_gs.sem, m["scat"])
                    prev = (de, j - 1) if j > 0 else (de - 1, ST_E - 1)
                    if ("dt3",) + prev in m:
                        gp.wait_ge(s_dve.sem, m[("dt3",) + prev])
                    s_gx.bump(gp.dma_start(out=gxi[:, :],
                                           in_=gidx[128 * st:128 * (st + 1), :]))
                    gp.wait_ge(s_gx.sem, s_gx.n)
                    s_gd.bump(gp.indirect_dma_start(
                        out=drow[:, :], out_offset=None, in_=x1_in[:, :],
                        in_offset=bass.IndirectOffsetOnAxis(ap=gxi[:, :], axis=0)))
                    s_gd.bump(gp.indirect_dma_start(
                        out=dsc[:, :], out_offset=None, in_=scal_in[:, :],
                        in_offset=bass.IndirectOffsetOnAxis(ap=gxi[:, :], axis=0)))
                    m[("dg", de, j)] = s_gd.n

                @blk.vector
                def _(v, j=j, de=de):
                    v.wait_ge(s_gd.sem, m[("dg", de, j)])
                    s_dve.bump(v.tensor_scalar_sub(dt3[:, :], drow[:, :],
                                                   dsc[:, 4:5]))
                    v.wait_ge(s_dve.sem, s_dve.n)
                    s_dve.bump(v.tensor_scalar_mul(dt3[:, :], dt3[:, :],
                                                   dsc[:, 5:6]))
                    m[("dt3", de, j)] = s_dve.n

            def em_dtp(blk, de, j, pc, host_e):
                @blk.tensor
                def _(t, j=j, pc=pc, de=de, host_e=host_e):
                    if pc == 0:
                        t.wait_ge(s_dve.sem, m[("dt3", de, j)])
                    if host_e is not None and j == 0 and pc == 0:
                        t.wait_ge(s_act.sem,
                                  m[("hge", host_e, 7, ST_E - 1)])
                    prev = (de, j - 1, pc) if j > 0 else (de - 1, ST_E - 1, pc)
                    if ("dte",) + prev in m:
                        t.wait_ge(s_dve.sem, m[("dte",) + prev])
                    for k in range(4):
                        fc = 4 * pc + k
                        ins = t.transpose(
                            p_h[pc][:, 128 * k:128 * (k + 1)],
                            dt3[:, 128 * fc:128 * (fc + 1)],
                            ident[:, :])
                    s_pe.bump(ins)
                    m[("dtp", de, j, pc)] = s_pe.n

                @blk.vector
                def _(v, j=j, pc=pc, de=de):
                    v.wait_ge(s_pe.sem, m[("dtp", de, j, pc)])
                    for k in range(4):
                        fc = 4 * pc + k
                        ins = v.tensor_scalar(
                            dspt[de % 2][:, j * D + 128 * fc: j * D + 128 * (fc + 1)],
                            p_h[pc][:, 128 * k:128 * (k + 1)],
                            gamv[:, fc:fc + 1], betv[:, fc:fc + 1],
                            ALU.mult, ALU.add)
                    s_dve.bump(ins)
                    m[("dte", de, j, pc)] = s_dve.n

            def em_dispatch(blk, de, host_e):
                for j in range(ST_E):
                    em_dgather(blk, de, j)
                    for pc in range(2):
                        em_dtp(blk, de, j, pc, host_e)

            for e in range(NE):
                    with nc.Block(no_gpsimd_drain=True) as blk:
                        if e == 0:
                            em_dispatch(blk, 0, None)

                        for ic in range(8):
                            if not (ic == 0 and e > 0):
                                @blk.sync
                                def _(sy, e=e, ic=ic):
                                    if ("wim", e, ic - 2) in m:
                                        sy.wait_ge(s_pe.sem, m[("wim", e, ic - 2)])
                                    if e > 0 and ic == 1:
                                        sy.wait_ge(s_pe.sem, m[("wom", e - 1, 1, 31)])
                                    if e > 0 and ic == 0:
                                        sy.wait_ge(s_pe.sem, m[("wom", e - 1, 1, 30)])
                                    for fc in range(8):
                                        s_w[ic % 2].bump(sy.dma_start(
                                            out=wmov[ic % 2][:, 512 * fc:512 * (fc + 1)],
                                            in_=wi_in[D * e + 128 * fc: D * e + 128 * (fc + 1),
                                                      512 * ic:512 * (ic + 1)]))
                                    m[("wil", e, ic)] = s_w[ic % 2].n

                            for j in range(ST_E):
                                @blk.tensor
                                def _(t, e=e, ic=ic, j=j):
                                    pre = (ic == 0 and e > 0)
                                    if pre:
                                        t.wait_ge(s_wp.sem, m[("wipre", e)])
                                    else:
                                        t.wait_ge(s_w[ic % 2].sem, m[("wil", e, ic)])
                                    if ic == 0 and j == 0:
                                        t.wait_ge(s_dve.sem, m[("dte", e, ST_E - 1, 1)])
                                    pj = (ic * ST_E + j) % 2
                                    prev = ic * ST_E + j - 2
                                    if prev >= 0:
                                        pic, pjj = divmod(prev, ST_E)
                                        t.wait_ge(s_act.sem, m[("hge", e, pic, pjj)])
                                    src = wipre if pre else wmov[ic % 2]
                                    for fc in range(8):
                                        ins = t.matmul(
                                            p_h[pj][:, :],
                                            dspt[e % 2][:, j * D + 128 * fc: j * D + 128 * (fc + 1)],
                                            src[:, 512 * fc:512 * (fc + 1)],
                                            start=(fc == 0), stop=(fc == 7))
                                    s_pe.bump(ins)
                                    m[("wim", e, ic, j)] = s_pe.n
                                    if j == ST_E - 1:
                                        m[("wim", e, ic)] = s_pe.n

                                @blk.scalar
                                def _(s, e=e, ic=ic, j=j):
                                    s.wait_ge(s_pe.sem, m[("wim", e, ic, j)])
                                    pj = (ic * ST_E + j) % 2
                                    s_act.bump(s.activation(
                                        hsb[j][:, 512 * ic:512 * (ic + 1)],
                                        p_h[pj][:, :], AF.Gelu, bias=0.0, scale=1.0))
                                    m[("hge", e, ic, j)] = s_act.n

                        if e + 1 < NE:
                            em_dgather(blk, e + 1, 0)
                        for j in range(ST_E):
                            for gg in range(8):
                                if gg == 4 and e + 1 < NE and j < ST_E:
                                    for pc in range(2):
                                        em_dtp(blk, e + 1, j, pc, e)
                                    if j + 1 < ST_E:
                                        em_dgather(blk, e + 1, j + 1)

                                @blk.tensor
                                def _(t, e=e, j=j, gg=gg):
                                    if gg == 0:
                                        t.wait_ge(s_act.sem, m[("hge", e, 7, j)])
                                    cc = j * 8 + gg
                                    if cc >= 2:
                                        pcc = cc - 2
                                        t.wait_ge(s_dve.sem, m[("hte", e, pcc // 8, pcc % 8)])
                                    bank = 512 * (gg % 2)
                                    for k in range(4):
                                        c = 4 * gg + k
                                        ins = t.transpose(
                                            p_t2[:, bank + 128 * k:bank + 128 * (k + 1)],
                                            hsb[j][:, 128 * c:128 * (c + 1)], ident[:, :])
                                    s_pe.bump(ins)
                                    m[("htp", e, j, gg)] = s_pe.n

                                @blk.vector
                                def _(v, e=e, j=j, gg=gg):
                                    v.wait_ge(s_pe.sem, m[("htp", e, j, gg)])
                                    bank = 512 * (gg % 2)
                                    s_dve.bump(v.tensor_copy(
                                        hsb[j][:, 512 * gg:512 * (gg + 1)],
                                        p_t2[:, bank:bank + 512]))
                                    m[("hte", e, j, gg)] = s_dve.n

                        for hc in range(2):
                            for ic in range(32):
                                @blk.sync
                                def _(sy, e=e, hc=hc, ic=ic):
                                    k = hc * 32 + ic
                                    if k >= 2:
                                        pk = k - 2
                                        t_pe = m[("wom", e, pk // 32, pk % 32)]
                                        sy.wait_ge(s_pe.sem, t_pe)
                                    if k < 2:
                                        sy.wait_ge(s_pe.sem, m[("wim", e, 6 + k)])
                                    s_w[ic % 2].bump(sy.dma_start(
                                        out=wmov[ic % 2][:, 0:512],
                                        in_=wo_in[I_FF * e + 128 * ic: I_FF * e + 128 * (ic + 1),
                                                  512 * hc:512 * (hc + 1)]))
                                    m[("wol", e, hc, ic)] = s_w[ic % 2].n

                                if e + 1 < NE and hc == 1 and ic < 8:
                                    @blk.sync
                                    def _(sy, e=e, fc=ic):
                                        if fc == 0:
                                            sy.wait_ge(s_pe.sem, m[("wim", e, 0)])
                                        s_wp.bump(sy.dma_start(
                                            out=wipre[:, 512 * fc:512 * (fc + 1)],
                                            in_=wi_in[D * (e + 1) + 128 * fc:
                                                      D * (e + 1) + 128 * (fc + 1),
                                                      0:512]))
                                        if fc == 7:
                                            m[("wipre", e + 1)] = s_wp.n

                                @blk.tensor
                                def _(t, e=e, hc=hc, ic=ic):
                                    t.wait_ge(s_w[ic % 2].sem, m[("wol", e, hc, ic)])
                                    if ic == 0:
                                        t.wait_ge(s_dve.sem, m[("hte", e, ST_E - 1, 7)])
                                        if hc == 1:
                                            for j in range(ST_E):
                                                t.wait_ge(s_dve.sem, m[("eoe", e, 0, j)])
                                    for j in range(ST_E):
                                        ins = t.matmul(
                                            p_eo[j][:, :],
                                            hsb[j][:, 128 * ic:128 * (ic + 1)],
                                            wmov[ic % 2][:, 0:512],
                                            start=(ic == 0), stop=(ic == 31))
                                    s_pe.bump(ins)
                                    m[("wom", e, hc, ic)] = s_pe.n

                            for j in range(ST_E):
                                @blk.vector
                                def _(v, e=e, hc=hc, j=j):
                                    v.wait_ge(s_pe.sem, m[("wom", e, hc, 31)])
                                    v.wait_ge(s_geo[j % 2].sem, m.get(("eost_last", j % 2), 0))
                                    s_dve.bump(v.tensor_copy(
                                        eosb[j % 2][:, 512 * hc:512 * (hc + 1)],
                                        p_eo[j][:, :]))
                                    m[("eoe", e, hc, j)] = s_dve.n

                                @blk.gpsimd
                                def _(gp, e=e, hc=hc, j=j):
                                    gp.wait_ge(s_dve.sem, m[("eoe", e, hc, j)])
                                    st = ST_E * e + j
                                    s_geo[j % 2].bump(gp.dma_start(
                                        out=eobuf[128 * st:128 * (st + 1),
                                                  512 * hc:512 * (hc + 1)],
                                        in_=eosb[j % 2][:, 512 * hc:512 * (hc + 1)]))
                                    m[("eost", e, hc, j)] = s_geo[j % 2].n
                                    m[("eost_last", j % 2)] = s_geo[j % 2].n

        with contextlib.ExitStack() as fin:
            x1t = [_sb(fin, nc, f"x1t{i}", [128, D]) for i in range(4)]
            eog = [_sb(fin, nc, f"eog{i}", [128, D]) for i in range(4)]
            xn2f = _sb(fin, nc, "xn2f", [128, D])
            res2 = _sb(fin, nc, "res2", [128, D])
            outf = [_sb(fin, nc, f"outf{i}", [128, D]) for i in range(4)]
            with nc.Block(no_gpsimd_drain=True) as blk:
                for i in range(NT):
                    @blk.gpsimd
                    def _(gp, i=i):
                        if i == 0:
                            for j in range(2):
                                gp.wait_ge(s_geo[j].sem, s_geo[j].n)
                        if ("comb", i - 4) in m:
                            gp.wait_ge(s_dve.sem, m[("comb", i - 4)])
                        s_geg[i % 4].bump(gp.indirect_dma_start(
                            out=eog[i % 4][:, :], out_offset=None, in_=eobuf[:, :],
                            in_offset=bass.IndirectOffsetOnAxis(ap=gposu[:, i:i + 1], axis=0)))
                        m[("eogth", i)] = s_geg[i % 4].n

                    @blk.scalar
                    def _(s, i=i):
                        if ("comb", i - 4) in m:
                            s.wait_ge(s_dve.sem, m[("comb", i - 4)])
                        s_x1[i % 4].bump(s.dma_start(out=x1t[i % 4][:, :],
                                                     in_=x1_in[128 * i:128 * (i + 1), :]))
                        m[("x1l", i)] = s_x1[i % 4].n

                    @blk.vector
                    def _(v, i=i):
                        v.wait_ge(s_geg[i % 4].sem, m[("eogth", i)])
                        v.wait_ge(s_x1[i % 4].sem, m[("x1l", i)])
                        if ("comb", i - 1) in m:
                            v.wait_ge(s_dve.sem, m[("comb", i - 1)])
                        if ("outst", i - 4) in m:
                            v.wait_ge(s_st[i % 4].sem, m[("outst", i - 4)])
                        s_dve.bump(v.tensor_scalar(xn2f[:, :], x1t[i % 4][:, :],
                                                   scal[:, 8 * i + 2:8 * i + 3],
                                                   scal[:, 8 * i + 3:8 * i + 4],
                                                   ALU.subtract, ALU.mult))
                        v.wait_ge(s_dve.sem, s_dve.n)
                        s_dve.bump(v.scalar_tensor_tensor(res2[:, :], eog[i % 4][:, :],
                                                          wkf[:, i:i + 1], x1t[i % 4][:, :],
                                                          ALU.mult, ALU.add))
                        v.wait_ge(s_dve.sem, s_dve.n)
                        s_dve.bump(v.tensor_add(outf[i % 4][:, :], res2[:, :],
                                                xn2f[:, :]))
                        m[("comb", i)] = s_dve.n

                    @blk.sync
                    def _(sy, i=i):
                        sy.wait_ge(s_dve.sem, m[("comb", i)])
                        s_st[i % 4].bump(sy.dma_start(out=out_dram[128 * i:128 * (i + 1), :],
                                                      in_=outf[i % 4][:, :]))
                        m[("outst", i)] = s_st[i % 4].n

                @blk.gpsimd
                def _(gp):
                    for j in range(4):
                        gp.wait_ge(s_st[j].sem, s_st[j].n)

    return nc


def _consts():
    ident = np.eye(128, dtype=np.float32)
    cst = np.zeros((128, 32), np.float32)
    cst[:, 0] = 1.0
    cst[:, 1] = np.arange(128, dtype=np.float32)
    cst[:, 18] = EPS_LN
    for j in range(8):
        cst[:, 2 + j] = 8.0 - j
        cst[:, 10 + j] = float(j)
    onesrow = np.ones((1, 128), np.float32)
    tok = (np.arange(NT, dtype=np.uint32)[None, :] * 128
           + np.arange(128, dtype=np.uint32)[:, None])
    return ident, cst, onesrow, tok.astype(np.uint32), tok.astype(np.float32)


def _prep_a_inputs(x_full, Wqkv, Wout, bout, gate, gamma, beta):
    ident, cst, onesrow, _, _ = _consts()
    gateT = np.ascontiguousarray(np.transpose(gate))
    gt = np.ascontiguousarray(
        gateT.reshape(8, 128, 8).transpose(1, 0, 2).reshape(128, 64))
    gam = np.ascontiguousarray(gamma.reshape(8, 128).T)
    bet = np.ascontiguousarray(beta.reshape(8, 128).T)
    maps = []
    for c in range(NCORE):
        xs = np.ascontiguousarray(
            np.concatenate([x_full[c], x_full[c + 8]], axis=0))
        maps.append({
            "x": xs, "wqkv": Wqkv, "wout": Wout,
            "bout": np.ascontiguousarray(bout.reshape(1, D)),
            "gt": gt, "gam": gam, "bet": bet,
            "ident": ident, "cst": cst, "onesrow": onesrow,
        })
    return maps


def kernel(L0_attn_x, L0_attn_Wqkv, L0_attn_Wout, L0_attn_bout,
           L0_ffn_gate_weight, L0_ffn_ln_gamma, L0_ffn_ln_beta,
           L0_ffn_wi, L0_ffn_wo, L0_ffn_conf_w1, L0_ffn_conf_b1,
           L0_ffn_conf_w2, L0_ffn_conf_b2, L0_ffn_comp_w, L0_ffn_comp_b):
    x = np.ascontiguousarray(np.asarray(L0_attn_x, dtype=np.float32))
    Wqkv = np.ascontiguousarray(np.asarray(L0_attn_Wqkv, dtype=np.float32))
    Wout = np.ascontiguousarray(np.asarray(L0_attn_Wout, dtype=np.float32))
    bout = np.asarray(L0_attn_bout, dtype=np.float32)
    gate = np.asarray(L0_ffn_gate_weight, dtype=np.float32)
    gamma = np.asarray(L0_ffn_ln_gamma, dtype=np.float32)
    beta = np.asarray(L0_ffn_ln_beta, dtype=np.float32)
    wi = np.ascontiguousarray(np.asarray(L0_ffn_wi, dtype=np.float32)
                              .reshape(NE * D, I_FF))
    wo = np.ascontiguousarray(np.asarray(L0_ffn_wo, dtype=np.float32)
                              .reshape(NE * I_FF, D))

    ident, cst, onesrow, toku, tokf = _consts()

    nc_a = build_a()
    maps_a = _prep_a_inputs(x, Wqkv, Wout, bout, gate, gamma, beta)
    res_a = run_bass_kernel_spmd(nc_a, maps_a, core_ids=list(range(NCORE)))

    cnt_all = np.zeros((16, 8), np.float32)
    for c in range(NCORE):
        cnt_all[c] = res_a.results[c]["cnto"][:, 0]
        cnt_all[c + 8] = res_a.results[c]["cnto"][:, 1]
    cnt_all = np.ascontiguousarray(cnt_all)

    gam = np.ascontiguousarray(gamma.reshape(8, 128).T)
    bet = np.ascontiguousarray(beta.reshape(8, 128).T)
    nc_b = build_b()
    maps_b = []
    for c in range(NCORE):
        b = np.arange(16)
        mask1 = (b < c).astype(np.float32)
        mask2 = ((b < 8 + c) & (b != c)).astype(np.float32)
        maskcol = np.ascontiguousarray(np.stack([mask1, mask2], axis=1))
        maps_b.append({
            "x1": np.ascontiguousarray(res_a.results[c]["x1o"]),
            "scal": np.ascontiguousarray(res_a.results[c]["scalo"]),
            "cntall": cnt_all, "maskcol": maskcol,
            "wi": wi, "wo": wo, "gam": gam, "bet": bet,
            "ident": ident, "cst": cst, "onesrow": onesrow,
            "toku": toku, "tokf": tokf,
        })
    res_b = run_bass_kernel_spmd(nc_b, maps_b, core_ids=list(range(NCORE)))

    out = np.empty((16, SEQLEN, D), np.float32)
    for c in range(NCORE):
        o = res_b.results[c]["out"]
        out[c] = o[:SEQLEN]
        out[c + 8] = o[SEQLEN:]
    return out
